# version 10
# baseline (speedup 1.0000x reference)
import contextlib
import numpy as np
import concourse.bass as bass
import concourse.mybir as mybir
from concourse.bass_utils import run_bass_kernel_spmd

F32 = mybir.dt.float32
F16 = mybir.dt.float16
AF = mybir.ActivationFunctionType
ALU = mybir.AluOpType

NCORES = 8
T = 2048
TAU = 1024
HALO = 3
CH = 256
D = 2048
DIN = 4096
NH = 32
FFI = 8192
NXC = 35
EPS_RMS = 1e-6
EPS_LN = 1e-5
AG_N = 64 * DIN + 32
SBUF_BASE = 16512
SBUF_MAX = 212736


class MK:
    def __init__(self, debug=()):
        self.debug = set(debug)
        nc = bass.Bass("TRN2", target_bir_lowering=False)
        self.nc = nc
        self.semvals = {}
        self.dbg = {}
        self.sballoc = {}

        dt_ = nc.dram_tensor
        self.ident32_h = dt_("ident32c", [128, 128], F32, kind="ExternalInput").ap()
        self.u_h = dt_("u", [D, T + HALO], F32, kind="ExternalInput").ap()
        self.wA_h = dt_("wA", [NXC, 128, 16, 128], F32, kind="ExternalInput").ap()
        self.wZ_h = dt_("wZ", [32, 128, 16, 128], F32, kind="ExternalInput").ap()
        self.wO_h = dt_("wO", [16, 128, 32, 128], F32, kind="ExternalInput").ap()
        self.wG_h = dt_("wG", [64, 128, 16, 128], F32, kind="ExternalInput").ap()
        self.wU_h = dt_("wU", [64, 128, 16, 128], F32, kind="ExternalInput").ap()
        self.wD_h = dt_("wD", [4, 64, 4, 128, 128], F32, kind="ExternalInput").ap()
        self.convw_h = dt_("convw", [128, 34 * 4], F32, kind="ExternalInput").ap()
        self.convb_h = dt_("convb", [128, 34], F32, kind="ExternalInput").ap()
        self.alog_h = dt_("alog", [32, 1], F32, kind="ExternalInput").ap()
        self.dtb_h = dt_("dtb", [32, 1], F32, kind="ExternalInput").ap()
        self.dskip_h = dt_("dskip", [1, 32], F32, kind="ExternalInput").ap()
        self.nw_h = dt_("nw", [128, 32], F32, kind="ExternalInput").ap()
        self.ident_h = dt_("identc", [128, 128], F32, kind="ExternalInput").ap()
        self.maskT_h = dt_("maskTc", [128, 128], F32, kind="ExternalInput").ap()
        self.o1x128_h = dt_("o1x128", [1, 128], F32, kind="ExternalInput").ap()
        self.o128x1_h = dt_("o128x1", [128, 1], F32, kind="ExternalInput").ap()
        self.oscan_h = dt_("oscan", [32, CH], F32, kind="ExternalInput").ap()
        self.gates_h = dt_("gates", [1, 16], F32, kind="ExternalInput").ap()
        self.ocols_h = dt_("ocols", [128, 32 * 32], F32, kind="ExternalInput").ap()

        self.out_h = dt_("out", [D, T], F32, kind="ExternalOutput").ap()

        self.ypark_h = dt_("ypark", [128, 32, T], F16, kind="Internal").ap()
        self.hscr_h = dt_("hscr", [64, 128, T], F16, kind="Internal").ap()
        self.agin_h = dt_("agin", [AG_N], F32, kind="Internal").ap()
        self.agout_h = dt_("agout", [NCORES, AG_N], F32, kind="Internal").ap()
        self.aflat_h = dt_("aflat", [1, 8192], F32, kind="Internal").ap()
        self.erow_h = dt_("erowh", [1, 32], F32, kind="Internal").ap()
        self.rst_h = dt_("rsth", [1, 1024], F32, kind="Internal").ap()
        self.rstdh_h = dt_("rstdh", [32, 1024], F32, kind="Internal").ap()
        self.aflat8_h = dt_("aflat8", [8, 8192], F32, kind="Internal").ap()
        self.erow8_h = dt_("erow8", [8, 32], F32, kind="Internal").ap()
        self.x1park_h = dt_("x1park", [128, 16, T], F16, kind="Internal").ap()

    def dbg_out(self, name, shape, dtype):
        if name in self.debug and name not in self.dbg:
            self.dbg[name] = self.nc.dram_tensor(
                name, shape, dtype, kind="ExternalOutput").ap()
        return self.dbg.get(name)

    def sem(self, name):
        s = self.es.enter_context(self.nc.semaphore(name))
        self.semvals[s.name] = 0
        setattr(self, name, s)
        return s

    def inc(self, s, n=1):
        self.semvals[s.name] += n
        return self.semvals[s.name]

    def val(self, s):
        return self.semvals[s.name]

    def sb_at(self, name, shape, dtype, off):
        nbytes = int(np.prod(shape[1:])) * mybir.dt.size(dtype)
        assert off + nbytes <= SBUF_MAX, (name, off, nbytes)
        self.sballoc[name] = (off, nbytes)
        return self.nc.alloc_sbuf_tensor_at(
            name, list(shape), dtype, offset=SBUF_BASE + off).ap()

    def build(self):
        nc = self.nc
        with contextlib.ExitStack() as es:
            self.es = es
            self._alloc_sbuf()
            self._alloc_psum()
            for s in ["s_wdma", "s_ldma", "s_pe", "s_dve", "s_act", "s_gp",
                      "s_st", "s_cc", "s_wu", "s_sl0", "s_sl1", "s_yd",
                      "s_ai", "s_ai2", "s_af", "s_er", "s_er2", "s_tp",
                      "s_ur", "s_ur1", "s_sl2", "s_sl3", "s_ur2", "s_ur3"]:
                self.sem(s)
            self.block = es.enter_context(nc.Block())
            self._emit_all()
        return nc

    def _alloc_sbuf(self):
        off = 0

        def A(name, shape, dtype):
            nonlocal off
            t = self.sb_at(name, shape, dtype, off)
            off += (self.sballoc[name][1] + 31) // 32 * 32
            return t

        self.ident = A("identS", [128, 128], F16)
        self.ident32 = A("ident32S", [128, 128], F32)
        self.maskT = A("maskTS", [128, 128], F16)
        self.o1x128 = A("o1x128S", [1, 128], F16)
        self.o128x1 = A("o128x1S", [128, 1], F16)
        self.oscan = A("oscanS", [32, CH], F32)
        self.convw = A("convwS", [128, 34, 4], F32)
        self.convb = A("convbS", [128, 34], F32)
        self.alog = A("alogS", [32, 1], F32)
        self.dtb = A("dtbS", [32, 1], F32)
        self.acol = A("acolS", [32, 1], F32)
        self.dbc = A("dbcS", [128, 32], F32)
        self.nw = A("nwS", [128, 32], F32)
        self.gbc = A("gbcS", [64, 16], F32)
        self.atot = A("atotS", [32, 1], F32)
        self.acum = A("acumS", [32, 8], F32)
        self.ag_atot = A("agatotS", [64, 8], F32)
        self.erow = A("erowS", [64, 32], F32)
        self.er32 = A("er32S", [32, 1], F32)

        self.B1 = A("B1", [128, NXC, TAU + HALO], F16)
        self.dt_park = A("dt_park", [32, T], F32)
        self.a_park = A("a_park", [32, T], F32)
        self.c_park = A("c_park", [64, T], F16)
        self.s_run = A("s_run", [64, DIN], F32)
        self.prevs = A("prevs", [64, DIN], F16)
        self.carry3 = A("carry3", [128, 34, 4], F16)

        self.AR_U = A("AR_U", [128, 16 * (TAU + HALO)], F16)
        self.A_I = self.sb_at("A_I", [128, 8192], F32,
                              self.sballoc["AR_U"][0])
        self.EA16 = self.sb_at("EA16", [64, 4096], F16,
                               self.sballoc["AR_U"][0] + 24576)
        self.XTM = A("XTM", [128, 2 * DIN], F16)
        self.s_in = self.sb_at("s_in", [64, DIN], F32,
                               self.sballoc["XTM"][0])
        self.SCL = A("SCL", [128, DIN], F16)
        self.W16 = A("W16", [128, 8192], F16)
        self.SSB = A("SSB", [128, 384], F32)
        self.WBUF = A("WBUF", [128, 2, 16, 128], F16)
        self.CONVACC = A("CONVACC", [128, 2, 1024], F32)
        self.TT32 = self.sb_at("TT32", [128, 256], F32,
                               self.sballoc["CONVACC"][0])
        self.atall = A("atallS", [32, 8], F32)
        self.X1F = None
        self.SINKG = self.sb_at("SINKG", [64, 4096], F32,
                                self.sballoc["W16"][0])
        self.SINK16 = self.sb_at("SINK16", [64, 4096], F16,
                                 self.sballoc["SCL"][0])
        self.AB64 = self.sb_at("AB64", [64, 4096], F32,
                               self.sballoc["AR_U"][0])
        self.WO_BUF = self.sb_at("WO_BUF", [128, 2, 32, 128], F16,
                                 self.sballoc["s_run"][0])
        self.X1PB = self.sb_at("X1PB", [128, 16, TAU], F16,
                               self.sballoc["AR_U"][0])
        self.X1 = self.sb_at("X1", [128, 16, T], F16,
                             self.sballoc["B1"][0])
        self.RROW = self.sb_at("RROW", [1, 512], F32,
                               self.sballoc["CONVACC"][0] + 4096)
        self.MROW = self.sb_at("MROW", [1, 512], F32,
                               self.sballoc["CONVACC"][0] + 6144)
        self.MRB = self.sb_at("MRB", [128, 2, 512], F32,
                              self.sballoc["prevs"][0])
        self.MRB2 = self.sb_at("MRB2", [128, 2, 1024], F32,
                               self.sballoc["prevs"][0])
        self.ocols = self.sb_at("ocolsS", [128, 32, 32], F16,
                                self.sballoc["dt_park"][0])
        self.erow8sb = self.sb_at("erow8sb", [32, 8], F32,
                                  self.sballoc["agatotS"][0])
        self.erow8 = self.sb_at("erow8", [64, 256], F32,
                                self.sballoc["SSB"][0])
        self.AB64s = [self.sb_at("AB64a", [64, 4096], F32,
                                 self.sballoc["AR_U"][0]),
                      self.sb_at("AB64b", [64, 4096], F32,
                                 self.sballoc["AR_U"][0] + 16384)]
        self.EA16s = [self.sb_at("EA16a", [64, 4096], F16,
                                 self.sballoc["W16"][0]),
                      self.sb_at("EA16b", [64, 4096], F16,
                                 self.sballoc["W16"][0] + 8192)]
        self.RST32 = self.sb_at("RST32", [32, 1024], F32,
                                self.sballoc["CONVACC"][0])
        self.SGM = self.sb_at("SGM", [128, 1024], F32,
                              self.sballoc["CONVACC"][0] + 0)
        self.URES = self.sb_at("URES", [128, 2, 512], F32,
                               self.sballoc["CONVACC"][0] + 0)
        self.SQ16 = self.sb_at("SQ16", [128, 2048], F16,
                               self.sballoc["SCL"][0])
        self.SQF2 = self.sb_at("SQF2", [128, 2, 512], F16,
                               self.sballoc["SCL"][0])
        self.GBUF = self.sb_at("GBUF", [128, 2048], F16,
                               self.sballoc["CONVACC"][0])
        self.HBUF = self.sb_at("HBUF", [128, 4, 2048], F16,
                               self.sballoc["dt_park"][0])
        self.ABS4 = self.sb_at("ABS4", [128, 4, 2048], F16,
                               self.sballoc["s_run"][0])
        self.F1S = self.sb_at("F1S", [128, 4, 2048], F16,
                              self.sballoc["AR_U"][0])
        self.WDB = self.sb_at("WDB", [128, 2, 512], F16,
                              self.sballoc["WBUF"][0])
        self.HB2 = self.sb_at("HB2", [128, 2, 1024], F16,
                              self.sballoc["WBUF"][0] + 2048)
        self.OUT32 = self.sb_at("OUT32", [128, 16, 512], F32,
                                self.sballoc["AR_U"][0])
        self.SQF = self.sb_at("SQF", [128, 512], F16,
                              self.sballoc["SCL"][0])
        self.end_fixed = off
        assert off <= SBUF_MAX, off

    def _alloc_psum(self):
        self.PS_A = self.nc.alloc_psum_tensor("PS_A", [128, 2048], F32).ap()
        self.PS_B = self.nc.alloc_psum_tensor("PS_B", [128, 2048], F32).ap()

    def bank(self, i, w=512, p=128):
        base = self.PS_A if i < 4 else self.PS_B
        j = i % 4
        return base[0:p, 512 * j:512 * j + w]

    def bank16(self, i, w=128, p=128):
        base = self.PS_A if i < 4 else self.PS_B
        j = i % 4
        return base.bitcast(F16)[0:p, 1024 * j:1024 * j + w]

    def _emit_all(self):
        self._consts()
        for tau in range(2):
            self._a1(tau)
            self._conv(tau)
            self._a2_prep(tau)
            for kc in range(4):
                self._a2_chunk(tau, kc)
            self._a2_tau_end(tau)
            self._dump_tau(tau)
        self._gather()
        self._gather_corrprep()
        for tau in range(2):
            self._a3(tau)
            self._a4(tau)
        self._f1()
        self._f2()
        self._f3()
        self._finish_debug()
        self._final_wait()

    def _consts(self):
        m = self

        @m.block.gpsimd
        def _(g):
            g.memset(m.s_run, 0.0).then_inc(m.s_gp, 1)
            m.inc(m.s_gp, 1)
            for dst, src in [(m.ident, m.ident_h), (m.o1x128, m.o1x128_h),
                             (m.o128x1, m.o128x1_h), (m.maskT, m.maskT_h)]:
                g.dma_start(out=dst, in_=src).then_inc(m.s_wdma, 16)
                m.inc(m.s_wdma, 16)
            for dst, src, p in [(m.dbc, m.dskip_h, 128), (m.gbc, m.gates_h, 64)]:
                g.dma_start(out=dst, in_=src.partition_broadcast(p)
                            ).then_inc(m.s_wdma, 16)
                m.inc(m.s_wdma, 16)

        @m.block.sync
        def _(s):
            for dst, src in [(m.ident32, m.ident32_h),
                             (m.oscan, m.oscan_h),
                             (m.convb, m.convb_h), (m.alog, m.alog_h),
                             (m.dtb, m.dtb_h), (m.nw, m.nw_h)]:
                s.dma_start(out=dst, in_=src).then_inc(m.s_ldma, 16)
                m.inc(m.s_ldma, 16)
            s.dma_start(out=m.convw,
                        in_=m.convw_h.rearrange("p (j k) -> p j k", k=4)
                        ).then_inc(m.s_ldma, 16)
            m.inc(m.s_ldma, 16)

        @m.block.scalar
        def _(sc):
            sc.wait_ge(m.s_ldma, m.val(m.s_ldma))
            sc.wait_ge(m.s_wdma, m.val(m.s_wdma))
            sc.activation(m.acol, m.alog, AF.Exp, scale=1.0)
            sc.drain()
            sc.mul(m.acol, m.acol, -1.0)
            sc.nop().then_inc(m.s_act, 1)
            m.inc(m.s_act, 1)

    def _a1(self, tau):
        m = self
        TW = TAU + HALO if tau == 0 else TAU
        u0 = 0 if tau == 0 else TAU + HALO
        tts = [(i, min(i + 512, TW)) for i in range(0, TW, 512)]
        ntt = len(tts)
        dtc0 = tau * TAU
        hof = HALO if tau == 0 else 0

        pre_act = m.val(m.s_act)
        pre_dve = m.val(m.s_dve)
        pre_pe = m.val(m.s_pe)
        pre_st = m.val(m.s_st)
        pre_yd = m.val(m.s_yd)

        @m.block.gpsimd
        def _(g):
            g.wait_ge(m.s_act, pre_act)
            g.wait_ge(m.s_dve, pre_dve)
            g.wait_ge(m.s_pe, pre_pe)
            g.wait_ge(m.s_st, pre_st)
            g.wait_ge(m.s_yd, pre_yd)
            g.dma_start(
                out=m.AR_U[:, 0:16 * TW].rearrange("p (k t) -> p k t", t=TW),
                in_=m.u_h[:, u0:u0 + TW].rearrange("(k p) t -> p k t", p=128),
            ).then_inc(m.s_wu, 16)
            m.inc(m.s_wu, 16)
        uload = m.val(m.s_wu)

        evac0 = m.val(m.s_act)
        pe0 = m.val(m.s_pe)
        sl = [m.s_sl0, m.s_sl1]
        sl0_0, sl1_0 = m.val(m.s_sl0), m.val(m.s_sl1)
        slbase = [sl0_0, sl1_0]

        @m.block.gpsimd
        def _(g):
            for e in range(NXC):
                if e >= 2:
                    g.wait_ge(m.s_pe, pe0 + (e - 1) * ntt)
                g.dma_start(out=m.WBUF[:, e % 2], in_=m.wA_h[e]
                            ).then_inc(sl[e % 2], 16)
                m.inc(sl[e % 2], 16)

        @m.block.tensor
        def _(t):
            t.wait_ge(m.s_wu, uload)
            gi = 0
            for e in range(NXC):
                t.wait_ge(sl[e % 2], slbase[e % 2] + 16 * (e // 2 + 1))
                for ti, (t0, t1) in enumerate(tts):
                    bk = gi % 6
                    if gi >= 6:
                        t.wait_ge(m.s_act, evac0 + (gi - 6 + 1))
                    last = None
                    for k in range(16):
                        last = t.matmul(self.bank(bk, t1 - t0),
                                        m.WBUF[:, e % 2, k],
                                        m.AR_U[:, 0:16 * TW].rearrange(
                                            "p (k t) -> p k t", t=TW)[:, k, t0:t1],
                                        start=(k == 0), stop=(k == 15))
                    last.then_inc(m.s_pe, 1)
                    m.inc(m.s_pe, 1)
                    gi += 1

        @m.block.scalar
        def _(sc):
            gi = 0
            for e in range(NXC):
                for ti, (t0, t1) in enumerate(tts):
                    bk = gi % 6
                    sc.wait_ge(m.s_pe, pe0 + gi + 1)
                    if e < 34:
                        i = sc.copy(m.B1[:, e, t0:t1], self.bank(bk, t1 - t0))
                    else:
                        lo = max(t0, hof)
                        if lo < t1:
                            i = sc.copy(
                                m.dt_park[:, dtc0 + lo - hof:dtc0 + t1 - hof],
                                self.bank(bk, t1 - t0, p=32)[:, lo - t0:t1 - t0])
                        else:
                            i = sc.nop()
                    i.then_inc(m.s_act, 1)
                    m.inc(m.s_act, 1)
                    gi += 1

    def _conv(self, tau):
        m = self
        TW = TAU + HALO if tau == 0 else TAU
        hof = HALO if tau == 0 else 0
        ntt = 3 if tau == 0 else 2
        pre = m.val(m.s_act)
        a1_evac0 = pre - NXC * ntt
        dve0 = m.val(m.s_dve)
        act0 = m.val(m.s_act)
        ldc = m.val(m.s_ldma)

        @m.block.vector
        def _(v):
            v.wait_ge(m.s_ldma, ldc)
            for j in range(34):
                v.wait_ge(m.s_act, a1_evac0 + (j + 1) * ntt)
                acc = m.CONVACC[:, j % 2]
                if j >= 2:
                    v.wait_ge(m.s_act, act0 + (j - 2) + 1)
                last = None
                if tau == 0:
                    for k in range(4):
                        src_ = m.B1[:, j, k:k + TAU]
                        if k == 0:
                            v.tensor_scalar(acc, src_, m.convw[:, j, 0:1],
                                            None, ALU.mult)
                        else:
                            v.drain()
                            last = v.scalar_tensor_tensor(
                                acc, src_, m.convw[:, j, k:k + 1],
                                acc, ALU.mult, ALU.add)
                else:
                    for k in range(4):
                        src_ = m.B1[:, j, k:k + TAU - 3]
                        if k == 0:
                            v.tensor_scalar(acc[:, 3:TAU], src_,
                                            m.convw[:, j, 0:1], None, ALU.mult)
                        else:
                            v.drain()
                            v.scalar_tensor_tensor(
                                acc[:, 3:TAU], src_, m.convw[:, j, k:k + 1],
                                acc[:, 3:TAU], ALU.mult, ALU.add)
                    for t in range(3):
                        first = True
                        for k in range(4):
                            p = t - 3 + k
                            src_ = (m.carry3[:, j, 3 + p:4 + p] if p < 0
                                    else m.B1[:, j, p:p + 1])
                            if first:
                                v.drain()
                                v.tensor_scalar(acc[:, t:t + 1], src_,
                                                m.convw[:, j, k:k + 1], None,
                                                ALU.mult)
                                first = False
                            else:
                                v.drain()
                                last = v.scalar_tensor_tensor(
                                    acc[:, t:t + 1], src_,
                                    m.convw[:, j, k:k + 1],
                                    acc[:, t:t + 1], ALU.mult, ALU.add)
                last.then_inc(m.s_dve, 1)
                m.inc(m.s_dve, 1)
                if tau == 0:
                    v.tensor_copy(m.carry3[:, j, 0:3], m.B1[:, j, TW - 3:TW]
                                  ).then_inc(m.s_dve, 1)
                    m.inc(m.s_dve, 1)

        nd = 2 if tau == 0 else 1

        @m.block.scalar
        def _(sc):
            for j in range(34):
                sc.wait_ge(m.s_dve, dve0 + nd * (j + 1))
                sc.activation(m.B1[:, j, hof:hof + TAU], m.CONVACC[:, j % 2],
                              AF.Silu, bias=m.convb[:, j:j + 1], scale=1.0
                              ).then_inc(m.s_act, 1)
                m.inc(m.s_act, 1)

    def _a2_prep(self, tau):
        m = self
        c0 = tau * TAU
        pre_act = m.val(m.s_act)
        V = m.CONVACC[0:32, 0, 0:1024]
        W = m.CONVACC[0:32, 1, 0:1024]

        @m.block.scalar
        def _(sc):
            sc.wait_ge(m.s_act, pre_act)
            sc.wait_ge(m.s_dve, m.val(m.s_dve))
            sc.drain()
            sc.activation(V, m.dt_park[:, c0:c0 + TAU], AF.Identity,
                          bias=m.dtb[:, 0:1], scale=1.0)
            sc.drain()
            sc.activation(W, V, AF.Abs)
            sc.drain()
            sc.activation(W, W, AF.Exp, scale=-1.0)
            sc.drain()
            sc.activation(W, W, AF.Ln, bias=1.0).then_inc(m.s_act, 1)
            m.inc(m.s_act, 1)
        actv = m.val(m.s_act)

        @m.block.vector
        def _(v):
            v.wait_ge(m.s_act, actv)
            v.tensor_scalar_max(V, V, 0.0)
            v.drain()
            v.tensor_add(m.dt_park[:, c0:c0 + TAU], V, W)
            v.drain()
            v.tensor_scalar_mul(V, m.dt_park[:, c0:c0 + TAU], m.acol[:, 0:1])
            v.drain()
            for kc in range(4):
                i = v.tensor_tensor_scan(
                    m.a_park[:, c0 + kc * CH:c0 + (kc + 1) * CH],
                    m.oscan, V[:, kc * CH:(kc + 1) * CH], 0.0,
                    ALU.mult, ALU.add)
            i.then_inc(m.s_dve, 1)
            m.inc(m.s_dve, 1)

    def pslot16(self, bank, slot, w=128):
        base = (bank % 4) * 1024 + slot * 128
        t = self.PS_A if bank < 4 else self.PS_B
        return t.bitcast(F16)[:, base:base + w]

    def pf32(self, bank, off, w, p=128):
        t = self.PS_A if bank < 4 else self.PS_B
        return t[0:p, (bank % 4) * 512 + off:(bank % 4) * 512 + off + w]

    def _a2_chunk(self, tau, kc):
        m = self
        hof = HALO if tau == 0 else 0
        cc0 = hof + kc * CH
        g0 = tau * TAU + kc * CH
        aw_dve = m.val(m.s_dve)
        pre_act = m.val(m.s_act)

        @m.block.scalar
        def _(sc):
            sc.wait_ge(m.s_dve, aw_dve)
            sc.activation(m.CONVACC[0:32, 1, 0:CH], m.a_park[:, g0:g0 + CH],
                          AF.Exp, bias=m.a_park[:, g0 + CH - 1:g0 + CH],
                          scale=-1.0)
            sc.drain()
            sc.activation(m.er32, m.a_park[:, g0 + CH - 1:g0 + CH], AF.Exp
                          ).then_inc(m.s_act, 1)
            m.inc(m.s_act, 1)
        scd_ready = m.val(m.s_act)

        @m.block.vector
        def _(v):
            v.wait_ge(m.s_act, scd_ready)
            v.tensor_tensor(m.CONVACC[0:32, 1, 0:CH],
                            m.CONVACC[0:32, 1, 0:CH],
                            m.dt_park[:, g0:g0 + CH], ALU.mult).then_inc(m.s_dve, 1)
            m.inc(m.s_dve, 1)
        scdm_ready = m.val(m.s_dve)

        af0 = m.val(m.s_af)
        er0 = m.val(m.s_er)

        @m.block.gpsimd
        def _(g):
            g.wait_ge(m.s_dve, aw_dve)
            g.wait_ge(m.s_ai, m.val(m.s_ai))
            g.dma_start(out=m.aflat_h.rearrange("a (h t) -> (a h) t", h=32),
                        in_=m.a_park[:, g0:g0 + CH]).then_inc(m.s_af, 16)
            m.inc(m.s_af, 16)
            g.wait_ge(m.s_act, scd_ready)
            g.dma_start(out=m.erow_h.rearrange("a h -> h a"), in_=m.er32
                        ).then_inc(m.s_er, 16)
            m.inc(m.s_er, 16)
            g.wait_ge(m.s_er, er0 + 16)
            g.dma_start(out=m.erow[:, 0:32],
                        in_=m.erow_h.partition_broadcast(64)
                        ).then_inc(m.s_er, 16)
            m.inc(m.s_er, 16)
        er_done = m.val(m.s_er)
        af_done = m.val(m.s_af)

        tp0 = m.val(m.s_tp)
        ev0 = m.val(m.s_dve)

        prev_xe = getattr(self, "_a2_prev_xevac", None)
        prev_sse = getattr(self, "_a2_prev_ssevac", None)
        prev_srun = getattr(self, "_a2_prev_srun", None)
        scd_act_mark = m.val(m.s_act)

        @m.block.tensor
        def _(t, prev_xe=prev_xe, prev_sse=prev_sse, prev_srun=prev_srun):
            t.wait_ge(m.s_act, pre_act)
            for grp in range(8):
                bank = 6 + grp % 2
                if grp >= 2:
                    t.wait_ge(m.s_dve, ev0 + (grp - 2) + 1)
                elif prev_xe is not None:
                    t.wait_ge(m.s_dve, prev_xe)
                last = None
                for s8 in range(8):
                    n = grp * 8 + s8
                    jt, ch = n // 32, n % 32
                    pd = self.pslot16(bank, s8)
                    last = t.transpose(
                        pd, m.B1[:, ch, cc0 + jt * 128:cc0 + jt * 128 + 128],
                        m.ident)
                last.then_inc(m.s_tp, 1)
                m.inc(m.s_tp, 1)
            if prev_sse is not None:
                t.wait_ge(m.s_dve, prev_sse)
            if prev_srun is not None:
                t.wait_ge(m.s_dve, prev_srun)
            t.wait_ge(m.s_dve, scdm_ready)
            last = None
            for jt in range(2):
                pd = self.pslot16(5, jt)[:, 0:64]
                last = t.transpose(pd, m.B1[0:64, 32, cc0 + jt * 128:
                                            cc0 + jt * 128 + 128],
                                   m.ident[0:64, 0:64])
            for which in range(3):
                for jt in range(2):
                    srcs = [m.a_park[:, g0 + jt * 128:g0 + jt * 128 + 128],
                            m.dt_park[:, g0 + jt * 128:g0 + jt * 128 + 128],
                            m.CONVACC[0:32, 1, jt * 128:jt * 128 + 128]]
                    pd = self.pf32(5, 192 + (which * 2 + jt) * 32, 32)
                    last = t.transpose(pd, srcs[which],
                                       m.ident32[0:32, 0:32])
            last.then_inc(m.s_tp, 1)
            m.inc(m.s_tp, 1)
            t.matmul(self.pf32(4, 0, 256), m.B1[0:64, 32, cc0:cc0 + 128],
                     m.B1[0:64, 33, cc0:cc0 + 256], start=True, stop=True)
            t.matmul(self.pf32(4, 256, 128),
                     m.B1[0:64, 32, cc0 + 128:cc0 + 256],
                     m.B1[0:64, 33, cc0 + 128:cc0 + 256],
                     start=True, stop=True).then_inc(m.s_tp, 1)
            m.inc(m.s_tp, 1)
        tp_small = tp0 + 9
        tp_sc = tp0 + 10

        @m.block.vector
        def _(v):
            for grp in range(8):
                bank = 6 + grp % 2
                v.wait_ge(m.s_tp, tp0 + grp + 1)
                v.tensor_copy(m.XTM[:, grp * 1024:grp * 1024 + 1024],
                              self.pslot16(bank, 0, 1024)
                              ).then_inc(m.s_dve, 1)
                m.inc(m.s_dve, 1)
        self._a2_prev_xevac = m.val(m.s_dve)

        @m.block.vector
        def _(v):
            v.wait_ge(m.s_tp, tp_small)
            for jt in range(2):
                v.tensor_copy(m.TT32.bitcast(F16)[:, 384 + 64 * jt:
                                                  448 + 64 * jt],
                              self.pslot16(5, jt)[:, 0:64])
            for which in range(3):
                for jt in range(2):
                    k = which * 2 + jt
                    i = v.tensor_copy(m.TT32[:, k * 32:k * 32 + 32],
                                      self.pf32(5, 192 + k * 32, 32))
            i.then_inc(m.s_dve, 1)
            m.inc(m.s_dve, 1)
            v.wait_ge(m.s_tp, tp_sc)
            v.tensor_tensor(m.SSB[:, 0:128], self.pf32(4, 0, 128), m.maskT,
                            ALU.mult)
            v.tensor_copy(m.SSB[:, 128:256], self.pf32(4, 128, 128))
            i = v.tensor_tensor(m.SSB[:, 256:384], self.pf32(4, 256, 128),
                                m.maskT, ALU.mult)
            i.then_inc(m.s_dve, 1)
            m.inc(m.s_dve, 1)
        self._a2_prev_ssevac = m.val(m.s_dve)
        ev_all = m.val(m.s_dve)

        @m.block.vector
        def _(v):
            v.wait_ge(m.s_gp, 1)
            v.tensor_copy(m.prevs, m.s_run).then_inc(m.s_dve, 1)
            m.inc(m.s_dve, 1)
            v.wait_ge(m.s_er, er_done)
            v.drain()
            v.tensor_tensor(
                m.s_run.rearrange("n (h p) -> n h p", h=32),
                m.s_run.rearrange("n (h p) -> n h p", h=32),
                m.erow[:, 0:32].unsqueeze(2).broadcast_to([64, 32, 128]),
                ALU.mult).then_inc(m.s_dve, 1)
            m.inc(m.s_dve, 1)
        prevs_done = m.val(m.s_dve) - 1
        scale_done = m.val(m.s_dve)

        for jt in range(2):
            pe_snap = m.val(m.s_pe)

            @m.block.vector
            def _(v, pe_snap=pe_snap, jt=jt):
                v.wait_ge(m.s_pe, pe_snap)
                v.tensor_tensor(
                    m.SCL[:, 0:4096].rearrange("p (h q) -> p h q", h=32),
                    m.XTM[:, jt * 4096:jt * 4096 + 4096].rearrange(
                        "p (h q) -> p h q", h=32),
                    m.TT32[:, 128 + jt * 32:160 + jt * 32].unsqueeze(2
                        ).broadcast_to([128, 32, 128]),
                    ALU.mult).then_inc(m.s_dve, 1)
                m.inc(m.s_dve, 1)
            xwd_done = m.val(m.s_dve)

            @m.block.tensor
            def _(t, jt=jt, xwd_done=xwd_done):
                t.wait_ge(m.s_dve, xwd_done)
                for p in range(8):
                    if p >= 2:
                        t.wait_ge(m.s_dve, xwd_done + (p - 2) + 1)
                    t.matmul(self.pf32(4 + p % 2, 0, 512, p=64),
                             m.TT32.bitcast(F16)[:, 384 + 64 * jt:448 + 64 * jt],
                             m.SCL[:, 512 * p:512 * p + 512],
                             start=True, stop=True).then_inc(m.s_pe, 1)
                    m.inc(m.s_pe, 1)
            st_pe = m.val(m.s_pe)

            @m.block.vector
            def _(v, jt=jt, st_pe=st_pe):
                for p in range(8):
                    v.wait_ge(m.s_pe, st_pe - 8 + p + 1)
                    if jt == 0 and p < 2:
                        v.wait_ge(m.s_dve, scale_done)
                    v.drain()
                    v.tensor_tensor(m.s_run[:, 512 * p:512 * p + 512],
                                    m.s_run[:, 512 * p:512 * p + 512],
                                    self.pf32(4 + p % 2, 0, 512, p=64),
                                    ALU.add).then_inc(m.s_dve, 1)
                    m.inc(m.s_dve, 1)

        self._a2_prev_srun = m.val(m.s_dve)

        for hg in range(2):
            ai0 = m.val(m.s_ai)

            @m.block.gpsimd
            def _(g, hg=hg, ai0=ai0):
                g.wait_ge(m.s_af, af_done)
                g.wait_ge(m.s_act, m.val(m.s_act))
                g.dma_start(
                    out=m.A_I[:, 0:4096],
                    in_=m.aflat_h[:, hg * 4096:hg * 4096 + 4096]
                    .partition_broadcast(128)).then_inc(m.s_ai, 16)
                m.inc(m.s_ai, 16)
                g.dma_start(
                    out=m.A_I[:, 4096:6144],
                    in_=m.aflat_h.rearrange("a (h t) -> a h t", h=32)
                    [:, hg * 16:hg * 16 + 16, 128:256]
                    .partition_broadcast(128)).then_inc(m.s_ai, 16)
                m.inc(m.s_ai, 16)
            ai_done = m.val(m.s_ai)

            pe_prev = m.val(m.s_pe)

            @m.block.scalar
            def _(sc, hg=hg, ai_done=ai_done, pe_prev=pe_prev):
                sc.wait_ge(m.s_ai, ai_done)
                sc.wait_ge(m.s_pe, pe_prev)
                sc.activation(m.EA16[:, 0:4096], m.A_I[0:64, 0:4096], AF.Exp
                              ).then_inc(m.s_act, 1)
                m.inc(m.s_act, 1)
            ea_done = m.val(m.s_act)

            @m.block.vector
            def _(v, hg=hg, ai_done=ai_done, ea_done=ea_done):
                v.wait_ge(m.s_ai, ai_done)
                v.wait_ge(m.s_act, ea_done)
                v.wait_ge(m.s_dve, ev_all * 0 + ev_all)
                v.tensor_tensor(
                    m.A_I[:, 0:4096].rearrange("p (h q) -> p h q", h=16),
                    m.A_I[:, 0:4096].rearrange("p (h q) -> p h q", h=16),
                    m.TT32[:, hg * 16:hg * 16 + 16].unsqueeze(2
                        ).broadcast_to([128, 16, 256]),
                    ALU.subtract)
                v.drain()
                v.tensor_scalar_min(m.A_I[:, 0:4096], m.A_I[:, 0:4096], 0.0).then_inc(m.s_dve, 1)
                m.inc(m.s_dve, 1)
                v.tensor_tensor(
                    m.A_I[:, 4096:6144].rearrange("p (h q) -> p h q", h=16),
                    m.A_I[:, 4096:6144].rearrange("p (h q) -> p h q", h=16),
                    m.TT32[:, 32 + hg * 16:32 + hg * 16 + 16].unsqueeze(2
                        ).broadcast_to([128, 16, 128]),
                    ALU.subtract)
                v.drain()
                v.tensor_scalar_min(m.A_I[:, 4096:6144], m.A_I[:, 4096:6144],
                                    0.0).then_inc(m.s_dve, 1)
                m.inc(m.s_dve, 1)
            seg_done = m.val(m.s_dve)

            @m.block.scalar
            def _(sc, seg_done=seg_done):
                sc.wait_ge(m.s_dve, seg_done)
                sc.activation(m.W16[:, 0:4096], m.A_I[:, 0:4096], AF.Exp)
                sc.drain()
                sc.activation(m.W16[:, 4096:6144], m.A_I[:, 4096:6144], AF.Exp
                              ).then_inc(m.s_act, 1)
                m.inc(m.s_act, 1)
            wexp_done = m.val(m.s_act)

            @m.block.vector
            def _(v, hg=hg, wexp_done=wexp_done, pe_prev=pe_prev):
                v.wait_ge(m.s_act, wexp_done)
                v.wait_ge(m.s_pe, pe_prev)
                v.tensor_tensor(
                    m.W16[:, 0:4096].rearrange("p (h q) -> p h q", h=16),
                    m.W16[:, 0:4096].rearrange("p (h q) -> p h q", h=16),
                    m.SSB[:, 0:256].unsqueeze(1).broadcast_to([128, 16, 256]),
                    ALU.mult)
                v.drain()
                v.tensor_tensor(
                    m.W16[:, 4096:6144].rearrange("p (h q) -> p h q", h=16),
                    m.W16[:, 4096:6144].rearrange("p (h q) -> p h q", h=16),
                    m.SSB[:, 256:384].unsqueeze(1).broadcast_to([128, 16, 128]),
                    ALU.mult)
                v.drain()
                v.tensor_tensor(
                    m.EA16[:, 0:4096].rearrange("p (h q) -> p h q", h=16),
                    m.EA16[:, 0:4096].rearrange("p (h q) -> p h q", h=16),
                    m.B1[0:64, 33, cc0:cc0 + 256].unsqueeze(1
                        ).broadcast_to([64, 16, 256]),
                    ALU.mult)
                v.drain()
                v.tensor_tensor(
                    m.SCL[:, 0:2048].rearrange("p (h q) -> p h q", h=16),
                    m.XTM[:, 0 * 4096 + hg * 2048:0 * 4096 + hg * 2048 + 2048
                          ].rearrange("p (h q) -> p h q", h=16),
                    m.TT32[:, 64 + hg * 16:64 + hg * 16 + 16].unsqueeze(2
                        ).broadcast_to([128, 16, 128]),
                    ALU.mult)
                v.drain()
                v.tensor_tensor(
                    m.SCL[:, 2048:4096].rearrange("p (h q) -> p h q", h=16),
                    m.XTM[:, 1 * 4096 + hg * 2048:1 * 4096 + hg * 2048 + 2048
                          ].rearrange("p (h q) -> p h q", h=16),
                    m.TT32[:, 96 + hg * 16:96 + hg * 16 + 16].unsqueeze(2
                        ).broadcast_to([128, 16, 128]),
                    ALU.mult).then_inc(m.s_dve, 1)
                m.inc(m.s_dve, 1)
            wready = m.val(m.s_dve)

            pe_y0 = m.val(m.s_pe)

            dve_y0 = m.val(m.s_dve)

            @m.block.tensor
            def _(t, hg=hg, wready=wready, pe_y0=pe_y0, dve_y0=dve_y0):
                t.wait_ge(m.s_dve, wready)
                for l in range(16):
                    h = hg * 16 + l
                    if l >= 4:
                        t.wait_ge(m.s_dve, dve_y0 + (l - 4) + 1)
                    yap = self.pf32(l % 4, 0, 256)
                    t.matmul(yap, m.SCL[:, l * 128:l * 128 + 128],
                             m.W16[:, l * 256:l * 256 + 256],
                             start=True, stop=False)
                    t.matmul(yap[:, 128:256],
                             m.SCL[:, 2048 + l * 128:2048 + l * 128 + 128],
                             m.W16[:, 4096 + l * 128:4096 + l * 128 + 128],
                             start=False, stop=False)
                    t.matmul(yap, m.prevs[:, h * 128:h * 128 + 128],
                             m.EA16[:, l * 256:l * 256 + 256],
                             start=False, stop=True).then_inc(m.s_pe, 1)
                    m.inc(m.s_pe, 1)

            @m.block.vector
            def _(v, hg=hg, pe_y0=pe_y0):
                for l in range(16):
                    h = hg * 16 + l
                    v.wait_ge(m.s_pe, pe_y0 + l + 1)
                    yap = self.pf32(l % 4, 0, 256)
                    v.scalar_tensor_tensor(
                        m.B1[:, h, cc0:cc0 + 256], m.B1[:, h, cc0:cc0 + 256],
                        m.dbc[:, h:h + 1], yap, ALU.mult, ALU.add
                        ).then_inc(m.s_dve, 1)
                    m.inc(m.s_dve, 1)

    def _a2_tau_end(self, tau):
        m = self
        hof = HALO if tau == 0 else 0
        dve_all = m.val(m.s_dve)
        act_all = m.val(m.s_act)

        @m.block.vector
        def _(v):
            v.wait_ge(m.s_dve, dve_all)
            v.tensor_copy(m.c_park[:, tau * TAU:tau * TAU + TAU],
                          m.B1[0:64, 33, hof:hof + TAU]).then_inc(m.s_dve, 1)
            m.inc(m.s_dve, 1)

        @m.block.sync
        def _(s):
            s.wait_ge(m.s_dve, m.val(m.s_dve))
            s.dma_start(out=m.ypark_h[:, :, tau * TAU:tau * TAU + TAU],
                        in_=m.B1[:, 0:32, hof:hof + TAU]).then_inc(m.s_yd, 16)
            m.inc(m.s_yd, 16)

    def _gather(self):
        m = self
        dve_all = m.val(m.s_dve)
        act_all = m.val(m.s_act)

        @m.block.vector
        def _(v):
            v.wait_ge(m.s_dve, dve_all)
            v.wait_ge(m.s_act, act_all)
            v.reduce_sum(m.atot, m.a_park[:, CH - 1:T:CH], mybir.AxisListType.X)
            v.drain()
            v.tensor_tensor_scan(m.acum[:, 1:8], m.oscan[:, 0:7],
                                 m.a_park[:, CH - 1:T - CH:CH], 0.0,
                                 ALU.mult, ALU.add).then_inc(m.s_dve, 1)
            m.inc(m.s_dve, 1)
        red_done = m.val(m.s_dve)

        @m.block.gpsimd
        def _(g):
            g.memset(m.acum[:, 0:1], 0.0)
            g.wait_ge(m.s_dve, red_done)
            g.memset(m.s_in, 0.0).then_inc(m.s_gp, 1)
            m.inc(m.s_gp, 1)

        @m.block.sync
        def _(s):
            s.wait_ge(m.s_dve, red_done)
            s.dma_start(out=m.agin_h[0:64 * DIN].rearrange("(n q) -> n q", n=64),
                        in_=m.s_run).then_inc(m.s_cc, 16)
            m.inc(m.s_cc, 16)
            s.dma_start(out=m.agin_h[64 * DIN:64 * DIN + 32
                                     ].rearrange("(h a) -> h a", a=1),
                        in_=m.atot).then_inc(m.s_cc, 16)
            m.inc(m.s_cc, 16)
        ag_in_done = m.val(m.s_cc)

        @m.block.gpsimd
        def _(g):
            g.wait_ge(m.s_cc, ag_in_done)
            g.collective_compute(
                "AllGather", ALU.bypass,
                replica_groups=[list(range(NCORES))],
                ins=[m.agin_h],
                outs=[m.agout_h],
            ).then_inc(m.s_gp, 1)
            m.inc(m.s_gp, 1)
        cc_done = m.val(m.s_gp)

        @m.block.sync
        def _(s):
            s.wait_ge(m.s_gp, cc_done)
            s.dma_start(out=m.atall,
                        in_=m.agout_h[:, 64 * DIN:64 * DIN + 32]
                        .rearrange("j h -> h j"),
                        allow_slow_non_contiguous=True).then_inc(m.s_cc, 16)
            m.inc(m.s_cc, 16)
        at_done = m.val(m.s_cc)

        for j in range(NCORES):
            er0 = m.val(m.s_er)

            @m.block.scalar
            def _(sc, j=j, at_done=at_done):
                if j == 0:
                    sc.wait_ge(m.s_cc, at_done)
                sc.wait_ge(m.s_er, m.val(m.s_er))
                sc.activation(m.er32, m.atall[:, j:j + 1], AF.Exp,
                              scale=m.gbc[0:32, 8 + j:9 + j]
                              ).then_inc(m.s_act, 1)
                m.inc(m.s_act, 1)
            e_done = m.val(m.s_act)

            @m.block.gpsimd
            def _(g, j=j, e_done=e_done, er0=er0, dve_prev=m.val(m.s_dve)):
                g.wait_ge(m.s_act, e_done)
                g.wait_ge(m.s_dve, dve_prev)
                g.dma_start(out=m.erow_h.rearrange("a h -> h a"), in_=m.er32
                            ).then_inc(m.s_er, 16)
                m.inc(m.s_er, 16)
                g.wait_ge(m.s_er, er0 + 16)
                g.dma_start(out=m.erow[:, 0:32],
                            in_=m.erow_h.partition_broadcast(64)
                            ).then_inc(m.s_er, 16)
                m.inc(m.s_er, 16)
                g.wait_ge(m.s_dve, m.val(m.s_dve))
                g.dma_start(out=m.AB64,
                            in_=m.agout_h[j, 0:64 * DIN]
                            .rearrange("(n q) -> n q", n=64)
                            ).then_inc(m.s_ai, 16)
                m.inc(m.s_ai, 16)
            er_done = m.val(m.s_er)
            sag_done = m.val(m.s_ai)

            @m.block.vector
            def _(v, j=j, er_done=er_done, sag_done=sag_done):
                if j == 0:
                    v.wait_ge(m.s_gp, m.val(m.s_gp))
                v.wait_ge(m.s_er, er_done)
                v.wait_ge(m.s_ai, sag_done)
                v.drain()
                v.tensor_tensor(
                    m.s_in.rearrange("n (h p) -> n h p", h=32),
                    m.s_in.rearrange("n (h p) -> n h p", h=32),
                    m.erow[:, 0:32].unsqueeze(2).broadcast_to([64, 32, 128]),
                    ALU.mult)
                v.drain()
                v.scalar_tensor_tensor(
                    m.s_in, m.AB64, m.gbc[:, j:j + 1], m.s_in,
                    ALU.mult, ALU.add).then_inc(m.s_dve, 1)
                m.inc(m.s_dve, 1)

    def _gather_corrprep(self):
        m = self
        red = m.val(m.s_dve)
        gp0 = m.val(m.s_gp)

        @m.block.sync
        def _(s):
            s.wait_ge(m.s_dve, red)
            s.wait_ge(m.s_gp, gp0)
            s.dma_start(out=m.aflat8_h.rearrange("g (h t) -> h g t", h=32),
                        in_=m.a_park.rearrange("h (g t) -> h g t", g=8)
                        ).then_inc(m.s_yd, 16)
            m.inc(m.s_yd, 16)
        e8_dve = m.val(m.s_dve)

        @m.block.scalar
        def _(sc):
            sc.wait_ge(m.s_dve, e8_dve)
            sc.wait_ge(m.s_gp, gp0)
            sc.drain()
            sc.activation(m.erow8sb, m.acum, AF.Exp).then_inc(m.s_act, 1)
            m.inc(m.s_act, 1)
        e8_done = m.val(m.s_act)
        er0 = m.val(m.s_er)

        @m.block.gpsimd
        def _(g, e8_done=e8_done, er0=er0):
            g.wait_ge(m.s_act, e8_done)
            g.dma_start(out=m.erow8_h.rearrange("g h -> h g"), in_=m.erow8sb,
                        allow_slow_non_contiguous=True).then_inc(m.s_er, 16)
            m.inc(m.s_er, 16)
            g.wait_ge(m.s_er, er0 + 16)
            g.dma_start(out=m.erow8,
                        in_=m.erow8_h.rearrange("g h -> (g h)").unsqueeze(0)
                        .partition_broadcast(64)).then_inc(m.s_er, 16)
            m.inc(m.s_er, 16)
        self.erow8_done = m.val(m.s_er)

    def _a3(self, tau):
        m = self
        pre_dve = m.val(m.s_dve)
        pre_yd = m.val(m.s_yd)

        @m.block.sync
        def _(s):
            s.wait_ge(m.s_dve, pre_dve)
            s.wait_ge(m.s_act, m.val(m.s_act))
            s.wait_ge(m.s_pe, m.val(m.s_pe))
            s.wait_ge(m.s_yd, pre_yd)
            s.dma_start(out=m.B1[:, 0:32, 0:TAU],
                        in_=m.ypark_h[:, :, tau * TAU:tau * TAU + TAU]
                        ).then_inc(m.s_yd, 16)
            m.inc(m.s_yd, 16)
        yl_done = m.val(m.s_yd)

        corr_pe = {}
        ea_act = {}
        sink_dve = {}
        for kc in range(4):
            g = tau * 4 + kc
            pe_prev_kc = m.val(m.s_pe)

            @m.block.vector
            def _(v, kc=kc, g=g, pe_prev_kc=pe_prev_kc):
                if tau == 0 and kc == 0:
                    v.wait_ge(m.s_er, self.erow8_done)
                v.wait_ge(m.s_pe, pe_prev_kc)
                v.drain()
                v.tensor_tensor(
                    m.SINK16.rearrange("n (h p) -> n h p", h=32),
                    m.s_in.rearrange("n (h p) -> n h p", h=32),
                    m.erow8[:, g * 32:g * 32 + 32].unsqueeze(2
                        ).broadcast_to([64, 32, 128]),
                    ALU.mult).then_inc(m.s_dve, 1)
                m.inc(m.s_dve, 1)
            sink_dve[kc] = m.val(m.s_dve)

            for hg in range(2):
                slot = (kc * 2 + hg) % 2
                use = kc * 2 + hg

                @m.block.gpsimd
                def _(g_, g=g, hg=hg, slot=slot, use=use,
                      st_snap=m.val(m.s_st), yd_snap=m.val(m.s_yd)):
                    if tau == 1 and use == 0:
                        g_.wait_ge(m.s_st, st_snap)
                    if use >= 2:
                        g_.wait_ge(m.s_act, ea_act[use - 2])
                    elif tau == 0 and use < 2:
                        g_.wait_ge(m.s_yd, yd_snap)
                    ais = m.s_ai if slot == 0 else m.s_ai2
                    g_.dma_start(
                        out=m.AB64s[slot],
                        in_=m.aflat8_h[g:g + 1, hg * 4096:hg * 4096 + 4096]
                        .partition_broadcast(64)).then_inc(ais, 16)
                    m.inc(ais, 16)
                ai_done = m.val(m.s_ai if slot == 0 else m.s_ai2)
                pe_prev = corr_pe.get(use - 2)

                @m.block.scalar
                def _(sc, slot=slot, ai_done=ai_done, pe_prev=pe_prev):
                    sc.wait_ge(m.s_ai if slot == 0 else m.s_ai2, ai_done)
                    if pe_prev is not None:
                        sc.wait_ge(m.s_pe, pe_prev)
                    sc.drain()
                    sc.activation(m.EA16s[slot], m.AB64s[slot], AF.Exp
                                  ).then_inc(m.s_act, 1)
                    m.inc(m.s_act, 1)
                ea_act[use] = m.val(m.s_act)

                @m.block.vector
                def _(v, g=g, slot=slot, ea_done=m.val(m.s_act)):
                    v.wait_ge(m.s_act, ea_done)
                    v.drain()
                    v.tensor_tensor(
                        m.EA16s[slot].rearrange("n (h p) -> n h p", h=16),
                        m.EA16s[slot].rearrange("n (h p) -> n h p", h=16),
                        m.c_park[:, g * CH:g * CH + CH].unsqueeze(1
                            ).broadcast_to([64, 16, 256]),
                        ALU.mult).then_inc(m.s_dve, 1)
                    m.inc(m.s_dve, 1)
                cp_done = m.val(m.s_dve)
                pe_c0 = m.val(m.s_pe)
                dve_c0 = m.val(m.s_dve)

                @m.block.tensor
                def _(t, hg=hg, slot=slot, kc=kc, cp_done=cp_done,
                      pe_c0=pe_c0, dve_c0=dve_c0):
                    t.wait_ge(m.s_dve, cp_done)
                    t.wait_ge(m.s_dve, sink_dve[kc])
                    t.wait_ge(m.s_yd, yl_done)
                    for l in range(16):
                        h = hg * 16 + l
                        if l >= 8:
                            t.wait_ge(m.s_dve, dve_c0 + (l - 8) + 1)
                        yap = self.pf32(l % 8, 0, 256)
                        t.matmul(yap, m.SINK16[:, h * 128:h * 128 + 128],
                                 m.EA16s[slot][:, l * 256:l * 256 + 256],
                                 start=True, stop=True).then_inc(m.s_pe, 1)
                        m.inc(m.s_pe, 1)
                corr_pe[use] = m.val(m.s_pe)

                @m.block.vector
                def _(v, hg=hg, pe_c0=pe_c0, kc=kc):
                    for l in range(16):
                        h = hg * 16 + l
                        v.wait_ge(m.s_pe, pe_c0 + l + 1)
                        yap = self.pf32(l % 8, 0, 256)
                        cc = kc * CH
                        v.tensor_tensor(m.B1[:, h, cc:cc + 256],
                                        m.B1[:, h, cc:cc + 256], yap,
                                        ALU.add).then_inc(m.s_dve, 1)
                        m.inc(m.s_dve, 1)

        TW = TAU
        u0 = (0 if tau == 0 else TAU + HALO) + (HALO if tau == 0 else 0)
        corr_dve = m.val(m.s_dve)

        @m.block.gpsimd
        def _(g):
            g.wait_ge(m.s_dve, corr_dve)
            g.wait_ge(m.s_act, m.val(m.s_act))
            g.dma_start(
                out=m.AR_U[:, 0:16 * TW].rearrange("p (k t) -> p k t", t=TW),
                in_=m.u_h[:, u0:u0 + TW].rearrange("(k p) t -> p k t", p=128),
            ).then_inc(m.s_wu, 16)
            m.inc(m.s_wu, 16)
        uz_done = m.val(m.s_wu)

        if tau == 0:
            @m.block.gpsimd
            def _(g):
                g.wait_ge(m.s_dve, corr_dve)
                g.dma_start(out=m.ocols, in_=m.ocols_h.rearrange(
                    "p (h j) -> p h j", j=32)).then_inc(m.s_wdma, 16)
                m.inc(m.s_wdma, 16)
        ocols_done = m.val(m.s_wdma)

        sl = [m.s_sl0, m.s_sl1]
        slbase = [m.val(m.s_sl0), m.val(m.s_sl1)]
        zmm_inc = {}
        sig_inc = {}
        sq_inc = {}
        sqm_inc = {}
        yg_inc = {}

        for h in range(32):
            @m.block.gpsimd
            def _(g, h=h):
                if h >= 2:
                    g.wait_ge(m.s_pe, zmm_inc[h - 2])
                g.dma_start(out=m.WBUF[:, h % 2], in_=m.wZ_h[h]
                            ).then_inc(sl[h % 2], 16)
                m.inc(sl[h % 2], 16)

            @m.block.tensor
            def _(t, h=h):
                if h == 0:
                    t.wait_ge(m.s_wu, uz_done)
                t.wait_ge(sl[h % 2], slbase[h % 2] + 16 * (h // 2 + 1))
                if h >= 2:
                    t.wait_ge(m.s_act, sig_inc[h - 2])
                last = None
                for tt in range(2):
                    for k in range(16):
                        last = t.matmul(
                            self.pf32(2 * (h % 2) + tt, 0, 512),
                            m.WBUF[:, h % 2, k],
                            m.AR_U[:, 0:16 * TW].rearrange(
                                "p (k t) -> p k t", t=TW)[:, k,
                                tt * 512:tt * 512 + 512],
                            start=(k == 0), stop=(k == 15))
                last.then_inc(m.s_pe, 1)
                m.inc(m.s_pe, 1)
                zmm_inc[h] = m.val(m.s_pe)
                if h >= 1:
                    if h == 1:
                        t.wait_ge(m.s_wdma, ocols_done)
                    t.wait_ge(m.s_dve, sq_inc[h - 1])
                    last = None
                    for tt in range(2):
                        last = t.matmul(
                            self.pf32(6 + tt, 0, 512, p=32),
                            m.ocols[:, h - 1],
                            m.SQ16[:, ((h - 1) % 2) * 1024 + tt * 512:
                                   ((h - 1) % 2) * 1024 + tt * 512 + 512],
                            start=(h - 1 == 0), stop=(h - 1 == 31))
                    last.then_inc(m.s_pe, 1)
                    m.inc(m.s_pe, 1)
                    sqm_inc[h - 1] = m.val(m.s_pe)

            zap = (self.PS_A[:, 0:1024] if h % 2 == 0
                   else self.PS_A[:, 1024:2048])
            dve_snap = m.val(m.s_dve)

            @m.block.scalar
            def _(sc, h=h, zap=zap, dve_snap=dve_snap):
                sc.wait_ge(m.s_pe, zmm_inc[h])
                if h >= 2:
                    sc.wait_ge(m.s_dve, yg_inc[h - 2])
                sc.drain()
                sc.activation(m.CONVACC[:, h % 2], zap, AF.Sigmoid
                              ).then_inc(m.s_act, 1)
                m.inc(m.s_act, 1)
            sig_inc[h] = m.val(m.s_act)

            @m.block.vector
            def _(v, h=h):
                v.wait_ge(m.s_act, sig_inc[h])
                v.drain()
                v.tensor_tensor(m.B1[:, h, 0:TAU], m.B1[:, h, 0:TAU],
                                m.CONVACC[:, h % 2], ALU.mult).then_inc(m.s_dve, 1)
                m.inc(m.s_dve, 1)
            yg_inc[h] = m.val(m.s_dve)

            @m.block.vector
            def _(v, h=h):
                if h >= 2:
                    v.wait_ge(m.s_pe, sqm_inc[h - 2])
                v.drain()
                v.tensor_tensor(m.SQ16[:, (h % 2) * 1024:(h % 2) * 1024 + 1024],
                                m.B1[:, h, 0:TAU], m.B1[:, h, 0:TAU],
                                ALU.mult).then_inc(m.s_dve, 1)
                m.inc(m.s_dve, 1)
            sq_inc[h] = m.val(m.s_dve)

        @m.block.tensor
        def _(t):
            t.wait_ge(m.s_dve, sq_inc[31])
            last = None
            for tt in range(2):
                last = t.matmul(
                    self.pf32(6 + tt, 0, 512, p=32),
                    m.ocols[:, 31],
                    m.SQ16[:, (31 % 2) * 1024 + tt * 512:
                           (31 % 2) * 1024 + tt * 512 + 512],
                    start=False, stop=True)
            last.then_inc(m.s_pe, 1)
            m.inc(m.s_pe, 1)
            sqm_inc[31] = m.val(m.s_pe)

        @m.block.vector
        def _(v, yg31=m.val(m.s_dve)):
            v.wait_ge(m.s_pe, sqm_inc[31])
            v.wait_ge(m.s_dve, yg31)
            v.drain()
            v.tensor_scalar(m.RST32[:, 0:512], self.pf32(6, 0, 512, p=32),
                            1.0 / 128, EPS_RMS, ALU.mult, ALU.add)
            v.drain()
            v.tensor_scalar(m.RST32[:, 512:1024], self.pf32(7, 0, 512, p=32),
                            1.0 / 128, EPS_RMS, ALU.mult, ALU.add).then_inc(m.s_dve, 1)
            m.inc(m.s_dve, 1)
        ms_done = m.val(m.s_dve)

        @m.block.scalar
        def _(sc, ms_done=ms_done):
            sc.wait_ge(m.s_dve, ms_done)
            sc.drain()
            sc.activation(m.RST32, m.RST32, AF.Sqrt).then_inc(m.s_act, 1)
            m.inc(m.s_act, 1)
        ars_done = m.val(m.s_act)

        @m.block.vector
        def _(v, ars_done=ars_done):
            v.wait_ge(m.s_act, ars_done)
            v.drain()
            v.reciprocal(m.RST32, m.RST32).then_inc(m.s_dve, 1)
            m.inc(m.s_dve, 1)
        rec_done = m.val(m.s_dve)

        @m.block.sync
        def _(s, rec_done=rec_done):
            s.wait_ge(m.s_dve, rec_done)
            s.dma_start(out=m.rstdh_h, in_=m.RST32).then_inc(m.s_yd, 16)
            m.inc(m.s_yd, 16)
        rsth_done = m.val(m.s_yd)

        yn_inc = {}
        ers = [m.s_er, m.s_er2]
        for h in range(32):
            @m.block.gpsimd
            def _(g, h=h):
                if h == 0:
                    g.wait_ge(m.s_yd, rsth_done)
                if h >= 2:
                    g.wait_ge(m.s_dve, yn_inc[h - 2])
                g.dma_start(out=m.MRB2[:, h % 2],
                            in_=m.rstdh_h[h:h + 1, :].partition_broadcast(128)
                            ).then_inc(ers[h % 2], 16)
                m.inc(ers[h % 2], 16)
            bc_done = m.val(ers[h % 2])

            @m.block.vector
            def _(v, h=h, bc_done=bc_done):
                v.wait_ge(ers[h % 2], bc_done)
                v.drain()
                v.scalar_tensor_tensor(
                    m.B1[:, h, 0:TAU], m.B1[:, h, 0:TAU], m.nw[:, h:h + 1],
                    m.MRB2[:, h % 2], ALU.mult, ALU.mult
                    ).then_inc(m.s_dve, 1)
                m.inc(m.s_dve, 1)
            yn_inc[h] = m.val(m.s_dve)

    def _a4(self, tau):
        m = self
        pre_dve = m.val(m.s_dve)
        sl = [m.s_sl0, m.s_sl1]
        slbase = [m.val(m.s_sl0), m.val(m.s_sl1)]
        pe0 = m.val(m.s_pe)
        dve0 = m.val(m.s_dve)
        u0 = (0 if tau == 0 else TAU + HALO) + (HALO if tau == 0 else 0)

        @m.block.gpsimd
        def _(g):
            g.wait_ge(m.s_dve, pre_dve)
            for e in range(16):
                if e >= 2:
                    g.wait_ge(m.s_pe, pe0 + 2 * (e - 1))
                g.dma_start(out=m.WO_BUF[:, e % 2], in_=m.wO_h[e]
                            ).then_inc(sl[e % 2], 16)
                m.inc(sl[e % 2], 16)

        for e in range(16):
            for tt in range(2):
                gidx = e * 2 + tt

                @m.block.sync
                def _(s, e=e, tt=tt, gidx=gidx):
                    if gidx >= 2:
                        s.wait_ge(m.s_dve, dve0 + (gidx - 2) + 1)
                    else:
                        s.wait_ge(m.s_dve, pre_dve)
                    us = m.s_ur if gidx % 2 == 0 else m.s_ur1
                    s.dma_start(
                        out=m.URES[:, gidx % 2],
                        in_=m.u_h[e * 128:e * 128 + 128,
                                  u0 + tt * 512:u0 + tt * 512 + 512]
                        ).then_inc(us, 16)
                    m.inc(us, 16)
                ures_done = m.val(m.s_ur if gidx % 2 == 0 else m.s_ur1)

                @m.block.tensor
                def _(t, e=e, tt=tt, gidx=gidx, ures_done=ures_done):
                    t.wait_ge(sl[e % 2], slbase[e % 2] + 16 * (e // 2 + 1))
                    if gidx >= 4:
                        t.wait_ge(m.s_dve, dve0 + (gidx - 4) + 1)
                    last = None
                    for k in range(32):
                        last = t.matmul(self.pf32(gidx % 4, 0, 512),
                                        m.WO_BUF[:, e % 2, k],
                                        m.B1[:, k, tt * 512:tt * 512 + 512],
                                        start=(k == 0), stop=(k == 31))
                    last.then_inc(m.s_pe, 1)
                    m.inc(m.s_pe, 1)
                mm_done = m.val(m.s_pe)

                @m.block.vector
                def _(v, e=e, tt=tt, gidx=gidx, mm_done=mm_done,
                      ures_done=ures_done):
                    v.wait_ge(m.s_pe, mm_done)
                    v.wait_ge(m.s_ur if gidx % 2 == 0 else m.s_ur1, ures_done)
                    v.drain()
                    v.tensor_tensor(
                        m.X1PB[:, e, tt * 512:tt * 512 + 512],
                        self.pf32(gidx % 4, 0, 512),
                        m.URES[:, gidx % 2], ALU.add).then_inc(m.s_dve, 1)
                    m.inc(m.s_dve, 1)

        evac_all = m.val(m.s_dve)

        sq0 = m.val(m.s_dve)
        pe_before = m.val(m.s_pe)

        @m.block.vector
        def _(v, evac_all=evac_all, pe_before=pe_before):
            v.wait_ge(m.s_dve, evac_all)
            v.wait_ge(m.s_pe, pe_before)
            n = 0
            for tt in range(2):
                for e in range(16):
                    if n > 0:
                        v.wait_ge(m.s_pe, pe_before + n)
                    v.drain()
                    v.tensor_tensor(m.SQF2[:, n % 2],
                                    m.X1PB[:, e, tt * 512:tt * 512 + 512],
                                    m.X1PB[:, e, tt * 512:tt * 512 + 512],
                                    ALU.mult).then_inc(m.s_dve, 1)
                    m.inc(m.s_dve, 1)
                    n += 1

        pe_ln0 = m.val(m.s_pe)

        @m.block.tensor
        def _(t, evac_all=evac_all, sq0=sq0):
            t.wait_ge(m.s_dve, evac_all)
            n = 0
            for tt in range(2):
                for e in range(16):
                    t.matmul(self.pf32(4 + tt, 0, 512)[0:1, :], m.o128x1,
                             m.X1PB[:, e, tt * 512:tt * 512 + 512],
                             start=(e == 0), stop=(e == 15))
                    t.wait_ge(m.s_dve, sq0 + n + 1)
                    t.matmul(self.pf32(6 + tt, 0, 512)[0:1, :], m.o128x1,
                             m.SQF2[:, n % 2],
                             start=(e == 0), stop=(e == 15)
                             ).then_inc(m.s_pe, 1)
                    m.inc(m.s_pe, 1)
                    n += 1

        ln_pe = m.val(m.s_pe)
        for tt in range(2):
            @m.block.vector
            def _(v, tt=tt):
                v.wait_ge(m.s_pe, ln_pe)
                v.wait_ge(m.s_er, m.val(m.s_er))
                v.drain()
                v.tensor_scalar_mul(m.MROW[:, 0:512],
                                    self.pf32(4 + tt, 0, 512, p=1), 1.0 / D)
                v.drain()
                v.tensor_scalar_mul(m.RROW[:, 0:512],
                                    self.pf32(6 + tt, 0, 512, p=1), 1.0 / D)
                v.drain()
                v.tensor_tensor(m.SGM[0:1, 0:512], m.MROW[:, 0:512],
                                m.MROW[:, 0:512], ALU.mult)
                v.drain()
                v.tensor_tensor(m.RROW[:, 0:512], m.RROW[:, 0:512],
                                m.SGM[0:1, 0:512], ALU.subtract)
                v.drain()
                v.tensor_scalar_add(m.RROW[:, 0:512], m.RROW[:, 0:512],
                                    EPS_LN).then_inc(m.s_dve, 1)
                m.inc(m.s_dve, 1)
            var_done = m.val(m.s_dve)

            @m.block.scalar
            def _(sc, var_done=var_done):
                sc.wait_ge(m.s_dve, var_done)
                sc.drain()
                sc.activation(m.RROW[:, 0:512], m.RROW[:, 0:512], AF.Sqrt).then_inc(m.s_act, 1)
                m.inc(m.s_act, 1)
            st_done = m.val(m.s_act)

            @m.block.vector
            def _(v, st_done=st_done):
                v.wait_ge(m.s_act, st_done)
                v.drain()
                v.reciprocal(m.RROW[:, 0:512], m.RROW[:, 0:512]).then_inc(m.s_dve, 1)
                m.inc(m.s_dve, 1)
            rr_done = m.val(m.s_dve)
            er0 = m.val(m.s_er)

            @m.block.gpsimd
            def _(g, rr_done=rr_done, er0=er0):
                g.wait_ge(m.s_dve, rr_done)
                g.dma_start(out=m.rst_h[:, 0:512], in_=m.RROW[:, 0:512]
                            ).then_inc(m.s_er, 16)
                m.inc(m.s_er, 16)
                g.dma_start(out=m.rst_h[:, 512:1024], in_=m.MROW[:, 0:512]
                            ).then_inc(m.s_er, 16)
                m.inc(m.s_er, 16)
                g.wait_ge(m.s_er, er0 + 32)
                g.dma_start(out=m.MRB[:, 0], in_=m.rst_h[:, 0:512]
                            .partition_broadcast(128)).then_inc(m.s_er, 16)
                m.inc(m.s_er, 16)
                g.dma_start(out=m.MRB[:, 1], in_=m.rst_h[:, 512:1024]
                            .partition_broadcast(128)).then_inc(m.s_er, 16)
                m.inc(m.s_er, 16)
            mrb_done = m.val(m.s_er)

            @m.block.vector
            def _(v, tt=tt, mrb_done=mrb_done):
                v.wait_ge(m.s_er, mrb_done)
                for e in range(16):
                    v.drain()
                    v.tensor_tensor(m.X1PB[:, e, tt * 512:tt * 512 + 512],
                                    m.X1PB[:, e, tt * 512:tt * 512 + 512],
                                    m.MRB[:, 1], ALU.subtract)
                    v.drain()
                    i = v.tensor_tensor(
                        m.X1PB[:, e, tt * 512:tt * 512 + 512],
                        m.X1PB[:, e, tt * 512:tt * 512 + 512],
                        m.MRB[:, 0], ALU.mult)
                i.then_inc(m.s_dve, 1)
                m.inc(m.s_dve, 1)
            norm_done = m.val(m.s_dve)

        fin = m.val(m.s_dve)

        @m.block.sync
        def _(s, tau=tau, fin=fin):
            s.wait_ge(m.s_dve, fin)
            s.dma_start(out=m.x1park_h[:, :, tau * TAU:tau * TAU + TAU],
                        in_=m.X1PB).then_inc(m.s_st, 16)
            m.inc(m.s_st, 16)

    def _f1(self):
        m = self
        X1 = self.sb_at("X1F", [128, 16, T], F16, self.sballoc["B1"][0])
        m.X1V = X1
        WB4 = self.sb_at("WB4", [128, 4, 16, 128], F16,
                         self.sballoc["XTM"][0])
        GB = self.sb_at("GB2", [128, 2, 2048], F16, self.sballoc["CONVACC"][0])
        UB = self.sb_at("UB2", [128, 2, 2048], F16, self.sballoc["SCL"][0])
        HB8 = self.sb_at("HB8", [128, 8, 2048], F16, self.sballoc["dt_park"][0])
        AB4 = self.sb_at("AB4", [128, 4, 2048], F16, self.sballoc["W16"][0])
        F1S = self.sb_at("F1Sv", [128, 4, 2048], F16, self.sballoc["AR_U"][0])
        pre_pe = m.val(m.s_pe)
        pre_dve = m.val(m.s_dve)
        pre_act = m.val(m.s_act)
        pre_st = m.val(m.s_st)

        @m.block.sync
        def _(s):
            s.wait_ge(m.s_pe, pre_pe)
            s.wait_ge(m.s_dve, pre_dve)
            s.wait_ge(m.s_act, pre_act)
            s.wait_ge(m.s_st, pre_st)
            s.dma_start(out=X1, in_=m.x1park_h).then_inc(m.s_ur, 16)
            m.inc(m.s_ur, 16)
        x1_done = m.val(m.s_ur)

        sl = [m.s_sl0, m.s_sl1]
        silu_inc = {}
        ucp_inc = {}
        hb_inc = {}
        hb_inc_d = {}
        sp_inc = {}
        st_inc = {}

        def sparsify(g):
            s0 = (4 * g) % 8

            @m.block.vector
            def _(v, g=g, s0=s0):
                A0, A1_, A2_, A3_ = (AB4[:, i] for i in range(4))
                S1, T1, S2, T2 = (F1S[:, i] for i in range(4))
                for i in range(4):
                    v.drain()
                    v.tensor_scalar(AB4[:, i], HB8[:, s0 + i], -1.0, None,
                                    ALU.mult)
                    v.drain()
                    v.tensor_tensor(AB4[:, i], AB4[:, i], HB8[:, s0 + i],
                                    ALU.max)
                v.drain()
                v.tensor_tensor(S1, A0, A1_, ALU.max)
                v.tensor_tensor(T1, A0, A1_, ALU.min)
                v.tensor_tensor(S2, A2_, A3_, ALU.max)
                v.tensor_tensor(T2, A2_, A3_, ALU.min)
                v.drain()
                v.tensor_tensor(S1, S1, S2, ALU.min)
                v.tensor_tensor(T1, T1, T2, ALU.max)
                v.drain()
                v.tensor_tensor(S1, S1, T1, ALU.max)
                v.drain()
                for i in range(4):
                    v.tensor_tensor(AB4[:, i], AB4[:, i], S1, ALU.is_ge)
                v.drain()
                for i in range(4):
                    ii = v.tensor_tensor(HB8[:, s0 + i], HB8[:, s0 + i],
                                         AB4[:, i], ALU.mult)
                ii.then_inc(m.s_dve, 1)
                m.inc(m.s_dve, 1)
            sp_inc[g] = m.val(m.s_dve)

            @m.block.sync
            def _(s, g=g, s0=s0):
                s.wait_ge(m.s_dve, sp_inc[g])
                for i in range(4):
                    s.dma_start(out=m.hscr_h[4 * g + i], in_=HB8[:, s0 + i]
                                ).then_inc(m.s_st, 16)
                    m.inc(m.s_st, 16)
            st_inc[g] = m.val(m.s_st)

        mmv = {}
        sl4 = [m.s_sl0, m.s_sl1, m.s_sl2, m.s_sl3]
        for f in range(64):
            wld = {}
            for wi, wh in [(0, m.wG_h), (1, m.wU_h)]:
                u = 2 * f + wi
                @m.block.gpsimd
                def _(g, f=f, wi=wi, wh=wh, u=u):
                    if u >= 4:
                        g.wait_ge(m.s_pe, mmv[u - 4])
                    g.dma_start(out=WB4[:, u % 4], in_=wh[f]
                                ).then_inc(sl4[u % 4], 16)
                    m.inc(sl4[u % 4], 16)
                wld[wi] = (sl4[u % 4], m.val(sl4[u % 4]))
            g_done = wld[0]
            u_done = wld[1]

            @m.block.tensor
            def _(t, f=f, g_done=g_done, u_done=u_done):
                if f == 0:
                    t.wait_ge(m.s_ur, x1_done)
                t.wait_ge(g_done[0], g_done[1])
                if f >= 1:
                    t.wait_ge(m.s_act, silu_inc[f - 1])
                last = None
                for k in range(16):
                    for tt in range(4):
                        last = t.matmul(self.pf32(tt, 0, 512),
                                        WB4[:, (2 * f) % 4, k],
                                        X1[:, k, tt * 512:tt * 512 + 512],
                                        start=(k == 0), stop=(k == 15))
                last.then_inc(m.s_pe, 1)
                m.inc(m.s_pe, 1)
                mmv[2 * f] = m.val(m.s_pe)
                t.wait_ge(u_done[0], u_done[1])
                if f >= 1:
                    t.wait_ge(m.s_act, ucp_inc[f - 1])
                last = None
                for k in range(16):
                    for tt in range(4):
                        last = t.matmul(self.pf32(4 + tt, 0, 512),
                                        WB4[:, (2 * f + 1) % 4, k],
                                        X1[:, k, tt * 512:tt * 512 + 512],
                                        start=(k == 0), stop=(k == 15))
                last.then_inc(m.s_pe, 1)
                m.inc(m.s_pe, 1)
                mmv[2 * f + 1] = m.val(m.s_pe)
            gmm = m.val(m.s_pe) - 1
            umm = m.val(m.s_pe)
            dve_snap = m.val(m.s_dve)

            @m.block.scalar
            def _(sc, f=f, gmm=gmm, umm=umm, dve_snap=dve_snap):
                sc.wait_ge(m.s_pe, gmm)
                if f >= 2:
                    sc.wait_ge(m.s_dve, hb_inc_d[f - 2])
                sc.drain()
                sc.activation(GB[:, f % 2], self.PS_A[:, 0:2048], AF.Silu
                              ).then_inc(m.s_act, 1)
                m.inc(m.s_act, 1)
                silu_inc[f] = m.val(m.s_act)
                sc.wait_ge(m.s_pe, umm)
                sc.drain()
                sc.copy(UB[:, f % 2], self.PS_B[:, 0:2048]
                        ).then_inc(m.s_act, 1)
                m.inc(m.s_act, 1)
                ucp_inc[f] = m.val(m.s_act)

            @m.block.vector
            def _(v, f=f):
                v.wait_ge(m.s_act, ucp_inc[f])
                if f >= 8:
                    v.wait_ge(m.s_st, st_inc[(f - 8) // 4])
                v.drain()
                v.tensor_tensor(HB8[:, f % 8], GB[:, f % 2], UB[:, f % 2],
                                ALU.mult).then_inc(m.s_dve, 1)
                m.inc(m.s_dve, 1)
            hb_inc_d[f] = m.val(m.s_dve)
            if f >= 6 and (f - 6) % 4 == 0:
                sparsify((f - 6) // 4)
        sparsify(15)

    def _f2(self):
        m = self
        X1 = m.X1V
        WDB = self.sb_at("WDBv", [128, 4, 2, 512], F16, self.sballoc["WBUF"][0])
        HB2 = self.sb_at("HB2v", [128, 4, 2, 1024], F16,
                         self.sballoc["s_run"][0])
        sl = [m.s_sl0, m.s_sl1, m.s_sl2, m.s_sl3]
        hs = [m.s_ur, m.s_ur1, m.s_ur2, m.s_ur3]
        pre_st = m.val(m.s_st)

        for eq in range(4):
            for ttp in range(2):
                ev0 = m.val(m.s_dve)
                pe00 = m.val(m.s_pe)
                for fp in range(32):
                    f = 2 * fp
                    snap_pe = m.val(m.s_pe)

                    @m.block.gpsimd
                    def _(g, eq=eq, fp=fp, f=f, snap_pe=snap_pe, pe00=pe00):
                        if fp >= 4:
                            g.wait_ge(m.s_pe, snap_pe - 3)
                        elif pe00 > 0:
                            g.wait_ge(m.s_pe, pe00)
                        g.dma_start(
                            out=WDB[:, fp % 4].rearrange(
                                "p f j -> p (f j)").rearrange(
                                "p (f e j) -> p f e j", f=2, e=4),
                            in_=m.wD_h[eq, f:f + 2].rearrange(
                                "f e p j -> p f e j")
                            ).then_inc(sl[fp % 4], 16)
                        m.inc(sl[fp % 4], 16)
                    wd_done = m.val(sl[fp % 4])

                    @m.block.sync
                    def _(s, eq=eq, ttp=ttp, fp=fp, f=f, snap_pe=snap_pe,
                          pe00=pe00, pre_st=pre_st):
                        if fp >= 4:
                            s.wait_ge(m.s_pe, snap_pe - 3)
                        else:
                            s.wait_ge(m.s_st, pre_st)
                            s.wait_ge(m.s_pe, pe00)
                        s.dma_start(
                            out=HB2[:, fp % 4],
                            in_=m.hscr_h[f:f + 2, :,
                                         ttp * 1024:ttp * 1024 + 1024]
                            .rearrange("f p t -> p f t")
                            ).then_inc(hs[fp % 4], 16)
                        m.inc(hs[fp % 4], 16)
                    h_done = m.val(hs[fp % 4])

                    @m.block.tensor
                    def _(t, eq=eq, ttp=ttp, fp=fp, wd_done=wd_done,
                          h_done=h_done, ev0=ev0):
                        t.wait_ge(sl[fp % 4], wd_done)
                        t.wait_ge(hs[fp % 4], h_done)
                        if fp == 0:
                            t.wait_ge(m.s_dve, ev0)
                        last = None
                        for fi in range(2):
                            for e2 in range(4):
                                for tt in range(2):
                                    last = t.matmul(
                                        self.pf32(e2 * 2 + tt, 0, 512),
                                        WDB[:, fp % 4, fi,
                                            e2 * 128:e2 * 128 + 128],
                                        HB2[:, fp % 4, fi,
                                            tt * 512:tt * 512 + 512],
                                        start=(fp == 0 and fi == 0),
                                        stop=(fp == 31 and fi == 1))
                        last.then_inc(m.s_pe, 1)
                        m.inc(m.s_pe, 1)
                dn_done = m.val(m.s_pe)

                @m.block.vector
                def _(v, eq=eq, ttp=ttp, dn_done=dn_done):
                    v.wait_ge(m.s_pe, dn_done)
                    for e2 in range(4):
                        for tt in range(2):
                            c = eq * 4 + e2
                            col = ttp * 1024 + tt * 512
                            i = v.tensor_tensor(
                                X1[:, c, col:col + 512],
                                X1[:, c, col:col + 512],
                                self.pf32(e2 * 2 + tt, 0, 512), ALU.add)
                    i.then_inc(m.s_dve, 1)
                    m.inc(m.s_dve, 1)

    def _f3(self):
        m = self
        X1 = m.X1V
        MRB4 = self.sb_at("MRB4", [128, 4, 512], F32, self.sballoc["prevs"][0])
        f2_done = m.val(m.s_dve)
        stats_read = {}
        norm_done = {}
        for tq in range(4):
            par = tq % 2
            bA, bB = 4 + 2 * par, 5 + 2 * par
            sq0 = m.val(m.s_dve)
            pe_before = m.val(m.s_pe)

            @m.block.vector
            def _(v, tq=tq, pe_before=pe_before, f2_done=f2_done):
                if tq == 0:
                    v.wait_ge(m.s_dve, f2_done)
                n = 0
                for e in range(16):
                    if n > 0:
                        v.wait_ge(m.s_pe, pe_before + n)
                    v.drain()
                    v.tensor_tensor(m.SQF2[:, tq % 2],
                                    X1[:, e, tq * 512:tq * 512 + 512],
                                    X1[:, e, tq * 512:tq * 512 + 512],
                                    ALU.mult).then_inc(m.s_dve, 1)
                    m.inc(m.s_dve, 1)
                    n += 1

            @m.block.tensor
            def _(t, tq=tq, sq0=sq0, f2_done=f2_done, bA=bA, bB=bB):
                if tq == 0:
                    t.wait_ge(m.s_dve, f2_done)
                elif tq >= 2:
                    t.wait_ge(m.s_dve, stats_read[tq - 2])
                for e in range(16):
                    t.matmul(self.pf32(bA, 0, 512)[0:1, :], m.o128x1,
                             X1[:, e, tq * 512:tq * 512 + 512],
                             start=(e == 0), stop=(e == 15))
                    t.wait_ge(m.s_dve, sq0 + e + 1)
                    t.matmul(self.pf32(bB, 0, 512)[0:1, :], m.o128x1,
                             m.SQF2[:, tq % 2],
                             start=(e == 0), stop=(e == 15)
                             ).then_inc(m.s_pe, 1)
                    m.inc(m.s_pe, 1)
            ln_pe = m.val(m.s_pe)

            @m.block.vector
            def _(v, tq=tq, ln_pe=ln_pe, bA=bA, bB=bB,
                  er_snap=m.val(m.s_er)):
                v.wait_ge(m.s_pe, ln_pe)
                v.wait_ge(m.s_er, er_snap)
                v.drain()
                v.tensor_scalar_mul(m.MROW[:, 0:512],
                                    self.pf32(bA, 0, 512, p=1), 1.0 / D)
                v.drain()
                v.tensor_scalar_mul(m.RROW[:, 0:512],
                                    self.pf32(bB, 0, 512, p=1), 1.0 / D)
                v.drain()
                v.tensor_tensor(m.SGM[0:1, 0:512], m.MROW[:, 0:512],
                                m.MROW[:, 0:512], ALU.mult)
                v.drain()
                v.tensor_tensor(m.RROW[:, 0:512], m.RROW[:, 0:512],
                                m.SGM[0:1, 0:512], ALU.subtract)
                v.drain()
                v.tensor_scalar_add(m.RROW[:, 0:512], m.RROW[:, 0:512],
                                    EPS_LN).then_inc(m.s_dve, 1)
                m.inc(m.s_dve, 1)
            stats_read[tq] = m.val(m.s_dve)
            var_done = m.val(m.s_dve)

            @m.block.scalar
            def _(sc, var_done=var_done):
                sc.wait_ge(m.s_dve, var_done)
                sc.drain()
                sc.activation(m.RROW[:, 0:512], m.RROW[:, 0:512], AF.Sqrt).then_inc(m.s_act, 1)
                m.inc(m.s_act, 1)
            st_done = m.val(m.s_act)

            @m.block.vector
            def _(v, st_done=st_done):
                v.wait_ge(m.s_act, st_done)
                v.drain()
                v.reciprocal(m.RROW[:, 0:512], m.RROW[:, 0:512]).then_inc(m.s_dve, 1)
                m.inc(m.s_dve, 1)
            rr_done = m.val(m.s_dve)
            er0 = m.val(m.s_er)

            @m.block.gpsimd
            def _(g, tq=tq, rr_done=rr_done, er0=er0, par=par):
                g.wait_ge(m.s_dve, rr_done)
                g.dma_start(out=m.rst_h[:, 0:512], in_=m.RROW[:, 0:512]
                            ).then_inc(m.s_er, 16)
                m.inc(m.s_er, 16)
                g.dma_start(out=m.rst_h[:, 512:1024], in_=m.MROW[:, 0:512]
                            ).then_inc(m.s_er, 16)
                m.inc(m.s_er, 16)
                g.wait_ge(m.s_er, er0 + 32)
                g.dma_start(out=MRB4[:, 2 * par], in_=m.rst_h[:, 0:512]
                            .partition_broadcast(128)).then_inc(m.s_er, 16)
                m.inc(m.s_er, 16)
                g.dma_start(out=MRB4[:, 2 * par + 1], in_=m.rst_h[:, 512:1024]
                            .partition_broadcast(128)).then_inc(m.s_er, 16)
                m.inc(m.s_er, 16)
            mrb_done = m.val(m.s_er)

            @m.block.vector
            def _(v, tq=tq, mrb_done=mrb_done, par=par,
                  st_snap=m.val(m.s_wdma)):
                v.wait_ge(m.s_er, mrb_done)
                v.wait_ge(m.s_wdma, st_snap)
                for e in range(16):
                    v.drain()
                    v.tensor_tensor(X1[:, e, tq * 512:tq * 512 + 512],
                                    X1[:, e, tq * 512:tq * 512 + 512],
                                    MRB4[:, 2 * par + 1], ALU.subtract)
                    v.drain()
                    i = v.tensor_tensor(X1[:, e, tq * 512:tq * 512 + 512],
                                        X1[:, e, tq * 512:tq * 512 + 512],
                                        MRB4[:, 2 * par], ALU.mult)
                i.then_inc(m.s_dve, 1)
                m.inc(m.s_dve, 1)
            norm_done[tq] = m.val(m.s_dve)

            @m.block.gpsimd
            def _(g, tq=tq, nd=norm_done[tq]):
                g.wait_ge(m.s_dve, nd)
                g.dma_start(
                    out=m.out_h.rearrange("(c p) t -> p c t", p=128)
                    [:, :, tq * 512:tq * 512 + 512],
                    in_=X1[:, :, tq * 512:tq * 512 + 512]
                    ).then_inc(m.s_wdma, 16)
                m.inc(m.s_wdma, 16)

    def _dump_tau(self, tau):
        m = self
        d = m.dbg_out(f"dbg_b1t{tau}", [128, 34, TAU + HALO], F16)
        if d is not None:
            hof = HALO if tau == 0 else 0
            @m.block.sync
            def _(s):
                s.wait_ge(m.s_act, m.val(m.s_act))
                s.wait_ge(m.s_dve, m.val(m.s_dve))
                s.dma_start(out=d[:, :, hof:hof + TAU],
                            in_=m.B1[:, 0:34, hof:hof + TAU]).then_inc(m.s_st, 16)
                m.inc(m.s_st, 16)
                s.wait_ge(m.s_st, m.val(m.s_st))

    def _finish_debug(self):
        m = self

        d3 = m.dbg_out("dbg_srun", [64, DIN], F32)
        if d3 is not None:
            @m.block.sync
            def _(s):
                s.wait_ge(m.s_dve, m.val(m.s_dve))
                s.dma_start(out=d3, in_=m.s_run).then_inc(m.s_st, 16)
                m.inc(m.s_st, 16)
        d4 = m.dbg_out("dbg_a", [32, T], F32)
        if d4 is not None:
            @m.block.sync
            def _(s):
                s.wait_ge(m.s_dve, m.val(m.s_dve))
                s.dma_start(out=d4, in_=m.a_park).then_inc(m.s_st, 16)
                m.inc(m.s_st, 16)
        d5 = m.dbg_out("dbg_y", [128, 32, T], F16)
        if d5 is not None:
            @m.block.sync
            def _(s):
                s.wait_ge(m.s_yd, m.val(m.s_yd))
                s.dma_start(out=d5, in_=m.ypark_h).then_inc(m.s_st, 16)
                m.inc(m.s_st, 16)
        d6 = m.dbg_out("dbg_x1", [128, 16, T], F16)
        if d6 is not None:
            @m.block.sync
            def _(s):
                s.wait_ge(m.s_st, m.val(m.s_st))
                s.dma_start(out=d6, in_=m.x1park_h).then_inc(m.s_st, 16)
                m.inc(m.s_st, 16)
        d2 = m.dbg_out("dbg_dt", [32, T], F32)
        if d2 is not None:
            @m.block.sync
            def _(s):
                s.wait_ge(m.s_act, m.val(m.s_act))
                s.dma_start(out=d2, in_=m.dt_park).then_inc(m.s_st, 16)
                m.inc(m.s_st, 16)

    def _final_wait(self):
        m = self

        @m.block.sync
        def _(s):
            for name in ["s_wdma", "s_ldma", "s_pe", "s_dve", "s_act", "s_gp",
                         "s_st", "s_cc", "s_wu", "s_sl0", "s_sl1", "s_yd",
                         "s_ai", "s_ai2", "s_af", "s_er", "s_er2", "s_tp",
                      "s_ur", "s_ur1", "s_sl2", "s_sl3", "s_ur2", "s_ur3"]:
                sem = getattr(m, name)
                s.wait_ge(sem, m.val(sem))



def prep_inputs(inputs):
    W = {}
    ip = np.ascontiguousarray(inputs["L0_attn_in_proj_w"])
    rows = np.zeros((NXC * 128, D), np.float32)
    rows[:4096] = ip[4096:8192]
    rows[4096:4160] = ip[8192:8256]
    rows[4224:4288] = ip[8256:8320]
    rows[4352:4384] = ip[8320:8352]
    W["wA"] = np.ascontiguousarray(
        rows.reshape(NXC, 128, 16, 128).transpose(0, 3, 2, 1))
    W["wZ"] = np.ascontiguousarray(
        ip[0:4096].reshape(32, 128, 16, 128).transpose(0, 3, 2, 1))
    op = np.ascontiguousarray(inputs["L0_attn_o_proj_w"])
    W["wO"] = np.ascontiguousarray(
        op.reshape(16, 128, 32, 128).transpose(0, 3, 2, 1))

    f_idx = np.arange(64)
    p_idx = np.arange(128)
    iperm = (4 * (128 * (f_idx[:, None] // 4) + p_idx[None, :])
             + (f_idx[:, None] % 4))
    wg = np.ascontiguousarray(inputs["L0_ffn_w_gate"])
    wu = np.ascontiguousarray(inputs["L0_ffn_w_up"])
    wd = np.ascontiguousarray(inputs["L0_ffn_w_down"])
    wgp = wg[:, iperm.reshape(-1)].reshape(D, 64, 128)
    W["wG"] = np.ascontiguousarray(
        wgp.reshape(16, 128, 64, 128).transpose(2, 1, 0, 3))
    wup = wu[:, iperm.reshape(-1)].reshape(D, 64, 128)
    W["wU"] = np.ascontiguousarray(
        wup.reshape(16, 128, 64, 128).transpose(2, 1, 0, 3))
    wdp = wd[iperm.reshape(-1), :].reshape(64, 128, 2048)
    W["wD"] = np.ascontiguousarray(
        wdp.reshape(64, 128, 4, 4, 128).transpose(2, 0, 3, 1, 4))

    cw = np.asarray(inputs["L0_attn_conv_w"], np.float32)
    cb = np.asarray(inputs["L0_attn_conv_b"], np.float32)
    cwp = np.zeros((34 * 128, 4), np.float32)
    cbp = np.zeros((34 * 128,), np.float32)
    cwp[:4096] = cw[:4096]
    cbp[:4096] = cb[:4096]
    cwp[4096:4160] = cw[4096:4160]
    cbp[4096:4160] = cb[4096:4160]
    cwp[4224:4288] = cw[4160:4224]
    cbp[4224:4288] = cb[4160:4224]
    W["convw"] = np.ascontiguousarray(
        cwp.reshape(34, 128, 4).transpose(1, 0, 2).reshape(128, 34 * 4))
    W["convb"] = np.ascontiguousarray(cbp.reshape(34, 128).T)

    W["alog"] = np.asarray(inputs["L0_attn_A_log"], np.float32).reshape(32, 1)
    W["dtb"] = np.asarray(inputs["L0_attn_dt_bias"], np.float32).reshape(32, 1)
    W["dskip"] = np.asarray(inputs["L0_attn_D_skip"], np.float32).reshape(1, 32)
    W["nw"] = np.ascontiguousarray(
        np.asarray(inputs["L0_attn_norm_w"], np.float32).reshape(32, 128).T)
    W["identc"] = np.eye(128, dtype=np.float32)
    W["ident32c"] = np.eye(128, dtype=np.float32)
    i_, j_ = np.mgrid[0:128, 0:128]
    W["maskTc"] = (j_ >= i_).astype(np.float32)
    W["o1x128"] = np.ones((1, 128), np.float32)
    W["o128x1"] = np.ones((128, 1), np.float32)
    oc = np.zeros((128, 32, 32), np.float32)
    for h in range(32):
        oc[:, h, h] = 1.0
    W["ocols"] = oc.reshape(128, 32 * 32)
    W["oscan"] = np.ones((32, CH), np.float32)
    return W


def prep_core_inputs(inputs, W, core):
    uT = inputs["_uT"]
    d = dict(W)
    d["u"] = np.ascontiguousarray(uT[:, core * T:core * T + T + HALO])
    g = (np.arange(8) < core).astype(np.float32)
    dd = (np.arange(8) <= core - 1).astype(np.float32)
    d["gates"] = np.concatenate([g, dd]).reshape(1, 16).astype(np.float32)
    return d


_BUILD_CACHE = {}


def kernel(**inputs):
    u = np.asarray(inputs["L0_attn_u"], np.float32)
    uT = np.zeros((D, HALO + 16384), np.float32)
    uT[:, HALO:] = u[0].T
    inputs = dict(inputs)
    inputs["_uT"] = uT

    W = prep_inputs(inputs)

    mk = MK()
    nc = mk.build()
    in_maps = [prep_core_inputs(inputs, W, c) for c in range(NCORES)]
    res = run_bass_kernel_spmd(nc, in_maps, core_ids=list(range(NCORES)))
    outs = [res.results[c]["out"] for c in range(NCORES)]
    full = np.concatenate([o.T for o in outs], axis=0)
    return full.reshape(1, 16384, D).astype(np.float32)


# version 12
# speedup vs baseline: 1.0004x; 1.0004x over previous
import contextlib
import numpy as np
import concourse.bass as bass
import concourse.mybir as mybir
from concourse.bass_utils import run_bass_kernel_spmd

F32 = mybir.dt.float32
F16 = mybir.dt.float16
AF = mybir.ActivationFunctionType
ALU = mybir.AluOpType

NCORES = 8
T = 2048
TAU = 1024
HALO = 3
CH = 256
D = 2048
DIN = 4096
NH = 32
FFI = 8192
NXC = 35
EPS_RMS = 1e-6
EPS_LN = 1e-5
AG_N = 64 * DIN + 32
SBUF_BASE = 16512
SBUF_MAX = 212736


class MK:
    def __init__(self, debug=()):
        self.debug = set(debug)
        nc = bass.Bass("TRN2", target_bir_lowering=False)
        self.nc = nc
        self.semvals = {}
        self.dbg = {}
        self.sballoc = {}

        dt_ = nc.dram_tensor
        self.ident32_h = dt_("ident32c", [128, 128], F32, kind="ExternalInput").ap()
        self.u_h = dt_("u", [D, T + HALO], F32, kind="ExternalInput").ap()
        self.wA_h = dt_("wA", [NXC, 128, 16, 128], F32, kind="ExternalInput").ap()
        self.wZ_h = dt_("wZ", [32, 128, 16, 128], F32, kind="ExternalInput").ap()
        self.wO_h = dt_("wO", [16, 128, 32, 128], F32, kind="ExternalInput").ap()
        self.wG_h = dt_("wG", [64, 128, 16, 128], F32, kind="ExternalInput").ap()
        self.wU_h = dt_("wU", [64, 128, 16, 128], F32, kind="ExternalInput").ap()
        self.wD_h = dt_("wD", [4, 64, 4, 128, 128], F32, kind="ExternalInput").ap()
        self.convw_h = dt_("convw", [128, 34 * 4], F32, kind="ExternalInput").ap()
        self.convb_h = dt_("convb", [128, 34], F32, kind="ExternalInput").ap()
        self.alog_h = dt_("alog", [32, 1], F32, kind="ExternalInput").ap()
        self.dtb_h = dt_("dtb", [32, 1], F32, kind="ExternalInput").ap()
        self.dskip_h = dt_("dskip", [1, 32], F32, kind="ExternalInput").ap()
        self.nw_h = dt_("nw", [128, 32], F32, kind="ExternalInput").ap()
        self.ident_h = dt_("identc", [128, 128], F32, kind="ExternalInput").ap()
        self.maskT_h = dt_("maskTc", [128, 128], F32, kind="ExternalInput").ap()
        self.o1x128_h = dt_("o1x128", [1, 128], F32, kind="ExternalInput").ap()
        self.o128x1_h = dt_("o128x1", [128, 1], F32, kind="ExternalInput").ap()
        self.oscan_h = dt_("oscan", [32, CH], F32, kind="ExternalInput").ap()
        self.gates_h = dt_("gates", [1, 16], F32, kind="ExternalInput").ap()
        self.ocols_h = dt_("ocols", [128, 32 * 32], F32, kind="ExternalInput").ap()

        self.out_h = dt_("out", [D, T], F32, kind="ExternalOutput").ap()

        self.ypark_h = dt_("ypark", [128, 32, T], F16, kind="Internal").ap()
        self.hscr_h = dt_("hscr", [64, 128, T], F16, kind="Internal").ap()
        self.agin_h = dt_("agin", [AG_N], F32, kind="Internal").ap()
        self.agout_h = dt_("agout", [NCORES, AG_N], F32, kind="Internal").ap()
        self.aflat_h = dt_("aflat", [1, 8192], F32, kind="Internal").ap()
        self.erow_h = dt_("erowh", [1, 32], F32, kind="Internal").ap()
        self.rst_h = dt_("rsth", [1, 1024], F32, kind="Internal").ap()
        self.rstdh_h = dt_("rstdh", [32, 1024], F32, kind="Internal").ap()
        self.aflat8_h = dt_("aflat8", [8, 8192], F32, kind="Internal").ap()
        self.erow8_h = dt_("erow8", [8, 32], F32, kind="Internal").ap()
        self.x1park_h = dt_("x1park", [128, 16, T], F16, kind="Internal").ap()

    def dbg_out(self, name, shape, dtype):
        if name in self.debug and name not in self.dbg:
            self.dbg[name] = self.nc.dram_tensor(
                name, shape, dtype, kind="ExternalOutput").ap()
        return self.dbg.get(name)

    def sem(self, name):
        s = self.es.enter_context(self.nc.semaphore(name))
        self.semvals[s.name] = 0
        setattr(self, name, s)
        return s

    def inc(self, s, n=1):
        self.semvals[s.name] += n
        return self.semvals[s.name]

    def val(self, s):
        return self.semvals[s.name]

    def sb_at(self, name, shape, dtype, off):
        nbytes = int(np.prod(shape[1:])) * mybir.dt.size(dtype)
        assert off + nbytes <= SBUF_MAX, (name, off, nbytes)
        self.sballoc[name] = (off, nbytes)
        return self.nc.alloc_sbuf_tensor_at(
            name, list(shape), dtype, offset=SBUF_BASE + off).ap()

    def build(self):
        nc = self.nc
        with contextlib.ExitStack() as es:
            self.es = es
            self._alloc_sbuf()
            self._alloc_psum()
            for s in ["s_wdma", "s_ldma", "s_pe", "s_dve", "s_act", "s_gp",
                      "s_st", "s_cc", "s_wu", "s_sl0", "s_sl1", "s_yd",
                      "s_ai", "s_ai2", "s_af", "s_er", "s_er2", "s_tp",
                      "s_ur", "s_ur1", "s_sl2", "s_sl3", "s_ur2", "s_ur3"]:
                self.sem(s)
            self.block = es.enter_context(nc.Block())
            self._emit_all()
        return nc

    def _alloc_sbuf(self):
        off = 0

        def A(name, shape, dtype):
            nonlocal off
            t = self.sb_at(name, shape, dtype, off)
            off += (self.sballoc[name][1] + 31) // 32 * 32
            return t

        self.ident = A("identS", [128, 128], F16)
        self.ident32 = A("ident32S", [128, 128], F32)
        self.maskT = A("maskTS", [128, 128], F16)
        self.o1x128 = A("o1x128S", [1, 128], F16)
        self.o128x1 = A("o128x1S", [128, 1], F16)
        self.oscan = A("oscanS", [32, CH], F32)
        self.convw = A("convwS", [128, 34, 4], F32)
        self.convb = A("convbS", [128, 34], F32)
        self.alog = A("alogS", [32, 1], F32)
        self.dtb = A("dtbS", [32, 1], F32)
        self.acol = A("acolS", [32, 1], F32)
        self.dbc = A("dbcS", [128, 32], F32)
        self.nw = A("nwS", [128, 32], F32)
        self.gbc = A("gbcS", [64, 16], F32)
        self.atot = A("atotS", [32, 1], F32)
        self.acum = A("acumS", [32, 8], F32)
        self.ag_atot = A("agatotS", [64, 8], F32)
        self.erow = A("erowS", [64, 32], F32)
        self.er32 = A("er32S", [32, 1], F32)

        self.B1 = A("B1", [128, NXC, TAU + HALO], F16)
        self.dt_park = A("dt_park", [32, T], F32)
        self.a_park = A("a_park", [32, T], F32)
        self.c_park = A("c_park", [64, T], F16)
        self.s_run = A("s_run", [64, DIN], F32)
        self.prevs = A("prevs", [64, DIN], F16)
        self.carry3 = A("carry3", [128, 34, 4], F16)

        self.AR_U = A("AR_U", [128, 16 * (TAU + HALO)], F16)
        self.A_I = self.sb_at("A_I", [128, 8192], F32,
                              self.sballoc["AR_U"][0])
        self.EA16 = self.sb_at("EA16", [64, 4096], F16,
                               self.sballoc["AR_U"][0] + 24576)
        self.XTM = A("XTM", [128, 2 * DIN], F16)
        self.s_in = self.sb_at("s_in", [64, DIN], F32,
                               self.sballoc["XTM"][0])
        self.SCL = A("SCL", [128, DIN], F16)
        self.W16 = A("W16", [128, 8192], F16)
        self.SSB = A("SSB", [128, 384], F32)
        self.WBUF = A("WBUF", [128, 2, 16, 128], F16)
        self.CONVACC = A("CONVACC", [128, 2, 1024], F32)
        self.TT32 = self.sb_at("TT32", [128, 256], F32,
                               self.sballoc["CONVACC"][0])
        self.atall = A("atallS", [32, 8], F32)
        self.X1F = None
        self.SINKG = self.sb_at("SINKG", [64, 4096], F32,
                                self.sballoc["W16"][0])
        self.SINK16 = self.sb_at("SINK16", [64, 4096], F16,
                                 self.sballoc["SCL"][0])
        self.AB64 = self.sb_at("AB64", [64, 4096], F32,
                               self.sballoc["AR_U"][0])
        self.WO_BUF = self.sb_at("WO_BUF", [128, 2, 32, 128], F16,
                                 self.sballoc["s_run"][0])
        self.X1PB = self.sb_at("X1PB", [128, 16, TAU], F16,
                               self.sballoc["AR_U"][0])
        self.X1 = self.sb_at("X1", [128, 16, T], F16,
                             self.sballoc["B1"][0])
        self.RROW = self.sb_at("RROW", [1, 512], F32,
                               self.sballoc["CONVACC"][0] + 4096)
        self.MROW = self.sb_at("MROW", [1, 512], F32,
                               self.sballoc["CONVACC"][0] + 6144)
        self.MRB = self.sb_at("MRB", [128, 2, 512], F32,
                              self.sballoc["prevs"][0])
        self.MRB2 = self.sb_at("MRB2", [128, 2, 1024], F32,
                               self.sballoc["prevs"][0])
        self.ocols = self.sb_at("ocolsS", [128, 32, 32], F16,
                                self.sballoc["dt_park"][0])
        self.erow8sb = self.sb_at("erow8sb", [32, 8], F32,
                                  self.sballoc["agatotS"][0])
        self.erow8 = self.sb_at("erow8", [64, 256], F32,
                                self.sballoc["SSB"][0])
        self.AB64s = [self.sb_at("AB64a", [64, 4096], F32,
                                 self.sballoc["AR_U"][0]),
                      self.sb_at("AB64b", [64, 4096], F32,
                                 self.sballoc["AR_U"][0] + 16384)]
        self.EA16s = [self.sb_at("EA16a", [64, 4096], F16,
                                 self.sballoc["W16"][0]),
                      self.sb_at("EA16b", [64, 4096], F16,
                                 self.sballoc["W16"][0] + 8192)]
        self.RST32 = self.sb_at("RST32", [32, 1024], F32,
                                self.sballoc["CONVACC"][0])
        self.SGM = self.sb_at("SGM", [128, 1024], F32,
                              self.sballoc["CONVACC"][0] + 0)
        self.URES = self.sb_at("URES", [128, 2, 512], F32,
                               self.sballoc["CONVACC"][0] + 0)
        self.SQ16 = self.sb_at("SQ16", [128, 2048], F16,
                               self.sballoc["SCL"][0])
        self.SQF2 = self.sb_at("SQF2", [128, 2, 512], F16,
                               self.sballoc["SCL"][0])
        self.GBUF = self.sb_at("GBUF", [128, 2048], F16,
                               self.sballoc["CONVACC"][0])
        self.HBUF = self.sb_at("HBUF", [128, 4, 2048], F16,
                               self.sballoc["dt_park"][0])
        self.ABS4 = self.sb_at("ABS4", [128, 4, 2048], F16,
                               self.sballoc["s_run"][0])
        self.F1S = self.sb_at("F1S", [128, 4, 2048], F16,
                              self.sballoc["AR_U"][0])
        self.WDB = self.sb_at("WDB", [128, 2, 512], F16,
                              self.sballoc["WBUF"][0])
        self.HB2 = self.sb_at("HB2", [128, 2, 1024], F16,
                              self.sballoc["WBUF"][0] + 2048)
        self.OUT32 = self.sb_at("OUT32", [128, 16, 512], F32,
                                self.sballoc["AR_U"][0])
        self.SQF = self.sb_at("SQF", [128, 512], F16,
                              self.sballoc["SCL"][0])
        self.end_fixed = off
        assert off <= SBUF_MAX, off

    def _alloc_psum(self):
        self.PS_A = self.nc.alloc_psum_tensor("PS_A", [128, 2048], F32).ap()
        self.PS_B = self.nc.alloc_psum_tensor("PS_B", [128, 2048], F32).ap()

    def bank(self, i, w=512, p=128):
        base = self.PS_A if i < 4 else self.PS_B
        j = i % 4
        return base[0:p, 512 * j:512 * j + w]

    def bank16(self, i, w=128, p=128):
        base = self.PS_A if i < 4 else self.PS_B
        j = i % 4
        return base.bitcast(F16)[0:p, 1024 * j:1024 * j + w]

    def _emit_all(self):
        self._consts()
        for tau in range(2):
            self._a1(tau)
            self._conv(tau)
            self._a2_prep(tau)
            for kc in range(4):
                self._a2_chunk(tau, kc)
            self._a2_tau_end(tau)
            self._dump_tau(tau)
        self._gather()
        self._gather_corrprep()
        for tau in range(2):
            self._a3(tau)
            self._a4(tau)
        self._f1()
        self._f2()
        self._f3()
        self._finish_debug()
        self._final_wait()

    def _consts(self):
        m = self

        @m.block.gpsimd
        def _(g):
            g.memset(m.s_run, 0.0).then_inc(m.s_gp, 1)
            m.inc(m.s_gp, 1)
            for dst, src in [(m.ident, m.ident_h), (m.o1x128, m.o1x128_h),
                             (m.o128x1, m.o128x1_h), (m.maskT, m.maskT_h)]:
                g.dma_start(out=dst, in_=src).then_inc(m.s_wdma, 16)
                m.inc(m.s_wdma, 16)
            for dst, src, p in [(m.dbc, m.dskip_h, 128), (m.gbc, m.gates_h, 64)]:
                g.dma_start(out=dst, in_=src.partition_broadcast(p)
                            ).then_inc(m.s_wdma, 16)
                m.inc(m.s_wdma, 16)

        @m.block.sync
        def _(s):
            for dst, src in [(m.ident32, m.ident32_h),
                             (m.oscan, m.oscan_h),
                             (m.convb, m.convb_h), (m.alog, m.alog_h),
                             (m.dtb, m.dtb_h), (m.nw, m.nw_h)]:
                s.dma_start(out=dst, in_=src).then_inc(m.s_ldma, 16)
                m.inc(m.s_ldma, 16)
            s.dma_start(out=m.convw,
                        in_=m.convw_h.rearrange("p (j k) -> p j k", k=4)
                        ).then_inc(m.s_ldma, 16)
            m.inc(m.s_ldma, 16)

        @m.block.scalar
        def _(sc):
            sc.wait_ge(m.s_ldma, m.val(m.s_ldma))
            sc.wait_ge(m.s_wdma, m.val(m.s_wdma))
            sc.activation(m.acol, m.alog, AF.Exp, scale=1.0)
            sc.drain()
            sc.mul(m.acol, m.acol, -1.0)
            sc.nop().then_inc(m.s_act, 1)
            m.inc(m.s_act, 1)

    def _a1(self, tau):
        m = self
        TW = TAU + HALO if tau == 0 else TAU
        u0 = 0 if tau == 0 else TAU + HALO
        tts = [(i, min(i + 512, TW)) for i in range(0, TW, 512)]
        ntt = len(tts)
        dtc0 = tau * TAU
        hof = HALO if tau == 0 else 0

        pre_act = m.val(m.s_act)
        pre_dve = m.val(m.s_dve)
        pre_pe = m.val(m.s_pe)
        pre_st = m.val(m.s_st)
        pre_yd = m.val(m.s_yd)

        @m.block.gpsimd
        def _(g):
            g.wait_ge(m.s_act, pre_act)
            g.wait_ge(m.s_dve, pre_dve)
            g.wait_ge(m.s_pe, pre_pe)
            g.wait_ge(m.s_st, pre_st)
            g.wait_ge(m.s_yd, pre_yd)
            for si, (t0, t1) in enumerate(tts):
                if si > 0:
                    g.wait_ge(m.s_wu, m.val(m.s_wu))
                g.dma_start(
                    out=m.AR_U[:, 0:16 * TW].rearrange(
                        "p (k t) -> p k t", t=TW)[:, :, t0:t1],
                    in_=m.u_h[:, u0:u0 + TW].rearrange(
                        "(k p) t -> p k t", p=128)[:, :, t0:t1],
                ).then_inc(m.s_wu, 16)
                m.inc(m.s_wu, 16)
        uload = m.val(m.s_wu)

        evac0 = m.val(m.s_act)
        pe0 = m.val(m.s_pe)
        sl = [m.s_sl0, m.s_sl1]
        sl0_0, sl1_0 = m.val(m.s_sl0), m.val(m.s_sl1)
        slbase = [sl0_0, sl1_0]

        @m.block.gpsimd
        def _(g):
            for e in range(NXC):
                if e >= 2:
                    g.wait_ge(m.s_pe, pe0 + (e - 1) * ntt)
                g.dma_start(out=m.WBUF[:, e % 2], in_=m.wA_h[e]
                            ).then_inc(sl[e % 2], 16)
                m.inc(sl[e % 2], 16)

        @m.block.tensor
        def _(t):
            gi = 0
            for e in range(NXC):
                t.wait_ge(sl[e % 2], slbase[e % 2] + 16 * (e // 2 + 1))
                for ti, (t0, t1) in enumerate(tts):
                    if e == 0:
                        t.wait_ge(m.s_wu, uload - 16 * (ntt - ti - 1))
                    bk = gi % 6
                    if gi >= 6:
                        t.wait_ge(m.s_act, evac0 + (gi - 6 + 1))
                    last = None
                    for k in range(16):
                        last = t.matmul(self.bank(bk, t1 - t0),
                                        m.WBUF[:, e % 2, k],
                                        m.AR_U[:, 0:16 * TW].rearrange(
                                            "p (k t) -> p k t", t=TW)[:, k, t0:t1],
                                        start=(k == 0), stop=(k == 15))
                    last.then_inc(m.s_pe, 1)
                    m.inc(m.s_pe, 1)
                    gi += 1

        @m.block.scalar
        def _(sc):
            gi = 0
            for e in range(NXC):
                for ti, (t0, t1) in enumerate(tts):
                    bk = gi % 6
                    sc.wait_ge(m.s_pe, pe0 + gi + 1)
                    if e < 34:
                        i = sc.copy(m.B1[:, e, t0:t1], self.bank(bk, t1 - t0))
                    else:
                        lo = max(t0, hof)
                        if lo < t1:
                            i = sc.copy(
                                m.dt_park[:, dtc0 + lo - hof:dtc0 + t1 - hof],
                                self.bank(bk, t1 - t0, p=32)[:, lo - t0:t1 - t0])
                        else:
                            i = sc.nop()
                    i.then_inc(m.s_act, 1)
                    m.inc(m.s_act, 1)
                    gi += 1

    def _conv(self, tau):
        m = self
        TW = TAU + HALO if tau == 0 else TAU
        hof = HALO if tau == 0 else 0
        ntt = 3 if tau == 0 else 2
        pre = m.val(m.s_act)
        a1_evac0 = pre - NXC * ntt
        dve0 = m.val(m.s_dve)
        act0 = m.val(m.s_act)
        ldc = m.val(m.s_ldma)

        @m.block.vector
        def _(v):
            v.wait_ge(m.s_ldma, ldc)
            for j in range(34):
                v.wait_ge(m.s_act, a1_evac0 + (j + 1) * ntt)
                acc = m.CONVACC[:, j % 2]
                if j >= 2:
                    v.wait_ge(m.s_act, act0 + (j - 2) + 1)
                last = None
                if tau == 0:
                    for k in range(4):
                        src_ = m.B1[:, j, k:k + TAU]
                        if k == 0:
                            v.tensor_scalar(acc, src_, m.convw[:, j, 0:1],
                                            None, ALU.mult)
                        else:
                            v.drain()
                            last = v.scalar_tensor_tensor(
                                acc, src_, m.convw[:, j, k:k + 1],
                                acc, ALU.mult, ALU.add)
                else:
                    for k in range(4):
                        src_ = m.B1[:, j, k:k + TAU - 3]
                        if k == 0:
                            v.tensor_scalar(acc[:, 3:TAU], src_,
                                            m.convw[:, j, 0:1], None, ALU.mult)
                        else:
                            v.drain()
                            v.scalar_tensor_tensor(
                                acc[:, 3:TAU], src_, m.convw[:, j, k:k + 1],
                                acc[:, 3:TAU], ALU.mult, ALU.add)
                    for t in range(3):
                        first = True
                        for k in range(4):
                            p = t - 3 + k
                            src_ = (m.carry3[:, j, 3 + p:4 + p] if p < 0
                                    else m.B1[:, j, p:p + 1])
                            if first:
                                v.drain()
                                v.tensor_scalar(acc[:, t:t + 1], src_,
                                                m.convw[:, j, k:k + 1], None,
                                                ALU.mult)
                                first = False
                            else:
                                v.drain()
                                last = v.scalar_tensor_tensor(
                                    acc[:, t:t + 1], src_,
                                    m.convw[:, j, k:k + 1],
                                    acc[:, t:t + 1], ALU.mult, ALU.add)
                last.then_inc(m.s_dve, 1)
                m.inc(m.s_dve, 1)
                if tau == 0:
                    v.tensor_copy(m.carry3[:, j, 0:3], m.B1[:, j, TW - 3:TW]
                                  ).then_inc(m.s_dve, 1)
                    m.inc(m.s_dve, 1)

        nd = 2 if tau == 0 else 1

        @m.block.scalar
        def _(sc):
            for j in range(34):
                sc.wait_ge(m.s_dve, dve0 + nd * (j + 1))
                sc.activation(m.B1[:, j, hof:hof + TAU], m.CONVACC[:, j % 2],
                              AF.Silu, bias=m.convb[:, j:j + 1], scale=1.0
                              ).then_inc(m.s_act, 1)
                m.inc(m.s_act, 1)

    def _a2_prep(self, tau):
        m = self
        c0 = tau * TAU
        pre_act = m.val(m.s_act)
        V = m.CONVACC[0:32, 0, 0:1024]
        W = m.CONVACC[0:32, 1, 0:1024]

        @m.block.scalar
        def _(sc):
            sc.wait_ge(m.s_act, pre_act)
            sc.wait_ge(m.s_dve, m.val(m.s_dve))
            sc.drain()
            sc.activation(V, m.dt_park[:, c0:c0 + TAU], AF.Identity,
                          bias=m.dtb[:, 0:1], scale=1.0)
            sc.drain()
            sc.activation(W, V, AF.Abs)
            sc.drain()
            sc.activation(W, W, AF.Exp, scale=-1.0)
            sc.drain()
            sc.activation(W, W, AF.Ln, bias=1.0).then_inc(m.s_act, 1)
            m.inc(m.s_act, 1)
        actv = m.val(m.s_act)

        @m.block.vector
        def _(v):
            v.wait_ge(m.s_act, actv)
            v.tensor_scalar_max(V, V, 0.0)
            v.drain()
            v.tensor_add(m.dt_park[:, c0:c0 + TAU], V, W)
            v.drain()
            v.tensor_scalar_mul(V, m.dt_park[:, c0:c0 + TAU], m.acol[:, 0:1])
            v.drain()
            for kc in range(4):
                i = v.tensor_tensor_scan(
                    m.a_park[:, c0 + kc * CH:c0 + (kc + 1) * CH],
                    m.oscan, V[:, kc * CH:(kc + 1) * CH], 0.0,
                    ALU.mult, ALU.add)
            i.then_inc(m.s_dve, 1)
            m.inc(m.s_dve, 1)

    def pslot16(self, bank, slot, w=128):
        base = (bank % 4) * 1024 + slot * 128
        t = self.PS_A if bank < 4 else self.PS_B
        return t.bitcast(F16)[:, base:base + w]

    def pf32(self, bank, off, w, p=128):
        t = self.PS_A if bank < 4 else self.PS_B
        return t[0:p, (bank % 4) * 512 + off:(bank % 4) * 512 + off + w]

    def _a2_chunk(self, tau, kc):
        m = self
        hof = HALO if tau == 0 else 0
        cc0 = hof + kc * CH
        g0 = tau * TAU + kc * CH
        aw_dve = m.val(m.s_dve)
        pre_act = m.val(m.s_act)

        @m.block.scalar
        def _(sc):
            sc.wait_ge(m.s_dve, aw_dve)
            sc.activation(m.CONVACC[0:32, 1, 0:CH], m.a_park[:, g0:g0 + CH],
                          AF.Exp, bias=m.a_park[:, g0 + CH - 1:g0 + CH],
                          scale=-1.0)
            sc.drain()
            sc.activation(m.er32, m.a_park[:, g0 + CH - 1:g0 + CH], AF.Exp
                          ).then_inc(m.s_act, 1)
            m.inc(m.s_act, 1)
        scd_ready = m.val(m.s_act)

        @m.block.vector
        def _(v):
            v.wait_ge(m.s_act, scd_ready)
            v.tensor_tensor(m.CONVACC[0:32, 1, 0:CH],
                            m.CONVACC[0:32, 1, 0:CH],
                            m.dt_park[:, g0:g0 + CH], ALU.mult).then_inc(m.s_dve, 1)
            m.inc(m.s_dve, 1)
        scdm_ready = m.val(m.s_dve)

        af0 = m.val(m.s_af)
        er0 = m.val(m.s_er)

        @m.block.gpsimd
        def _(g):
            g.wait_ge(m.s_dve, aw_dve)
            g.wait_ge(m.s_ai, m.val(m.s_ai))
            g.dma_start(out=m.aflat_h.rearrange("a (h t) -> (a h) t", h=32),
                        in_=m.a_park[:, g0:g0 + CH]).then_inc(m.s_af, 16)
            m.inc(m.s_af, 16)
            g.wait_ge(m.s_act, scd_ready)
            g.dma_start(out=m.erow_h.rearrange("a h -> h a"), in_=m.er32
                        ).then_inc(m.s_er, 16)
            m.inc(m.s_er, 16)
            g.wait_ge(m.s_er, er0 + 16)
            g.dma_start(out=m.erow[:, 0:32],
                        in_=m.erow_h.partition_broadcast(64)
                        ).then_inc(m.s_er, 16)
            m.inc(m.s_er, 16)
        er_done = m.val(m.s_er)
        af_done = m.val(m.s_af)

        tp0 = m.val(m.s_tp)
        ev0 = m.val(m.s_dve)

        prev_xe = getattr(self, "_a2_prev_xevac", None)
        prev_sse = getattr(self, "_a2_prev_ssevac", None)
        prev_srun = getattr(self, "_a2_prev_srun", None)
        scd_act_mark = m.val(m.s_act)

        @m.block.tensor
        def _(t, prev_xe=prev_xe, prev_sse=prev_sse, prev_srun=prev_srun):
            t.wait_ge(m.s_act, pre_act)
            for grp in range(8):
                bank = 6 + grp % 2
                if grp >= 2:
                    t.wait_ge(m.s_dve, ev0 + (grp - 2) + 1)
                elif prev_xe is not None:
                    t.wait_ge(m.s_dve, prev_xe)
                last = None
                for s8 in range(8):
                    n = grp * 8 + s8
                    jt, ch = n // 32, n % 32
                    pd = self.pslot16(bank, s8)
                    last = t.transpose(
                        pd, m.B1[:, ch, cc0 + jt * 128:cc0 + jt * 128 + 128],
                        m.ident)
                last.then_inc(m.s_tp, 1)
                m.inc(m.s_tp, 1)
            if prev_sse is not None:
                t.wait_ge(m.s_dve, prev_sse)
            if prev_srun is not None:
                t.wait_ge(m.s_dve, prev_srun)
            t.wait_ge(m.s_dve, scdm_ready)
            last = None
            for jt in range(2):
                pd = self.pslot16(5, jt)[:, 0:64]
                last = t.transpose(pd, m.B1[0:64, 32, cc0 + jt * 128:
                                            cc0 + jt * 128 + 128],
                                   m.ident[0:64, 0:64])
            for which in range(3):
                for jt in range(2):
                    srcs = [m.a_park[:, g0 + jt * 128:g0 + jt * 128 + 128],
                            m.dt_park[:, g0 + jt * 128:g0 + jt * 128 + 128],
                            m.CONVACC[0:32, 1, jt * 128:jt * 128 + 128]]
                    pd = self.pf32(5, 192 + (which * 2 + jt) * 32, 32)
                    last = t.transpose(pd, srcs[which],
                                       m.ident32[0:32, 0:32])
            last.then_inc(m.s_tp, 1)
            m.inc(m.s_tp, 1)
            t.matmul(self.pf32(4, 0, 256), m.B1[0:64, 32, cc0:cc0 + 128],
                     m.B1[0:64, 33, cc0:cc0 + 256], start=True, stop=True)
            t.matmul(self.pf32(4, 256, 128),
                     m.B1[0:64, 32, cc0 + 128:cc0 + 256],
                     m.B1[0:64, 33, cc0 + 128:cc0 + 256],
                     start=True, stop=True).then_inc(m.s_tp, 1)
            m.inc(m.s_tp, 1)
        tp_small = tp0 + 9
        tp_sc = tp0 + 10

        @m.block.vector
        def _(v):
            for grp in range(8):
                bank = 6 + grp % 2
                v.wait_ge(m.s_tp, tp0 + grp + 1)
                v.tensor_copy(m.XTM[:, grp * 1024:grp * 1024 + 1024],
                              self.pslot16(bank, 0, 1024)
                              ).then_inc(m.s_dve, 1)
                m.inc(m.s_dve, 1)
        self._a2_prev_xevac = m.val(m.s_dve)

        @m.block.vector
        def _(v):
            v.wait_ge(m.s_tp, tp_small)
            for jt in range(2):
                v.tensor_copy(m.TT32.bitcast(F16)[:, 384 + 64 * jt:
                                                  448 + 64 * jt],
                              self.pslot16(5, jt)[:, 0:64])
            for which in range(3):
                for jt in range(2):
                    k = which * 2 + jt
                    i = v.tensor_copy(m.TT32[:, k * 32:k * 32 + 32],
                                      self.pf32(5, 192 + k * 32, 32))
            i.then_inc(m.s_dve, 1)
            m.inc(m.s_dve, 1)
            v.wait_ge(m.s_tp, tp_sc)
            v.tensor_tensor(m.SSB[:, 0:128], self.pf32(4, 0, 128), m.maskT,
                            ALU.mult)
            v.tensor_copy(m.SSB[:, 128:256], self.pf32(4, 128, 128))
            i = v.tensor_tensor(m.SSB[:, 256:384], self.pf32(4, 256, 128),
                                m.maskT, ALU.mult)
            i.then_inc(m.s_dve, 1)
            m.inc(m.s_dve, 1)
        self._a2_prev_ssevac = m.val(m.s_dve)
        ev_all = m.val(m.s_dve)

        @m.block.vector
        def _(v):
            v.wait_ge(m.s_gp, 1)
            v.tensor_copy(m.prevs, m.s_run).then_inc(m.s_dve, 1)
            m.inc(m.s_dve, 1)
            v.wait_ge(m.s_er, er_done)
            v.drain()
            v.tensor_tensor(
                m.s_run.rearrange("n (h p) -> n h p", h=32),
                m.s_run.rearrange("n (h p) -> n h p", h=32),
                m.erow[:, 0:32].unsqueeze(2).broadcast_to([64, 32, 128]),
                ALU.mult).then_inc(m.s_dve, 1)
            m.inc(m.s_dve, 1)
        prevs_done = m.val(m.s_dve) - 1
        scale_done = m.val(m.s_dve)

        for jt in range(2):
            pe_snap = m.val(m.s_pe)

            @m.block.vector
            def _(v, pe_snap=pe_snap, jt=jt):
                v.wait_ge(m.s_pe, pe_snap)
                v.tensor_tensor(
                    m.SCL[:, 0:4096].rearrange("p (h q) -> p h q", h=32),
                    m.XTM[:, jt * 4096:jt * 4096 + 4096].rearrange(
                        "p (h q) -> p h q", h=32),
                    m.TT32[:, 128 + jt * 32:160 + jt * 32].unsqueeze(2
                        ).broadcast_to([128, 32, 128]),
                    ALU.mult).then_inc(m.s_dve, 1)
                m.inc(m.s_dve, 1)
            xwd_done = m.val(m.s_dve)

            @m.block.tensor
            def _(t, jt=jt, xwd_done=xwd_done):
                t.wait_ge(m.s_dve, xwd_done)
                for p in range(8):
                    if p >= 2:
                        t.wait_ge(m.s_dve, xwd_done + (p - 2) + 1)
                    t.matmul(self.pf32(4 + p % 2, 0, 512, p=64),
                             m.TT32.bitcast(F16)[:, 384 + 64 * jt:448 + 64 * jt],
                             m.SCL[:, 512 * p:512 * p + 512],
                             start=True, stop=True).then_inc(m.s_pe, 1)
                    m.inc(m.s_pe, 1)
            st_pe = m.val(m.s_pe)

            @m.block.vector
            def _(v, jt=jt, st_pe=st_pe):
                for p in range(8):
                    v.wait_ge(m.s_pe, st_pe - 8 + p + 1)
                    if jt == 0 and p < 2:
                        v.wait_ge(m.s_dve, scale_done)
                    v.drain()
                    v.tensor_tensor(m.s_run[:, 512 * p:512 * p + 512],
                                    m.s_run[:, 512 * p:512 * p + 512],
                                    self.pf32(4 + p % 2, 0, 512, p=64),
                                    ALU.add).then_inc(m.s_dve, 1)
                    m.inc(m.s_dve, 1)

        self._a2_prev_srun = m.val(m.s_dve)

        for hg in range(2):
            ai0 = m.val(m.s_ai)

            @m.block.gpsimd
            def _(g, hg=hg, ai0=ai0):
                g.wait_ge(m.s_af, af_done)
                g.wait_ge(m.s_act, m.val(m.s_act))
                g.dma_start(
                    out=m.A_I[:, 0:4096],
                    in_=m.aflat_h[:, hg * 4096:hg * 4096 + 4096]
                    .partition_broadcast(128)).then_inc(m.s_ai, 16)
                m.inc(m.s_ai, 16)
                g.dma_start(
                    out=m.A_I[:, 4096:6144],
                    in_=m.aflat_h.rearrange("a (h t) -> a h t", h=32)
                    [:, hg * 16:hg * 16 + 16, 128:256]
                    .partition_broadcast(128)).then_inc(m.s_ai, 16)
                m.inc(m.s_ai, 16)
            ai_done = m.val(m.s_ai)

            pe_prev = m.val(m.s_pe)

            @m.block.scalar
            def _(sc, hg=hg, ai_done=ai_done, pe_prev=pe_prev):
                sc.wait_ge(m.s_ai, ai_done)
                sc.wait_ge(m.s_pe, pe_prev)
                sc.activation(m.EA16[:, 0:4096], m.A_I[0:64, 0:4096], AF.Exp
                              ).then_inc(m.s_act, 1)
                m.inc(m.s_act, 1)
            ea_done = m.val(m.s_act)

            @m.block.vector
            def _(v, hg=hg, ai_done=ai_done, ea_done=ea_done):
                v.wait_ge(m.s_ai, ai_done)
                v.wait_ge(m.s_act, ea_done)
                v.wait_ge(m.s_dve, ev_all * 0 + ev_all)
                v.tensor_tensor(
                    m.A_I[:, 0:4096].rearrange("p (h q) -> p h q", h=16),
                    m.A_I[:, 0:4096].rearrange("p (h q) -> p h q", h=16),
                    m.TT32[:, hg * 16:hg * 16 + 16].unsqueeze(2
                        ).broadcast_to([128, 16, 256]),
                    ALU.subtract)
                v.drain()
                v.tensor_scalar_min(m.A_I[:, 0:4096], m.A_I[:, 0:4096], 0.0).then_inc(m.s_dve, 1)
                m.inc(m.s_dve, 1)
                v.tensor_tensor(
                    m.A_I[:, 4096:6144].rearrange("p (h q) -> p h q", h=16),
                    m.A_I[:, 4096:6144].rearrange("p (h q) -> p h q", h=16),
                    m.TT32[:, 32 + hg * 16:32 + hg * 16 + 16].unsqueeze(2
                        ).broadcast_to([128, 16, 128]),
                    ALU.subtract)
                v.drain()
                v.tensor_scalar_min(m.A_I[:, 4096:6144], m.A_I[:, 4096:6144],
                                    0.0).then_inc(m.s_dve, 1)
                m.inc(m.s_dve, 1)
            seg_done = m.val(m.s_dve)

            @m.block.scalar
            def _(sc, seg_done=seg_done):
                sc.wait_ge(m.s_dve, seg_done)
                sc.activation(m.W16[:, 0:4096], m.A_I[:, 0:4096], AF.Exp)
                sc.drain()
                sc.activation(m.W16[:, 4096:6144], m.A_I[:, 4096:6144], AF.Exp
                              ).then_inc(m.s_act, 1)
                m.inc(m.s_act, 1)
            wexp_done = m.val(m.s_act)

            @m.block.vector
            def _(v, hg=hg, wexp_done=wexp_done, pe_prev=pe_prev):
                v.wait_ge(m.s_act, wexp_done)
                v.wait_ge(m.s_pe, pe_prev)
                v.tensor_tensor(
                    m.W16[:, 0:4096].rearrange("p (h q) -> p h q", h=16),
                    m.W16[:, 0:4096].rearrange("p (h q) -> p h q", h=16),
                    m.SSB[:, 0:256].unsqueeze(1).broadcast_to([128, 16, 256]),
                    ALU.mult)
                v.drain()
                v.tensor_tensor(
                    m.W16[:, 4096:6144].rearrange("p (h q) -> p h q", h=16),
                    m.W16[:, 4096:6144].rearrange("p (h q) -> p h q", h=16),
                    m.SSB[:, 256:384].unsqueeze(1).broadcast_to([128, 16, 128]),
                    ALU.mult)
                v.drain()
                v.tensor_tensor(
                    m.EA16[:, 0:4096].rearrange("p (h q) -> p h q", h=16),
                    m.EA16[:, 0:4096].rearrange("p (h q) -> p h q", h=16),
                    m.B1[0:64, 33, cc0:cc0 + 256].unsqueeze(1
                        ).broadcast_to([64, 16, 256]),
                    ALU.mult)
                v.drain()
                v.tensor_tensor(
                    m.SCL[:, 0:2048].rearrange("p (h q) -> p h q", h=16),
                    m.XTM[:, 0 * 4096 + hg * 2048:0 * 4096 + hg * 2048 + 2048
                          ].rearrange("p (h q) -> p h q", h=16),
                    m.TT32[:, 64 + hg * 16:64 + hg * 16 + 16].unsqueeze(2
                        ).broadcast_to([128, 16, 128]),
                    ALU.mult)
                v.drain()
                v.tensor_tensor(
                    m.SCL[:, 2048:4096].rearrange("p (h q) -> p h q", h=16),
                    m.XTM[:, 1 * 4096 + hg * 2048:1 * 4096 + hg * 2048 + 2048
                          ].rearrange("p (h q) -> p h q", h=16),
                    m.TT32[:, 96 + hg * 16:96 + hg * 16 + 16].unsqueeze(2
                        ).broadcast_to([128, 16, 128]),
                    ALU.mult).then_inc(m.s_dve, 1)
                m.inc(m.s_dve, 1)
            wready = m.val(m.s_dve)

            pe_y0 = m.val(m.s_pe)

            dve_y0 = m.val(m.s_dve)

            @m.block.tensor
            def _(t, hg=hg, wready=wready, pe_y0=pe_y0, dve_y0=dve_y0):
                t.wait_ge(m.s_dve, wready)
                for l in range(16):
                    h = hg * 16 + l
                    if l >= 4:
                        t.wait_ge(m.s_dve, dve_y0 + (l - 4) + 1)
                    yap = self.pf32(l % 4, 0, 256)
                    t.matmul(yap, m.SCL[:, l * 128:l * 128 + 128],
                             m.W16[:, l * 256:l * 256 + 256],
                             start=True, stop=False)
                    t.matmul(yap[:, 128:256],
                             m.SCL[:, 2048 + l * 128:2048 + l * 128 + 128],
                             m.W16[:, 4096 + l * 128:4096 + l * 128 + 128],
                             start=False, stop=False)
                    t.matmul(yap, m.prevs[:, h * 128:h * 128 + 128],
                             m.EA16[:, l * 256:l * 256 + 256],
                             start=False, stop=True).then_inc(m.s_pe, 1)
                    m.inc(m.s_pe, 1)

            @m.block.vector
            def _(v, hg=hg, pe_y0=pe_y0):
                for l in range(16):
                    h = hg * 16 + l
                    v.wait_ge(m.s_pe, pe_y0 + l + 1)
                    yap = self.pf32(l % 4, 0, 256)
                    v.scalar_tensor_tensor(
                        m.B1[:, h, cc0:cc0 + 256], m.B1[:, h, cc0:cc0 + 256],
                        m.dbc[:, h:h + 1], yap, ALU.mult, ALU.add
                        ).then_inc(m.s_dve, 1)
                    m.inc(m.s_dve, 1)

    def _a2_tau_end(self, tau):
        m = self
        hof = HALO if tau == 0 else 0
        dve_all = m.val(m.s_dve)
        act_all = m.val(m.s_act)

        @m.block.vector
        def _(v):
            v.wait_ge(m.s_dve, dve_all)
            v.tensor_copy(m.c_park[:, tau * TAU:tau * TAU + TAU],
                          m.B1[0:64, 33, hof:hof + TAU]).then_inc(m.s_dve, 1)
            m.inc(m.s_dve, 1)

        @m.block.sync
        def _(s):
            s.wait_ge(m.s_dve, m.val(m.s_dve))
            s.dma_start(out=m.ypark_h[:, :, tau * TAU:tau * TAU + TAU],
                        in_=m.B1[:, 0:32, hof:hof + TAU]).then_inc(m.s_yd, 16)
            m.inc(m.s_yd, 16)

    def _gather(self):
        m = self
        dve_all = m.val(m.s_dve)
        act_all = m.val(m.s_act)
        self._a2_end_dve = dve_all
        self._a2_end_act = act_all

        @m.block.vector
        def _(v):
            v.wait_ge(m.s_dve, dve_all)
            v.wait_ge(m.s_act, act_all)
            v.reduce_sum(m.atot, m.a_park[:, CH - 1:T:CH], mybir.AxisListType.X)
            v.drain()
            v.tensor_tensor_scan(m.acum[:, 1:8], m.oscan[:, 0:7],
                                 m.a_park[:, CH - 1:T - CH:CH], 0.0,
                                 ALU.mult, ALU.add).then_inc(m.s_dve, 1)
            m.inc(m.s_dve, 1)
        red_done = m.val(m.s_dve)

        @m.block.gpsimd
        def _(g):
            g.memset(m.acum[:, 0:1], 0.0)
            g.wait_ge(m.s_dve, red_done)
            g.memset(m.s_in, 0.0).then_inc(m.s_gp, 1)
            m.inc(m.s_gp, 1)

        @m.block.sync
        def _(s):
            s.wait_ge(m.s_dve, red_done)
            s.dma_start(out=m.agin_h[0:64 * DIN].rearrange("(n q) -> n q", n=64),
                        in_=m.s_run).then_inc(m.s_cc, 16)
            m.inc(m.s_cc, 16)
            s.dma_start(out=m.agin_h[64 * DIN:64 * DIN + 32
                                     ].rearrange("(h a) -> h a", a=1),
                        in_=m.atot).then_inc(m.s_cc, 16)
            m.inc(m.s_cc, 16)
        ag_in_done = m.val(m.s_cc)

        @m.block.gpsimd
        def _(g):
            g.wait_ge(m.s_cc, ag_in_done)
            g.collective_compute(
                "AllGather", ALU.bypass,
                replica_groups=[list(range(NCORES))],
                ins=[m.agin_h],
                outs=[m.agout_h],
            ).then_inc(m.s_gp, 1)
            m.inc(m.s_gp, 1)
        cc_done = m.val(m.s_gp)

        @m.block.sync
        def _(s):
            s.wait_ge(m.s_gp, cc_done)
            s.dma_start(out=m.atall,
                        in_=m.agout_h[:, 64 * DIN:64 * DIN + 32]
                        .rearrange("j h -> h j"),
                        allow_slow_non_contiguous=True).then_inc(m.s_cc, 16)
            m.inc(m.s_cc, 16)
        at_done = m.val(m.s_cc)

        for j in range(NCORES):
            er0 = m.val(m.s_er)

            @m.block.scalar
            def _(sc, j=j, at_done=at_done):
                if j == 0:
                    sc.wait_ge(m.s_cc, at_done)
                sc.wait_ge(m.s_er, m.val(m.s_er))
                sc.activation(m.er32, m.atall[:, j:j + 1], AF.Exp,
                              scale=m.gbc[0:32, 8 + j:9 + j]
                              ).then_inc(m.s_act, 1)
                m.inc(m.s_act, 1)
            e_done = m.val(m.s_act)

            @m.block.gpsimd
            def _(g, j=j, e_done=e_done, er0=er0, dve_prev=m.val(m.s_dve)):
                g.wait_ge(m.s_act, e_done)
                g.wait_ge(m.s_dve, dve_prev)
                g.dma_start(out=m.erow_h.rearrange("a h -> h a"), in_=m.er32
                            ).then_inc(m.s_er, 16)
                m.inc(m.s_er, 16)
                g.wait_ge(m.s_er, er0 + 16)
                g.dma_start(out=m.erow[:, 0:32],
                            in_=m.erow_h.partition_broadcast(64)
                            ).then_inc(m.s_er, 16)
                m.inc(m.s_er, 16)
                g.wait_ge(m.s_dve, m.val(m.s_dve))
                g.dma_start(out=m.AB64,
                            in_=m.agout_h[j, 0:64 * DIN]
                            .rearrange("(n q) -> n q", n=64)
                            ).then_inc(m.s_ai, 16)
                m.inc(m.s_ai, 16)
            er_done = m.val(m.s_er)
            sag_done = m.val(m.s_ai)

            @m.block.vector
            def _(v, j=j, er_done=er_done, sag_done=sag_done):
                if j == 0:
                    v.wait_ge(m.s_gp, m.val(m.s_gp))
                v.wait_ge(m.s_er, er_done)
                v.wait_ge(m.s_ai, sag_done)
                v.drain()
                v.tensor_tensor(
                    m.s_in.rearrange("n (h p) -> n h p", h=32),
                    m.s_in.rearrange("n (h p) -> n h p", h=32),
                    m.erow[:, 0:32].unsqueeze(2).broadcast_to([64, 32, 128]),
                    ALU.mult)
                v.drain()
                v.scalar_tensor_tensor(
                    m.s_in, m.AB64, m.gbc[:, j:j + 1], m.s_in,
                    ALU.mult, ALU.add).then_inc(m.s_dve, 1)
                m.inc(m.s_dve, 1)

    def _gather_corrprep(self):
        m = self
        red = m.val(m.s_dve)
        gp0 = m.val(m.s_gp)

        @m.block.sync
        def _(s):
            s.wait_ge(m.s_dve, red)
            s.wait_ge(m.s_gp, gp0)
            s.dma_start(out=m.aflat8_h.rearrange("g (h t) -> h g t", h=32),
                        in_=m.a_park.rearrange("h (g t) -> h g t", g=8)
                        ).then_inc(m.s_yd, 16)
            m.inc(m.s_yd, 16)
        e8_dve = m.val(m.s_dve)

        @m.block.scalar
        def _(sc):
            sc.wait_ge(m.s_dve, e8_dve)
            sc.wait_ge(m.s_gp, gp0)
            sc.drain()
            sc.activation(m.erow8sb, m.acum, AF.Exp).then_inc(m.s_act, 1)
            m.inc(m.s_act, 1)
        e8_done = m.val(m.s_act)
        er0 = m.val(m.s_er)

        @m.block.gpsimd
        def _(g, e8_done=e8_done, er0=er0):
            g.wait_ge(m.s_act, e8_done)
            g.dma_start(out=m.erow8_h.rearrange("g h -> h g"), in_=m.erow8sb,
                        allow_slow_non_contiguous=True).then_inc(m.s_er, 16)
            m.inc(m.s_er, 16)
            g.wait_ge(m.s_er, er0 + 16)
            g.dma_start(out=m.erow8,
                        in_=m.erow8_h.rearrange("g h -> (g h)").unsqueeze(0)
                        .partition_broadcast(64)).then_inc(m.s_er, 16)
            m.inc(m.s_er, 16)
        self.erow8_done = m.val(m.s_er)

    def _a3(self, tau):
        m = self
        pre_dve = m.val(m.s_dve)
        pre_yd = m.val(m.s_yd)

        @m.block.sync
        def _(s, tau=tau):
            if tau == 0:
                s.wait_ge(m.s_dve, self._a2_end_dve)
                s.wait_ge(m.s_act, self._a2_end_act)
            else:
                s.wait_ge(m.s_dve, pre_dve)
                s.wait_ge(m.s_act, m.val(m.s_act))
            s.wait_ge(m.s_pe, m.val(m.s_pe))
            s.wait_ge(m.s_yd, pre_yd)
            s.dma_start(out=m.B1[:, 0:32, 0:TAU],
                        in_=m.ypark_h[:, :, tau * TAU:tau * TAU + TAU]
                        ).then_inc(m.s_yd, 16)
            m.inc(m.s_yd, 16)
        yl_done = m.val(m.s_yd)

        corr_pe = {}
        ea_act = {}
        sink_dve = {}
        for kc in range(4):
            g = tau * 4 + kc
            pe_prev_kc = m.val(m.s_pe)

            @m.block.vector
            def _(v, kc=kc, g=g, pe_prev_kc=pe_prev_kc):
                if tau == 0 and kc == 0:
                    v.wait_ge(m.s_er, self.erow8_done)
                v.wait_ge(m.s_pe, pe_prev_kc)
                v.drain()
                v.tensor_tensor(
                    m.SINK16.rearrange("n (h p) -> n h p", h=32),
                    m.s_in.rearrange("n (h p) -> n h p", h=32),
                    m.erow8[:, g * 32:g * 32 + 32].unsqueeze(2
                        ).broadcast_to([64, 32, 128]),
                    ALU.mult).then_inc(m.s_dve, 1)
                m.inc(m.s_dve, 1)
            sink_dve[kc] = m.val(m.s_dve)

            for hg in range(2):
                slot = (kc * 2 + hg) % 2
                use = kc * 2 + hg

                @m.block.gpsimd
                def _(g_, g=g, hg=hg, slot=slot, use=use,
                      st_snap=m.val(m.s_st), yd_snap=m.val(m.s_yd)):
                    if tau == 1 and use == 0:
                        g_.wait_ge(m.s_st, st_snap)
                    if use >= 2:
                        g_.wait_ge(m.s_act, ea_act[use - 2])
                    elif tau == 0 and use < 2:
                        g_.wait_ge(m.s_yd, yd_snap)
                    ais = m.s_ai if slot == 0 else m.s_ai2
                    g_.dma_start(
                        out=m.AB64s[slot],
                        in_=m.aflat8_h[g:g + 1, hg * 4096:hg * 4096 + 4096]
                        .partition_broadcast(64)).then_inc(ais, 16)
                    m.inc(ais, 16)
                ai_done = m.val(m.s_ai if slot == 0 else m.s_ai2)
                pe_prev = corr_pe.get(use - 2)

                @m.block.scalar
                def _(sc, slot=slot, ai_done=ai_done, pe_prev=pe_prev):
                    sc.wait_ge(m.s_ai if slot == 0 else m.s_ai2, ai_done)
                    if pe_prev is not None:
                        sc.wait_ge(m.s_pe, pe_prev)
                    sc.drain()
                    sc.activation(m.EA16s[slot], m.AB64s[slot], AF.Exp
                                  ).then_inc(m.s_act, 1)
                    m.inc(m.s_act, 1)
                ea_act[use] = m.val(m.s_act)

                @m.block.vector
                def _(v, g=g, slot=slot, ea_done=m.val(m.s_act)):
                    v.wait_ge(m.s_act, ea_done)
                    v.drain()
                    v.tensor_tensor(
                        m.EA16s[slot].rearrange("n (h p) -> n h p", h=16),
                        m.EA16s[slot].rearrange("n (h p) -> n h p", h=16),
                        m.c_park[:, g * CH:g * CH + CH].unsqueeze(1
                            ).broadcast_to([64, 16, 256]),
                        ALU.mult).then_inc(m.s_dve, 1)
                    m.inc(m.s_dve, 1)
                cp_done = m.val(m.s_dve)
                pe_c0 = m.val(m.s_pe)
                dve_c0 = m.val(m.s_dve)

                @m.block.tensor
                def _(t, hg=hg, slot=slot, kc=kc, cp_done=cp_done,
                      pe_c0=pe_c0, dve_c0=dve_c0):
                    t.wait_ge(m.s_dve, cp_done)
                    t.wait_ge(m.s_dve, sink_dve[kc])
                    t.wait_ge(m.s_yd, yl_done)
                    for l in range(16):
                        h = hg * 16 + l
                        if l >= 8:
                            t.wait_ge(m.s_dve, dve_c0 + (l - 8) + 1)
                        yap = self.pf32(l % 8, 0, 256)
                        t.matmul(yap, m.SINK16[:, h * 128:h * 128 + 128],
                                 m.EA16s[slot][:, l * 256:l * 256 + 256],
                                 start=True, stop=True).then_inc(m.s_pe, 1)
                        m.inc(m.s_pe, 1)
                corr_pe[use] = m.val(m.s_pe)

                @m.block.vector
                def _(v, hg=hg, pe_c0=pe_c0, kc=kc):
                    for l in range(16):
                        h = hg * 16 + l
                        v.wait_ge(m.s_pe, pe_c0 + l + 1)
                        yap = self.pf32(l % 8, 0, 256)
                        cc = kc * CH
                        v.tensor_tensor(m.B1[:, h, cc:cc + 256],
                                        m.B1[:, h, cc:cc + 256], yap,
                                        ALU.add).then_inc(m.s_dve, 1)
                        m.inc(m.s_dve, 1)

        TW = TAU
        u0 = (0 if tau == 0 else TAU + HALO) + (HALO if tau == 0 else 0)
        corr_dve = m.val(m.s_dve)

        @m.block.gpsimd
        def _(g):
            g.wait_ge(m.s_dve, corr_dve)
            g.wait_ge(m.s_act, m.val(m.s_act))
            g.dma_start(
                out=m.AR_U[:, 0:16 * TW].rearrange("p (k t) -> p k t", t=TW),
                in_=m.u_h[:, u0:u0 + TW].rearrange("(k p) t -> p k t", p=128),
            ).then_inc(m.s_wu, 16)
            m.inc(m.s_wu, 16)
        uz_done = m.val(m.s_wu)

        if tau == 0:
            @m.block.gpsimd
            def _(g):
                g.wait_ge(m.s_dve, corr_dve)
                g.dma_start(out=m.ocols, in_=m.ocols_h.rearrange(
                    "p (h j) -> p h j", j=32)).then_inc(m.s_wdma, 16)
                m.inc(m.s_wdma, 16)
        ocols_done = m.val(m.s_wdma)

        sl = [m.s_sl0, m.s_sl1]
        slbase = [m.val(m.s_sl0), m.val(m.s_sl1)]
        zmm_inc = {}
        sig_inc = {}
        sq_inc = {}
        sqm_inc = {}
        yg_inc = {}

        for h in range(32):
            @m.block.gpsimd
            def _(g, h=h):
                if h >= 2:
                    g.wait_ge(m.s_pe, zmm_inc[h - 2])
                g.dma_start(out=m.WBUF[:, h % 2], in_=m.wZ_h[h]
                            ).then_inc(sl[h % 2], 16)
                m.inc(sl[h % 2], 16)

            @m.block.tensor
            def _(t, h=h):
                if h == 0:
                    t.wait_ge(m.s_wu, uz_done)
                t.wait_ge(sl[h % 2], slbase[h % 2] + 16 * (h // 2 + 1))
                if h >= 2:
                    t.wait_ge(m.s_act, sig_inc[h - 2])
                last = None
                for tt in range(2):
                    for k in range(16):
                        last = t.matmul(
                            self.pf32(2 * (h % 2) + tt, 0, 512),
                            m.WBUF[:, h % 2, k],
                            m.AR_U[:, 0:16 * TW].rearrange(
                                "p (k t) -> p k t", t=TW)[:, k,
                                tt * 512:tt * 512 + 512],
                            start=(k == 0), stop=(k == 15))
                last.then_inc(m.s_pe, 1)
                m.inc(m.s_pe, 1)
                zmm_inc[h] = m.val(m.s_pe)
                if h >= 1:
                    if h == 1:
                        t.wait_ge(m.s_wdma, ocols_done)
                    t.wait_ge(m.s_dve, sq_inc[h - 1])
                    last = None
                    for tt in range(2):
                        last = t.matmul(
                            self.pf32(6 + tt, 0, 512, p=32),
                            m.ocols[:, h - 1],
                            m.SQ16[:, ((h - 1) % 2) * 1024 + tt * 512:
                                   ((h - 1) % 2) * 1024 + tt * 512 + 512],
                            start=(h - 1 == 0), stop=(h - 1 == 31))
                    last.then_inc(m.s_pe, 1)
                    m.inc(m.s_pe, 1)
                    sqm_inc[h - 1] = m.val(m.s_pe)

            zap = (self.PS_A[:, 0:1024] if h % 2 == 0
                   else self.PS_A[:, 1024:2048])
            dve_snap = m.val(m.s_dve)

            @m.block.scalar
            def _(sc, h=h, zap=zap, dve_snap=dve_snap):
                sc.wait_ge(m.s_pe, zmm_inc[h])
                if h >= 2:
                    sc.wait_ge(m.s_dve, yg_inc[h - 2])
                sc.drain()
                sc.activation(m.CONVACC[:, h % 2], zap, AF.Sigmoid
                              ).then_inc(m.s_act, 1)
                m.inc(m.s_act, 1)
            sig_inc[h] = m.val(m.s_act)

            @m.block.vector
            def _(v, h=h):
                v.wait_ge(m.s_act, sig_inc[h])
                v.drain()
                v.tensor_tensor(m.B1[:, h, 0:TAU], m.B1[:, h, 0:TAU],
                                m.CONVACC[:, h % 2], ALU.mult).then_inc(m.s_dve, 1)
                m.inc(m.s_dve, 1)
            yg_inc[h] = m.val(m.s_dve)

            @m.block.vector
            def _(v, h=h):
                if h >= 2:
                    v.wait_ge(m.s_pe, sqm_inc[h - 2])
                v.drain()
                v.tensor_tensor(m.SQ16[:, (h % 2) * 1024:(h % 2) * 1024 + 1024],
                                m.B1[:, h, 0:TAU], m.B1[:, h, 0:TAU],
                                ALU.mult).then_inc(m.s_dve, 1)
                m.inc(m.s_dve, 1)
            sq_inc[h] = m.val(m.s_dve)

        @m.block.tensor
        def _(t):
            t.wait_ge(m.s_dve, sq_inc[31])
            last = None
            for tt in range(2):
                last = t.matmul(
                    self.pf32(6 + tt, 0, 512, p=32),
                    m.ocols[:, 31],
                    m.SQ16[:, (31 % 2) * 1024 + tt * 512:
                           (31 % 2) * 1024 + tt * 512 + 512],
                    start=False, stop=True)
            last.then_inc(m.s_pe, 1)
            m.inc(m.s_pe, 1)
            sqm_inc[31] = m.val(m.s_pe)

        @m.block.vector
        def _(v, yg31=m.val(m.s_dve)):
            v.wait_ge(m.s_pe, sqm_inc[31])
            v.wait_ge(m.s_dve, yg31)
            v.drain()
            v.tensor_scalar(m.RST32[:, 0:512], self.pf32(6, 0, 512, p=32),
                            1.0 / 128, EPS_RMS, ALU.mult, ALU.add)
            v.drain()
            v.tensor_scalar(m.RST32[:, 512:1024], self.pf32(7, 0, 512, p=32),
                            1.0 / 128, EPS_RMS, ALU.mult, ALU.add).then_inc(m.s_dve, 1)
            m.inc(m.s_dve, 1)
        ms_done = m.val(m.s_dve)

        @m.block.scalar
        def _(sc, ms_done=ms_done):
            sc.wait_ge(m.s_dve, ms_done)
            sc.drain()
            sc.activation(m.RST32, m.RST32, AF.Sqrt).then_inc(m.s_act, 1)
            m.inc(m.s_act, 1)
        ars_done = m.val(m.s_act)

        @m.block.vector
        def _(v, ars_done=ars_done):
            v.wait_ge(m.s_act, ars_done)
            v.drain()
            v.reciprocal(m.RST32, m.RST32).then_inc(m.s_dve, 1)
            m.inc(m.s_dve, 1)
        rec_done = m.val(m.s_dve)

        @m.block.sync
        def _(s, rec_done=rec_done):
            s.wait_ge(m.s_dve, rec_done)
            s.dma_start(out=m.rstdh_h, in_=m.RST32).then_inc(m.s_yd, 16)
            m.inc(m.s_yd, 16)
        rsth_done = m.val(m.s_yd)

        yn_inc = {}
        ers = [m.s_er, m.s_er2]
        for h in range(32):
            @m.block.gpsimd
            def _(g, h=h):
                if h == 0:
                    g.wait_ge(m.s_yd, rsth_done)
                if h >= 2:
                    g.wait_ge(m.s_dve, yn_inc[h - 2])
                g.dma_start(out=m.MRB2[:, h % 2],
                            in_=m.rstdh_h[h:h + 1, :].partition_broadcast(128)
                            ).then_inc(ers[h % 2], 16)
                m.inc(ers[h % 2], 16)
            bc_done = m.val(ers[h % 2])

            @m.block.vector
            def _(v, h=h, bc_done=bc_done):
                v.wait_ge(ers[h % 2], bc_done)
                v.drain()
                v.scalar_tensor_tensor(
                    m.B1[:, h, 0:TAU], m.B1[:, h, 0:TAU], m.nw[:, h:h + 1],
                    m.MRB2[:, h % 2], ALU.mult, ALU.mult
                    ).then_inc(m.s_dve, 1)
                m.inc(m.s_dve, 1)
            yn_inc[h] = m.val(m.s_dve)

    def _a4(self, tau):
        m = self
        pre_dve = m.val(m.s_dve)
        sl = [m.s_sl0, m.s_sl1]
        slbase = [m.val(m.s_sl0), m.val(m.s_sl1)]
        pe0 = m.val(m.s_pe)
        dve0 = m.val(m.s_dve)
        u0 = (0 if tau == 0 else TAU + HALO) + (HALO if tau == 0 else 0)

        @m.block.gpsimd
        def _(g):
            g.wait_ge(m.s_dve, pre_dve)
            for e in range(16):
                if e >= 2:
                    g.wait_ge(m.s_pe, pe0 + 2 * (e - 1))
                g.dma_start(out=m.WO_BUF[:, e % 2], in_=m.wO_h[e]
                            ).then_inc(sl[e % 2], 16)
                m.inc(sl[e % 2], 16)

        for e in range(16):
            for tt in range(2):
                gidx = e * 2 + tt

                @m.block.sync
                def _(s, e=e, tt=tt, gidx=gidx):
                    if gidx >= 2:
                        s.wait_ge(m.s_dve, dve0 + (gidx - 2) + 1)
                    else:
                        s.wait_ge(m.s_dve, pre_dve)
                    us = m.s_ur if gidx % 2 == 0 else m.s_ur1
                    s.dma_start(
                        out=m.URES[:, gidx % 2],
                        in_=m.u_h[e * 128:e * 128 + 128,
                                  u0 + tt * 512:u0 + tt * 512 + 512]
                        ).then_inc(us, 16)
                    m.inc(us, 16)
                ures_done = m.val(m.s_ur if gidx % 2 == 0 else m.s_ur1)

                @m.block.tensor
                def _(t, e=e, tt=tt, gidx=gidx, ures_done=ures_done):
                    t.wait_ge(sl[e % 2], slbase[e % 2] + 16 * (e // 2 + 1))
                    if gidx >= 4:
                        t.wait_ge(m.s_dve, dve0 + (gidx - 4) + 1)
                    last = None
                    for k in range(32):
                        last = t.matmul(self.pf32(gidx % 4, 0, 512),
                                        m.WO_BUF[:, e % 2, k],
                                        m.B1[:, k, tt * 512:tt * 512 + 512],
                                        start=(k == 0), stop=(k == 31))
                    last.then_inc(m.s_pe, 1)
                    m.inc(m.s_pe, 1)
                mm_done = m.val(m.s_pe)

                @m.block.vector
                def _(v, e=e, tt=tt, gidx=gidx, mm_done=mm_done,
                      ures_done=ures_done):
                    v.wait_ge(m.s_pe, mm_done)
                    v.wait_ge(m.s_ur if gidx % 2 == 0 else m.s_ur1, ures_done)
                    v.drain()
                    v.tensor_tensor(
                        m.X1PB[:, e, tt * 512:tt * 512 + 512],
                        self.pf32(gidx % 4, 0, 512),
                        m.URES[:, gidx % 2], ALU.add).then_inc(m.s_dve, 1)
                    m.inc(m.s_dve, 1)

        evac_all = m.val(m.s_dve)

        sq0 = m.val(m.s_dve)
        pe_before = m.val(m.s_pe)

        @m.block.vector
        def _(v, evac_all=evac_all, pe_before=pe_before):
            v.wait_ge(m.s_dve, evac_all)
            v.wait_ge(m.s_pe, pe_before)
            n = 0
            for tt in range(2):
                for e in range(16):
                    if n > 0:
                        v.wait_ge(m.s_pe, pe_before + n)
                    v.drain()
                    v.tensor_tensor(m.SQF2[:, n % 2],
                                    m.X1PB[:, e, tt * 512:tt * 512 + 512],
                                    m.X1PB[:, e, tt * 512:tt * 512 + 512],
                                    ALU.mult).then_inc(m.s_dve, 1)
                    m.inc(m.s_dve, 1)
                    n += 1

        pe_ln0 = m.val(m.s_pe)

        @m.block.tensor
        def _(t, evac_all=evac_all, sq0=sq0):
            t.wait_ge(m.s_dve, evac_all)
            n = 0
            for tt in range(2):
                for e in range(16):
                    t.matmul(self.pf32(4 + tt, 0, 512)[0:1, :], m.o128x1,
                             m.X1PB[:, e, tt * 512:tt * 512 + 512],
                             start=(e == 0), stop=(e == 15))
                    t.wait_ge(m.s_dve, sq0 + n + 1)
                    t.matmul(self.pf32(6 + tt, 0, 512)[0:1, :], m.o128x1,
                             m.SQF2[:, n % 2],
                             start=(e == 0), stop=(e == 15)
                             ).then_inc(m.s_pe, 1)
                    m.inc(m.s_pe, 1)
                    n += 1

        ln_pe = m.val(m.s_pe)
        for tt in range(2):
            @m.block.vector
            def _(v, tt=tt):
                v.wait_ge(m.s_pe, ln_pe)
                v.wait_ge(m.s_er, m.val(m.s_er))
                v.drain()
                v.tensor_scalar_mul(m.MROW[:, 0:512],
                                    self.pf32(4 + tt, 0, 512, p=1), 1.0 / D)
                v.drain()
                v.tensor_scalar_mul(m.RROW[:, 0:512],
                                    self.pf32(6 + tt, 0, 512, p=1), 1.0 / D)
                v.drain()
                v.tensor_tensor(m.SGM[0:1, 0:512], m.MROW[:, 0:512],
                                m.MROW[:, 0:512], ALU.mult)
                v.drain()
                v.tensor_tensor(m.RROW[:, 0:512], m.RROW[:, 0:512],
                                m.SGM[0:1, 0:512], ALU.subtract)
                v.drain()
                v.tensor_scalar_add(m.RROW[:, 0:512], m.RROW[:, 0:512],
                                    EPS_LN).then_inc(m.s_dve, 1)
                m.inc(m.s_dve, 1)
            var_done = m.val(m.s_dve)

            @m.block.scalar
            def _(sc, var_done=var_done):
                sc.wait_ge(m.s_dve, var_done)
                sc.drain()
                sc.activation(m.RROW[:, 0:512], m.RROW[:, 0:512], AF.Sqrt).then_inc(m.s_act, 1)
                m.inc(m.s_act, 1)
            st_done = m.val(m.s_act)

            @m.block.vector
            def _(v, st_done=st_done):
                v.wait_ge(m.s_act, st_done)
                v.drain()
                v.reciprocal(m.RROW[:, 0:512], m.RROW[:, 0:512]).then_inc(m.s_dve, 1)
                m.inc(m.s_dve, 1)
            rr_done = m.val(m.s_dve)
            er0 = m.val(m.s_er)

            @m.block.gpsimd
            def _(g, rr_done=rr_done, er0=er0):
                g.wait_ge(m.s_dve, rr_done)
                g.dma_start(out=m.rst_h[:, 0:512], in_=m.RROW[:, 0:512]
                            ).then_inc(m.s_er, 16)
                m.inc(m.s_er, 16)
                g.dma_start(out=m.rst_h[:, 512:1024], in_=m.MROW[:, 0:512]
                            ).then_inc(m.s_er, 16)
                m.inc(m.s_er, 16)
                g.wait_ge(m.s_er, er0 + 32)
                g.dma_start(out=m.MRB[:, 0], in_=m.rst_h[:, 0:512]
                            .partition_broadcast(128)).then_inc(m.s_er, 16)
                m.inc(m.s_er, 16)
                g.dma_start(out=m.MRB[:, 1], in_=m.rst_h[:, 512:1024]
                            .partition_broadcast(128)).then_inc(m.s_er, 16)
                m.inc(m.s_er, 16)
            mrb_done = m.val(m.s_er)

            @m.block.vector
            def _(v, tt=tt, mrb_done=mrb_done):
                v.wait_ge(m.s_er, mrb_done)
                for e in range(16):
                    v.drain()
                    v.tensor_tensor(m.X1PB[:, e, tt * 512:tt * 512 + 512],
                                    m.X1PB[:, e, tt * 512:tt * 512 + 512],
                                    m.MRB[:, 1], ALU.subtract)
                    v.drain()
                    i = v.tensor_tensor(
                        m.X1PB[:, e, tt * 512:tt * 512 + 512],
                        m.X1PB[:, e, tt * 512:tt * 512 + 512],
                        m.MRB[:, 0], ALU.mult)
                i.then_inc(m.s_dve, 1)
                m.inc(m.s_dve, 1)
            norm_done = m.val(m.s_dve)

        fin = m.val(m.s_dve)

        @m.block.sync
        def _(s, tau=tau, fin=fin):
            s.wait_ge(m.s_dve, fin)
            s.dma_start(out=m.x1park_h[:, :, tau * TAU:tau * TAU + TAU],
                        in_=m.X1PB).then_inc(m.s_st, 16)
            m.inc(m.s_st, 16)

    def _f1(self):
        m = self
        X1 = self.sb_at("X1F", [128, 16, T], F16, self.sballoc["B1"][0])
        m.X1V = X1
        WB4 = self.sb_at("WB4", [128, 4, 16, 128], F16,
                         self.sballoc["XTM"][0])
        GB = self.sb_at("GB2", [128, 2, 2048], F16, self.sballoc["CONVACC"][0])
        UB = self.sb_at("UB2", [128, 2, 2048], F16, self.sballoc["SCL"][0])
        HB8 = self.sb_at("HB8", [128, 8, 2048], F16, self.sballoc["dt_park"][0])
        AB4 = self.sb_at("AB4", [128, 4, 2048], F16, self.sballoc["W16"][0])
        F1S = self.sb_at("F1Sv", [128, 4, 2048], F16, self.sballoc["AR_U"][0])
        pre_pe = m.val(m.s_pe)
        pre_dve = m.val(m.s_dve)
        pre_act = m.val(m.s_act)
        pre_st = m.val(m.s_st)

        @m.block.sync
        def _(s):
            s.wait_ge(m.s_pe, pre_pe)
            s.wait_ge(m.s_dve, pre_dve)
            s.wait_ge(m.s_act, pre_act)
            s.wait_ge(m.s_st, pre_st)
            s.dma_start(out=X1, in_=m.x1park_h).then_inc(m.s_ur, 16)
            m.inc(m.s_ur, 16)
        x1_done = m.val(m.s_ur)

        sl = [m.s_sl0, m.s_sl1]
        silu_inc = {}
        ucp_inc = {}
        hb_inc = {}
        hb_inc_d = {}
        sp_inc = {}
        st_inc = {}

        def sparsify(g):
            s0 = (4 * g) % 8

            @m.block.vector
            def _(v, g=g, s0=s0):
                A0, A1_, A2_, A3_ = (AB4[:, i] for i in range(4))
                S1, T1, S2, T2 = (F1S[:, i] for i in range(4))
                for i in range(4):
                    v.drain()
                    v.tensor_scalar(AB4[:, i], HB8[:, s0 + i], -1.0, None,
                                    ALU.mult)
                    v.drain()
                    v.tensor_tensor(AB4[:, i], AB4[:, i], HB8[:, s0 + i],
                                    ALU.max)
                v.drain()
                v.tensor_tensor(S1, A0, A1_, ALU.max)
                v.tensor_tensor(T1, A0, A1_, ALU.min)
                v.tensor_tensor(S2, A2_, A3_, ALU.max)
                v.tensor_tensor(T2, A2_, A3_, ALU.min)
                v.drain()
                v.tensor_tensor(S1, S1, S2, ALU.min)
                v.tensor_tensor(T1, T1, T2, ALU.max)
                v.drain()
                v.tensor_tensor(S1, S1, T1, ALU.max)
                v.drain()
                for i in range(4):
                    v.tensor_tensor(AB4[:, i], AB4[:, i], S1, ALU.is_ge)
                v.drain()
                for i in range(4):
                    ii = v.tensor_tensor(HB8[:, s0 + i], HB8[:, s0 + i],
                                         AB4[:, i], ALU.mult)
                ii.then_inc(m.s_dve, 1)
                m.inc(m.s_dve, 1)
            sp_inc[g] = m.val(m.s_dve)

            @m.block.sync
            def _(s, g=g, s0=s0):
                s.wait_ge(m.s_dve, sp_inc[g])
                for i in range(4):
                    s.dma_start(out=m.hscr_h[4 * g + i], in_=HB8[:, s0 + i]
                                ).then_inc(m.s_st, 16)
                    m.inc(m.s_st, 16)
            st_inc[g] = m.val(m.s_st)

        mmv = {}
        sl4 = [m.s_sl0, m.s_sl1, m.s_sl2, m.s_sl3]
        for f in range(64):
            wld = {}
            for wi, wh in [(0, m.wG_h), (1, m.wU_h)]:
                u = 2 * f + wi
                @m.block.gpsimd
                def _(g, f=f, wi=wi, wh=wh, u=u):
                    if u >= 4:
                        g.wait_ge(m.s_pe, mmv[u - 4])
                    g.dma_start(out=WB4[:, u % 4], in_=wh[f]
                                ).then_inc(sl4[u % 4], 16)
                    m.inc(sl4[u % 4], 16)
                wld[wi] = (sl4[u % 4], m.val(sl4[u % 4]))
            g_done = wld[0]
            u_done = wld[1]

            @m.block.tensor
            def _(t, f=f, g_done=g_done, u_done=u_done):
                if f == 0:
                    t.wait_ge(m.s_ur, x1_done)
                t.wait_ge(g_done[0], g_done[1])
                if f >= 1:
                    t.wait_ge(m.s_act, silu_inc[f - 1])
                last = None
                for k in range(16):
                    for tt in range(4):
                        last = t.matmul(self.pf32(tt, 0, 512),
                                        WB4[:, (2 * f) % 4, k],
                                        X1[:, k, tt * 512:tt * 512 + 512],
                                        start=(k == 0), stop=(k == 15))
                last.then_inc(m.s_pe, 1)
                m.inc(m.s_pe, 1)
                mmv[2 * f] = m.val(m.s_pe)
                t.wait_ge(u_done[0], u_done[1])
                if f >= 1:
                    t.wait_ge(m.s_act, ucp_inc[f - 1])
                last = None
                for k in range(16):
                    for tt in range(4):
                        last = t.matmul(self.pf32(4 + tt, 0, 512),
                                        WB4[:, (2 * f + 1) % 4, k],
                                        X1[:, k, tt * 512:tt * 512 + 512],
                                        start=(k == 0), stop=(k == 15))
                last.then_inc(m.s_pe, 1)
                m.inc(m.s_pe, 1)
                mmv[2 * f + 1] = m.val(m.s_pe)
            gmm = m.val(m.s_pe) - 1
            umm = m.val(m.s_pe)
            dve_snap = m.val(m.s_dve)

            @m.block.scalar
            def _(sc, f=f, gmm=gmm, umm=umm, dve_snap=dve_snap):
                sc.wait_ge(m.s_pe, gmm)
                if f >= 2:
                    sc.wait_ge(m.s_dve, hb_inc_d[f - 2])
                sc.drain()
                sc.activation(GB[:, f % 2], self.PS_A[:, 0:2048], AF.Silu
                              ).then_inc(m.s_act, 1)
                m.inc(m.s_act, 1)
                silu_inc[f] = m.val(m.s_act)
                sc.wait_ge(m.s_pe, umm)
                sc.drain()
                sc.copy(UB[:, f % 2], self.PS_B[:, 0:2048]
                        ).then_inc(m.s_act, 1)
                m.inc(m.s_act, 1)
                ucp_inc[f] = m.val(m.s_act)

            @m.block.vector
            def _(v, f=f):
                v.wait_ge(m.s_act, ucp_inc[f])
                if f >= 8:
                    v.wait_ge(m.s_st, st_inc[(f - 8) // 4])
                v.drain()
                v.tensor_tensor(HB8[:, f % 8], GB[:, f % 2], UB[:, f % 2],
                                ALU.mult).then_inc(m.s_dve, 1)
                m.inc(m.s_dve, 1)
            hb_inc_d[f] = m.val(m.s_dve)
            if f >= 6 and (f - 6) % 4 == 0:
                sparsify((f - 6) // 4)
        sparsify(15)

    def _f2(self):
        m = self
        X1 = m.X1V
        WDB = self.sb_at("WDBv", [128, 4, 2, 512], F16, self.sballoc["WBUF"][0])
        HB2 = self.sb_at("HB2v", [128, 4, 2, 1024], F16,
                         self.sballoc["s_run"][0])
        sl = [m.s_sl0, m.s_sl1, m.s_sl2, m.s_sl3]
        hs = [m.s_ur, m.s_ur1, m.s_ur2, m.s_ur3]
        pre_st = m.val(m.s_st)

        for eq in range(4):
            for ttp in range(2):
                ev0 = m.val(m.s_dve)
                pe00 = m.val(m.s_pe)
                for fp in range(32):
                    f = 2 * fp
                    snap_pe = m.val(m.s_pe)

                    @m.block.gpsimd
                    def _(g, eq=eq, fp=fp, f=f, snap_pe=snap_pe, pe00=pe00):
                        if fp >= 4:
                            g.wait_ge(m.s_pe, snap_pe - 3)
                        elif pe00 > 0:
                            g.wait_ge(m.s_pe, pe00)
                        g.dma_start(
                            out=WDB[:, fp % 4].rearrange(
                                "p f j -> p (f j)").rearrange(
                                "p (f e j) -> p f e j", f=2, e=4),
                            in_=m.wD_h[eq, f:f + 2].rearrange(
                                "f e p j -> p f e j")
                            ).then_inc(sl[fp % 4], 16)
                        m.inc(sl[fp % 4], 16)
                    wd_done = m.val(sl[fp % 4])

                    @m.block.sync
                    def _(s, eq=eq, ttp=ttp, fp=fp, f=f, snap_pe=snap_pe,
                          pe00=pe00, pre_st=pre_st):
                        if fp >= 4:
                            s.wait_ge(m.s_pe, snap_pe - 3)
                        else:
                            s.wait_ge(m.s_st, pre_st)
                            s.wait_ge(m.s_pe, pe00)
                        s.dma_start(
                            out=HB2[:, fp % 4],
                            in_=m.hscr_h[f:f + 2, :,
                                         ttp * 1024:ttp * 1024 + 1024]
                            .rearrange("f p t -> p f t")
                            ).then_inc(hs[fp % 4], 16)
                        m.inc(hs[fp % 4], 16)
                    h_done = m.val(hs[fp % 4])

                    @m.block.tensor
                    def _(t, eq=eq, ttp=ttp, fp=fp, wd_done=wd_done,
                          h_done=h_done, ev0=ev0):
                        t.wait_ge(sl[fp % 4], wd_done)
                        t.wait_ge(hs[fp % 4], h_done)
                        if fp == 0:
                            t.wait_ge(m.s_dve, ev0)
                        last = None
                        for fi in range(2):
                            for e2 in range(4):
                                for tt in range(2):
                                    last = t.matmul(
                                        self.pf32(e2 * 2 + tt, 0, 512),
                                        WDB[:, fp % 4, fi,
                                            e2 * 128:e2 * 128 + 128],
                                        HB2[:, fp % 4, fi,
                                            tt * 512:tt * 512 + 512],
                                        start=(fp == 0 and fi == 0),
                                        stop=(fp == 31 and fi == 1))
                        last.then_inc(m.s_pe, 1)
                        m.inc(m.s_pe, 1)
                dn_done = m.val(m.s_pe)

                @m.block.vector
                def _(v, eq=eq, ttp=ttp, dn_done=dn_done):
                    v.wait_ge(m.s_pe, dn_done)
                    for e2 in range(4):
                        for tt in range(2):
                            c = eq * 4 + e2
                            col = ttp * 1024 + tt * 512
                            i = v.tensor_tensor(
                                X1[:, c, col:col + 512],
                                X1[:, c, col:col + 512],
                                self.pf32(e2 * 2 + tt, 0, 512), ALU.add)
                    i.then_inc(m.s_dve, 1)
                    m.inc(m.s_dve, 1)

    def _f3(self):
        m = self
        X1 = m.X1V
        MRB4 = self.sb_at("MRB4", [128, 4, 512], F32, self.sballoc["prevs"][0])
        f2_done = m.val(m.s_dve)
        stats_read = {}
        norm_done = {}
        for tq in range(4):
            par = tq % 2
            bA, bB = 4 + 2 * par, 5 + 2 * par
            sq0 = m.val(m.s_dve)
            pe_before = m.val(m.s_pe)

            @m.block.vector
            def _(v, tq=tq, pe_before=pe_before, f2_done=f2_done):
                if tq == 0:
                    v.wait_ge(m.s_dve, f2_done)
                n = 0
                for e in range(16):
                    if n > 0:
                        v.wait_ge(m.s_pe, pe_before + n)
                    v.drain()
                    v.tensor_tensor(m.SQF2[:, tq % 2],
                                    X1[:, e, tq * 512:tq * 512 + 512],
                                    X1[:, e, tq * 512:tq * 512 + 512],
                                    ALU.mult).then_inc(m.s_dve, 1)
                    m.inc(m.s_dve, 1)
                    n += 1

            @m.block.tensor
            def _(t, tq=tq, sq0=sq0, f2_done=f2_done, bA=bA, bB=bB):
                if tq == 0:
                    t.wait_ge(m.s_dve, f2_done)
                elif tq >= 2:
                    t.wait_ge(m.s_dve, stats_read[tq - 2])
                for e in range(16):
                    t.matmul(self.pf32(bA, 0, 512)[0:1, :], m.o128x1,
                             X1[:, e, tq * 512:tq * 512 + 512],
                             start=(e == 0), stop=(e == 15))
                    t.wait_ge(m.s_dve, sq0 + e + 1)
                    t.matmul(self.pf32(bB, 0, 512)[0:1, :], m.o128x1,
                             m.SQF2[:, tq % 2],
                             start=(e == 0), stop=(e == 15)
                             ).then_inc(m.s_pe, 1)
                    m.inc(m.s_pe, 1)
            ln_pe = m.val(m.s_pe)

            @m.block.vector
            def _(v, tq=tq, ln_pe=ln_pe, bA=bA, bB=bB,
                  er_snap=m.val(m.s_er)):
                v.wait_ge(m.s_pe, ln_pe)
                v.wait_ge(m.s_er, er_snap)
                v.drain()
                v.tensor_scalar_mul(m.MROW[:, 0:512],
                                    self.pf32(bA, 0, 512, p=1), 1.0 / D)
                v.drain()
                v.tensor_scalar_mul(m.RROW[:, 0:512],
                                    self.pf32(bB, 0, 512, p=1), 1.0 / D)
                v.drain()
                v.tensor_tensor(m.SGM[0:1, 0:512], m.MROW[:, 0:512],
                                m.MROW[:, 0:512], ALU.mult)
                v.drain()
                v.tensor_tensor(m.RROW[:, 0:512], m.RROW[:, 0:512],
                                m.SGM[0:1, 0:512], ALU.subtract)
                v.drain()
                v.tensor_scalar_add(m.RROW[:, 0:512], m.RROW[:, 0:512],
                                    EPS_LN).then_inc(m.s_dve, 1)
                m.inc(m.s_dve, 1)
            stats_read[tq] = m.val(m.s_dve)
            var_done = m.val(m.s_dve)

            @m.block.scalar
            def _(sc, var_done=var_done):
                sc.wait_ge(m.s_dve, var_done)
                sc.drain()
                sc.activation(m.RROW[:, 0:512], m.RROW[:, 0:512], AF.Sqrt).then_inc(m.s_act, 1)
                m.inc(m.s_act, 1)
            st_done = m.val(m.s_act)

            @m.block.vector
            def _(v, st_done=st_done):
                v.wait_ge(m.s_act, st_done)
                v.drain()
                v.reciprocal(m.RROW[:, 0:512], m.RROW[:, 0:512]).then_inc(m.s_dve, 1)
                m.inc(m.s_dve, 1)
            rr_done = m.val(m.s_dve)
            er0 = m.val(m.s_er)

            @m.block.gpsimd
            def _(g, tq=tq, rr_done=rr_done, er0=er0, par=par):
                g.wait_ge(m.s_dve, rr_done)
                g.dma_start(out=m.rst_h[:, 0:512], in_=m.RROW[:, 0:512]
                            ).then_inc(m.s_er, 16)
                m.inc(m.s_er, 16)
                g.dma_start(out=m.rst_h[:, 512:1024], in_=m.MROW[:, 0:512]
                            ).then_inc(m.s_er, 16)
                m.inc(m.s_er, 16)
                g.wait_ge(m.s_er, er0 + 32)
                g.dma_start(out=MRB4[:, 2 * par], in_=m.rst_h[:, 0:512]
                            .partition_broadcast(128)).then_inc(m.s_er, 16)
                m.inc(m.s_er, 16)
                g.dma_start(out=MRB4[:, 2 * par + 1], in_=m.rst_h[:, 512:1024]
                            .partition_broadcast(128)).then_inc(m.s_er, 16)
                m.inc(m.s_er, 16)
            mrb_done = m.val(m.s_er)

            @m.block.vector
            def _(v, tq=tq, mrb_done=mrb_done, par=par,
                  st_snap=m.val(m.s_wdma)):
                v.wait_ge(m.s_er, mrb_done)
                v.wait_ge(m.s_wdma, st_snap)
                for e in range(16):
                    v.drain()
                    v.tensor_tensor(X1[:, e, tq * 512:tq * 512 + 512],
                                    X1[:, e, tq * 512:tq * 512 + 512],
                                    MRB4[:, 2 * par + 1], ALU.subtract)
                    v.drain()
                    i = v.tensor_tensor(X1[:, e, tq * 512:tq * 512 + 512],
                                        X1[:, e, tq * 512:tq * 512 + 512],
                                        MRB4[:, 2 * par], ALU.mult)
                i.then_inc(m.s_dve, 1)
                m.inc(m.s_dve, 1)
            norm_done[tq] = m.val(m.s_dve)

            @m.block.gpsimd
            def _(g, tq=tq, nd=norm_done[tq]):
                g.wait_ge(m.s_dve, nd)
                g.dma_start(
                    out=m.out_h.rearrange("(c p) t -> p c t", p=128)
                    [:, :, tq * 512:tq * 512 + 512],
                    in_=X1[:, :, tq * 512:tq * 512 + 512]
                    ).then_inc(m.s_wdma, 16)
                m.inc(m.s_wdma, 16)

    def _dump_tau(self, tau):
        m = self
        d = m.dbg_out(f"dbg_b1t{tau}", [128, 34, TAU + HALO], F16)
        if d is not None:
            hof = HALO if tau == 0 else 0
            @m.block.sync
            def _(s):
                s.wait_ge(m.s_act, m.val(m.s_act))
                s.wait_ge(m.s_dve, m.val(m.s_dve))
                s.dma_start(out=d[:, :, hof:hof + TAU],
                            in_=m.B1[:, 0:34, hof:hof + TAU]).then_inc(m.s_st, 16)
                m.inc(m.s_st, 16)
                s.wait_ge(m.s_st, m.val(m.s_st))

    def _finish_debug(self):
        m = self

        d3 = m.dbg_out("dbg_srun", [64, DIN], F32)
        if d3 is not None:
            @m.block.sync
            def _(s):
                s.wait_ge(m.s_dve, m.val(m.s_dve))
                s.dma_start(out=d3, in_=m.s_run).then_inc(m.s_st, 16)
                m.inc(m.s_st, 16)
        d4 = m.dbg_out("dbg_a", [32, T], F32)
        if d4 is not None:
            @m.block.sync
            def _(s):
                s.wait_ge(m.s_dve, m.val(m.s_dve))
                s.dma_start(out=d4, in_=m.a_park).then_inc(m.s_st, 16)
                m.inc(m.s_st, 16)
        d5 = m.dbg_out("dbg_y", [128, 32, T], F16)
        if d5 is not None:
            @m.block.sync
            def _(s):
                s.wait_ge(m.s_yd, m.val(m.s_yd))
                s.dma_start(out=d5, in_=m.ypark_h).then_inc(m.s_st, 16)
                m.inc(m.s_st, 16)
        d6 = m.dbg_out("dbg_x1", [128, 16, T], F16)
        if d6 is not None:
            @m.block.sync
            def _(s):
                s.wait_ge(m.s_st, m.val(m.s_st))
                s.dma_start(out=d6, in_=m.x1park_h).then_inc(m.s_st, 16)
                m.inc(m.s_st, 16)
        d2 = m.dbg_out("dbg_dt", [32, T], F32)
        if d2 is not None:
            @m.block.sync
            def _(s):
                s.wait_ge(m.s_act, m.val(m.s_act))
                s.dma_start(out=d2, in_=m.dt_park).then_inc(m.s_st, 16)
                m.inc(m.s_st, 16)

    def _final_wait(self):
        m = self

        @m.block.sync
        def _(s):
            for name in ["s_wdma", "s_ldma", "s_pe", "s_dve", "s_act", "s_gp",
                         "s_st", "s_cc", "s_wu", "s_sl0", "s_sl1", "s_yd",
                         "s_ai", "s_ai2", "s_af", "s_er", "s_er2", "s_tp",
                      "s_ur", "s_ur1", "s_sl2", "s_sl3", "s_ur2", "s_ur3"]:
                sem = getattr(m, name)
                s.wait_ge(sem, m.val(sem))



def prep_inputs(inputs):
    W = {}
    ip = np.ascontiguousarray(inputs["L0_attn_in_proj_w"])
    rows = np.zeros((NXC * 128, D), np.float32)
    rows[:4096] = ip[4096:8192]
    rows[4096:4160] = ip[8192:8256]
    rows[4224:4288] = ip[8256:8320]
    rows[4352:4384] = ip[8320:8352]
    W["wA"] = np.ascontiguousarray(
        rows.reshape(NXC, 128, 16, 128).transpose(0, 3, 2, 1))
    W["wZ"] = np.ascontiguousarray(
        ip[0:4096].reshape(32, 128, 16, 128).transpose(0, 3, 2, 1))
    op = np.ascontiguousarray(inputs["L0_attn_o_proj_w"])
    W["wO"] = np.ascontiguousarray(
        op.reshape(16, 128, 32, 128).transpose(0, 3, 2, 1))

    f_idx = np.arange(64)
    p_idx = np.arange(128)
    iperm = (4 * (128 * (f_idx[:, None] // 4) + p_idx[None, :])
             + (f_idx[:, None] % 4))
    wg = np.ascontiguousarray(inputs["L0_ffn_w_gate"])
    wu = np.ascontiguousarray(inputs["L0_ffn_w_up"])
    wd = np.ascontiguousarray(inputs["L0_ffn_w_down"])
    wgp = wg[:, iperm.reshape(-1)].reshape(D, 64, 128)
    W["wG"] = np.ascontiguousarray(
        wgp.reshape(16, 128, 64, 128).transpose(2, 1, 0, 3))
    wup = wu[:, iperm.reshape(-1)].reshape(D, 64, 128)
    W["wU"] = np.ascontiguousarray(
        wup.reshape(16, 128, 64, 128).transpose(2, 1, 0, 3))
    wdp = wd[iperm.reshape(-1), :].reshape(64, 128, 2048)
    W["wD"] = np.ascontiguousarray(
        wdp.reshape(64, 128, 4, 4, 128).transpose(2, 0, 3, 1, 4))

    cw = np.asarray(inputs["L0_attn_conv_w"], np.float32)
    cb = np.asarray(inputs["L0_attn_conv_b"], np.float32)
    cwp = np.zeros((34 * 128, 4), np.float32)
    cbp = np.zeros((34 * 128,), np.float32)
    cwp[:4096] = cw[:4096]
    cbp[:4096] = cb[:4096]
    cwp[4096:4160] = cw[4096:4160]
    cbp[4096:4160] = cb[4096:4160]
    cwp[4224:4288] = cw[4160:4224]
    cbp[4224:4288] = cb[4160:4224]
    W["convw"] = np.ascontiguousarray(
        cwp.reshape(34, 128, 4).transpose(1, 0, 2).reshape(128, 34 * 4))
    W["convb"] = np.ascontiguousarray(cbp.reshape(34, 128).T)

    W["alog"] = np.asarray(inputs["L0_attn_A_log"], np.float32).reshape(32, 1)
    W["dtb"] = np.asarray(inputs["L0_attn_dt_bias"], np.float32).reshape(32, 1)
    W["dskip"] = np.asarray(inputs["L0_attn_D_skip"], np.float32).reshape(1, 32)
    W["nw"] = np.ascontiguousarray(
        np.asarray(inputs["L0_attn_norm_w"], np.float32).reshape(32, 128).T)
    W["identc"] = np.eye(128, dtype=np.float32)
    W["ident32c"] = np.eye(128, dtype=np.float32)
    i_, j_ = np.mgrid[0:128, 0:128]
    W["maskTc"] = (j_ >= i_).astype(np.float32)
    W["o1x128"] = np.ones((1, 128), np.float32)
    W["o128x1"] = np.ones((128, 1), np.float32)
    oc = np.zeros((128, 32, 32), np.float32)
    for h in range(32):
        oc[:, h, h] = 1.0
    W["ocols"] = oc.reshape(128, 32 * 32)
    W["oscan"] = np.ones((32, CH), np.float32)
    return W


def prep_core_inputs(inputs, W, core):
    uT = inputs["_uT"]
    d = dict(W)
    d["u"] = np.ascontiguousarray(uT[:, core * T:core * T + T + HALO])
    g = (np.arange(8) < core).astype(np.float32)
    dd = (np.arange(8) <= core - 1).astype(np.float32)
    d["gates"] = np.concatenate([g, dd]).reshape(1, 16).astype(np.float32)
    return d


_BUILD_CACHE = {}


def kernel(**inputs):
    u = np.asarray(inputs["L0_attn_u"], np.float32)
    uT = np.zeros((D, HALO + 16384), np.float32)
    uT[:, HALO:] = u[0].T
    inputs = dict(inputs)
    inputs["_uT"] = uT

    W = prep_inputs(inputs)

    mk = MK()
    nc = mk.build()
    in_maps = [prep_core_inputs(inputs, W, c) for c in range(NCORES)]
    res = run_bass_kernel_spmd(nc, in_maps, core_ids=list(range(NCORES)))
    outs = [res.results[c]["out"] for c in range(NCORES)]
    full = np.concatenate([o.T for o in outs], axis=0)
    return full.reshape(1, 16384, D).astype(np.float32)


# version 13 (final)
# speedup vs baseline: 1.0045x; 1.0040x over previous
import contextlib
import numpy as np
import concourse.bass as bass
import concourse.mybir as mybir
from concourse.bass_utils import run_bass_kernel_spmd

F32 = mybir.dt.float32
F16 = mybir.dt.float16
AF = mybir.ActivationFunctionType
ALU = mybir.AluOpType

NCORES = 8
T = 2048
TAU = 1024
HALO = 3
CH = 256
D = 2048
DIN = 4096
NH = 32
FFI = 8192
NXC = 35
EPS_RMS = 1e-6
EPS_LN = 1e-5
AG_N = 64 * DIN + 32
SBUF_BASE = 16512
SBUF_MAX = 212736


class MK:
    def __init__(self, debug=()):
        self.debug = set(debug)
        nc = bass.Bass("TRN2", target_bir_lowering=False)
        self.nc = nc
        self.semvals = {}
        self.dbg = {}
        self.sballoc = {}

        dt_ = nc.dram_tensor
        self.ident32_h = dt_("ident32c", [128, 128], F32, kind="ExternalInput").ap()
        self.u_h = dt_("u", [D, T + HALO], F32, kind="ExternalInput").ap()
        self.wA_h = dt_("wA", [NXC, 128, 16, 128], F32, kind="ExternalInput").ap()
        self.wZ_h = dt_("wZ", [32, 128, 16, 128], F32, kind="ExternalInput").ap()
        self.wO_h = dt_("wO", [16, 128, 32, 128], F32, kind="ExternalInput").ap()
        self.wG_h = dt_("wG", [64, 128, 16, 128], F32, kind="ExternalInput").ap()
        self.wU_h = dt_("wU", [64, 128, 16, 128], F32, kind="ExternalInput").ap()
        self.wD_h = dt_("wD", [4, 64, 4, 128, 128], F32, kind="ExternalInput").ap()
        self.convw_h = dt_("convw", [128, 34 * 4], F32, kind="ExternalInput").ap()
        self.convb_h = dt_("convb", [128, 34], F32, kind="ExternalInput").ap()
        self.alog_h = dt_("alog", [32, 1], F32, kind="ExternalInput").ap()
        self.dtb_h = dt_("dtb", [32, 1], F32, kind="ExternalInput").ap()
        self.dskip_h = dt_("dskip", [1, 32], F32, kind="ExternalInput").ap()
        self.nw_h = dt_("nw", [128, 32], F32, kind="ExternalInput").ap()
        self.ident_h = dt_("identc", [128, 128], F32, kind="ExternalInput").ap()
        self.maskT_h = dt_("maskTc", [128, 128], F32, kind="ExternalInput").ap()
        self.o1x128_h = dt_("o1x128", [1, 128], F32, kind="ExternalInput").ap()
        self.o128x1_h = dt_("o128x1", [128, 1], F32, kind="ExternalInput").ap()
        self.oscan_h = dt_("oscan", [32, CH], F32, kind="ExternalInput").ap()
        self.gates_h = dt_("gates", [1, 16], F32, kind="ExternalInput").ap()
        self.ocols_h = dt_("ocols", [128, 32 * 32], F32, kind="ExternalInput").ap()

        self.out_h = dt_("out", [D, T], F32, kind="ExternalOutput").ap()

        self.ypark_h = dt_("ypark", [128, 32, T], F16, kind="Internal").ap()
        self.hscr_h = dt_("hscr", [64, 128, T], F16, kind="Internal").ap()
        self.agin_h = dt_("agin", [AG_N], F32, kind="Internal").ap()
        self.agout_h = dt_("agout", [NCORES, AG_N], F32, kind="Internal").ap()
        self.aflat_h = dt_("aflat", [1, 8192], F32, kind="Internal").ap()
        self.erow_h = dt_("erowh", [1, 32], F32, kind="Internal").ap()
        self.rst_h = dt_("rsth", [1, 1024], F32, kind="Internal").ap()
        self.rstdh_h = dt_("rstdh", [32, 1024], F32, kind="Internal").ap()
        self.aflat8_h = dt_("aflat8", [8, 8192], F32, kind="Internal").ap()
        self.erow8_h = dt_("erow8", [8, 32], F32, kind="Internal").ap()
        self.x1park_h = dt_("x1park", [128, 16, T], F16, kind="Internal").ap()

    def dbg_out(self, name, shape, dtype):
        if name in self.debug and name not in self.dbg:
            self.dbg[name] = self.nc.dram_tensor(
                name, shape, dtype, kind="ExternalOutput").ap()
        return self.dbg.get(name)

    def sem(self, name):
        s = self.es.enter_context(self.nc.semaphore(name))
        self.semvals[s.name] = 0
        setattr(self, name, s)
        return s

    def inc(self, s, n=1):
        self.semvals[s.name] += n
        return self.semvals[s.name]

    def val(self, s):
        return self.semvals[s.name]

    def sb_at(self, name, shape, dtype, off):
        nbytes = int(np.prod(shape[1:])) * mybir.dt.size(dtype)
        assert off + nbytes <= SBUF_MAX, (name, off, nbytes)
        self.sballoc[name] = (off, nbytes)
        return self.nc.alloc_sbuf_tensor_at(
            name, list(shape), dtype, offset=SBUF_BASE + off).ap()

    def build(self):
        nc = self.nc
        with contextlib.ExitStack() as es:
            self.es = es
            self._alloc_sbuf()
            self._alloc_psum()
            for s in ["s_wdma", "s_ldma", "s_pe", "s_dve", "s_act", "s_gp",
                      "s_st", "s_cc", "s_wu", "s_sl0", "s_sl1", "s_yd",
                      "s_ai", "s_ai2", "s_af", "s_er", "s_er2", "s_tp",
                      "s_ur", "s_ur1", "s_sl2", "s_sl3", "s_ur2", "s_ur3"]:
                self.sem(s)
            self.block = es.enter_context(nc.Block())
            self._emit_all()
        return nc

    def _alloc_sbuf(self):
        off = 0

        def A(name, shape, dtype):
            nonlocal off
            t = self.sb_at(name, shape, dtype, off)
            off += (self.sballoc[name][1] + 31) // 32 * 32
            return t

        self.ident = A("identS", [128, 128], F16)
        self.ident32 = A("ident32S", [128, 128], F32)
        self.maskT = A("maskTS", [128, 128], F16)
        self.o1x128 = A("o1x128S", [1, 128], F16)
        self.o128x1 = A("o128x1S", [128, 1], F16)
        self.oscan = A("oscanS", [32, CH], F32)
        self.convw = A("convwS", [128, 34, 4], F32)
        self.convb = A("convbS", [128, 34], F32)
        self.alog = A("alogS", [32, 1], F32)
        self.dtb = A("dtbS", [32, 1], F32)
        self.acol = A("acolS", [32, 1], F32)
        self.dbc = A("dbcS", [128, 32], F32)
        self.nw = A("nwS", [128, 32], F32)
        self.gbc = A("gbcS", [64, 16], F32)
        self.atot = A("atotS", [32, 1], F32)
        self.acum = A("acumS", [32, 8], F32)
        self.ag_atot = A("agatotS", [64, 8], F32)
        self.erow = A("erowS", [64, 32], F32)
        self.er32 = A("er32S", [32, 1], F32)

        self.B1 = A("B1", [128, NXC, TAU + HALO], F16)
        self.dt_park = A("dt_park", [32, T], F32)
        self.a_park = A("a_park", [32, T], F32)
        self.c_park = A("c_park", [64, T], F16)
        self.s_run = A("s_run", [64, DIN], F32)
        self.prevs = A("prevs", [64, DIN], F16)
        self.carry3 = A("carry3", [128, 34, 4], F16)

        self.AR_U = A("AR_U", [128, 16 * (TAU + HALO)], F16)
        self.A_I = self.sb_at("A_I", [128, 8192], F32,
                              self.sballoc["AR_U"][0])
        self.EA16 = self.sb_at("EA16", [64, 4096], F16,
                               self.sballoc["AR_U"][0] + 24576)
        self.XTM = A("XTM", [128, 2 * DIN], F16)
        self.s_in = self.sb_at("s_in", [64, DIN], F32,
                               self.sballoc["XTM"][0])
        self.SCL = A("SCL", [128, DIN], F16)
        self.W16 = A("W16", [128, 8192], F16)
        self.SSB = A("SSB", [128, 384], F32)
        self.WBUF = A("WBUF", [128, 2, 16, 128], F16)
        self.CONVACC = A("CONVACC", [128, 2, 1024], F32)
        self.TT32 = self.sb_at("TT32", [128, 256], F32,
                               self.sballoc["CONVACC"][0])
        self.atall = A("atallS", [32, 8], F32)
        self.X1F = None
        self.SINKG = self.sb_at("SINKG", [64, 4096], F32,
                                self.sballoc["W16"][0])
        self.SINK16 = self.sb_at("SINK16", [64, 4096], F16,
                                 self.sballoc["SCL"][0])
        self.AB64 = self.sb_at("AB64", [64, 4096], F32,
                               self.sballoc["AR_U"][0])
        self.WO_BUF = self.sb_at("WO_BUF", [128, 2, 32, 128], F16,
                                 self.sballoc["s_run"][0])
        self.X1PB = self.sb_at("X1PB", [128, 16, TAU], F16,
                               self.sballoc["AR_U"][0])
        self.X1 = self.sb_at("X1", [128, 16, T], F16,
                             self.sballoc["B1"][0])
        self.RROW = self.sb_at("RROW", [1, 512], F32,
                               self.sballoc["CONVACC"][0] + 4096)
        self.MROW = self.sb_at("MROW", [1, 512], F32,
                               self.sballoc["CONVACC"][0] + 6144)
        self.MRB = self.sb_at("MRB", [128, 2, 512], F32,
                              self.sballoc["prevs"][0])
        self.MRB2 = self.sb_at("MRB2", [128, 2, 1024], F32,
                               self.sballoc["prevs"][0])
        self.ocols = self.sb_at("ocolsS", [128, 32, 32], F16,
                                self.sballoc["dt_park"][0])
        self.erow8sb = self.sb_at("erow8sb", [32, 8], F32,
                                  self.sballoc["agatotS"][0])
        self.erow8 = self.sb_at("erow8", [64, 256], F32,
                                self.sballoc["SSB"][0])
        self.AB64s = [self.sb_at("AB64a", [64, 4096], F32,
                                 self.sballoc["AR_U"][0]),
                      self.sb_at("AB64b", [64, 4096], F32,
                                 self.sballoc["AR_U"][0] + 16384)]
        self.EA16s = [self.sb_at("EA16a", [64, 4096], F16,
                                 self.sballoc["W16"][0]),
                      self.sb_at("EA16b", [64, 4096], F16,
                                 self.sballoc["W16"][0] + 8192)]
        self.RST32 = self.sb_at("RST32", [32, 1024], F32,
                                self.sballoc["CONVACC"][0])
        self.SGM = self.sb_at("SGM", [128, 1024], F32,
                              self.sballoc["CONVACC"][0] + 0)
        self.URES = self.sb_at("URES", [128, 2, 512], F32,
                               self.sballoc["CONVACC"][0] + 0)
        self.SQ16 = self.sb_at("SQ16", [128, 2048], F16,
                               self.sballoc["SCL"][0])
        self.SQF2 = self.sb_at("SQF2", [128, 2, 512], F16,
                               self.sballoc["SCL"][0])
        self.GBUF = self.sb_at("GBUF", [128, 2048], F16,
                               self.sballoc["CONVACC"][0])
        self.HBUF = self.sb_at("HBUF", [128, 4, 2048], F16,
                               self.sballoc["dt_park"][0])
        self.ABS4 = self.sb_at("ABS4", [128, 4, 2048], F16,
                               self.sballoc["s_run"][0])
        self.F1S = self.sb_at("F1S", [128, 4, 2048], F16,
                              self.sballoc["AR_U"][0])
        self.WDB = self.sb_at("WDB", [128, 2, 512], F16,
                              self.sballoc["WBUF"][0])
        self.HB2 = self.sb_at("HB2", [128, 2, 1024], F16,
                              self.sballoc["WBUF"][0] + 2048)
        self.OUT32 = self.sb_at("OUT32", [128, 16, 512], F32,
                                self.sballoc["AR_U"][0])
        self.SQF = self.sb_at("SQF", [128, 512], F16,
                              self.sballoc["SCL"][0])
        self.end_fixed = off
        assert off <= SBUF_MAX, off

    def _alloc_psum(self):
        self.PS_A = self.nc.alloc_psum_tensor("PS_A", [128, 2048], F32).ap()
        self.PS_B = self.nc.alloc_psum_tensor("PS_B", [128, 2048], F32).ap()

    def bank(self, i, w=512, p=128):
        base = self.PS_A if i < 4 else self.PS_B
        j = i % 4
        return base[0:p, 512 * j:512 * j + w]

    def bank16(self, i, w=128, p=128):
        base = self.PS_A if i < 4 else self.PS_B
        j = i % 4
        return base.bitcast(F16)[0:p, 1024 * j:1024 * j + w]

    def _emit_all(self):
        self._consts()
        for tau in range(2):
            self._a1(tau)
            self._conv(tau)
            self._a2_prep(tau)
            for kc in range(4):
                self._a2_chunk(tau, kc)
            self._a2_tau_end(tau)
            self._dump_tau(tau)
        self._gather()
        self._gather_corrprep()
        for tau in range(2):
            self._a3(tau)
            self._a4(tau)
        self._f1()
        self._f2()
        self._f3()
        self._finish_debug()
        self._final_wait()

    def _consts(self):
        m = self

        @m.block.gpsimd
        def _(g):
            g.memset(m.s_run, 0.0).then_inc(m.s_gp, 1)
            m.inc(m.s_gp, 1)
            for dst, src in [(m.ident, m.ident_h), (m.o1x128, m.o1x128_h),
                             (m.o128x1, m.o128x1_h), (m.maskT, m.maskT_h)]:
                g.dma_start(out=dst, in_=src).then_inc(m.s_wdma, 16)
                m.inc(m.s_wdma, 16)
            for dst, src, p in [(m.dbc, m.dskip_h, 128), (m.gbc, m.gates_h, 64)]:
                g.dma_start(out=dst, in_=src.partition_broadcast(p)
                            ).then_inc(m.s_wdma, 16)
                m.inc(m.s_wdma, 16)

        @m.block.sync
        def _(s):
            for dst, src in [(m.ident32, m.ident32_h),
                             (m.oscan, m.oscan_h),
                             (m.convb, m.convb_h), (m.alog, m.alog_h),
                             (m.dtb, m.dtb_h), (m.nw, m.nw_h)]:
                s.dma_start(out=dst, in_=src).then_inc(m.s_ldma, 16)
                m.inc(m.s_ldma, 16)
            s.dma_start(out=m.convw,
                        in_=m.convw_h.rearrange("p (j k) -> p j k", k=4)
                        ).then_inc(m.s_ldma, 16)
            m.inc(m.s_ldma, 16)

        @m.block.scalar
        def _(sc):
            sc.wait_ge(m.s_ldma, m.val(m.s_ldma))
            sc.wait_ge(m.s_wdma, m.val(m.s_wdma))
            sc.activation(m.acol, m.alog, AF.Exp, scale=1.0)
            sc.drain()
            sc.mul(m.acol, m.acol, -1.0)
            sc.nop().then_inc(m.s_act, 1)
            m.inc(m.s_act, 1)

    def _a1(self, tau):
        m = self
        TW = TAU + HALO if tau == 0 else TAU
        u0 = 0 if tau == 0 else TAU + HALO
        tts = [(i, min(i + 512, TW)) for i in range(0, TW, 512)]
        ntt = len(tts)
        dtc0 = tau * TAU
        hof = HALO if tau == 0 else 0

        pre_act = m.val(m.s_act)
        pre_dve = m.val(m.s_dve)
        pre_pe = m.val(m.s_pe)
        pre_st = m.val(m.s_st)
        pre_yd = m.val(m.s_yd)

        @m.block.gpsimd
        def _(g):
            g.wait_ge(m.s_act, pre_act)
            g.wait_ge(m.s_dve, pre_dve)
            g.wait_ge(m.s_pe, pre_pe)
            g.wait_ge(m.s_st, pre_st)
            g.wait_ge(m.s_yd, pre_yd)
            for si, (t0, t1) in enumerate(tts):
                if si > 0:
                    g.wait_ge(m.s_wu, m.val(m.s_wu))
                g.dma_start(
                    out=m.AR_U[:, 0:16 * TW].rearrange(
                        "p (k t) -> p k t", t=TW)[:, :, t0:t1],
                    in_=m.u_h[:, u0:u0 + TW].rearrange(
                        "(k p) t -> p k t", p=128)[:, :, t0:t1],
                ).then_inc(m.s_wu, 16)
                m.inc(m.s_wu, 16)
        uload = m.val(m.s_wu)

        evac0 = m.val(m.s_act)
        pe0 = m.val(m.s_pe)
        sl = [m.s_sl0, m.s_sl1]
        sl0_0, sl1_0 = m.val(m.s_sl0), m.val(m.s_sl1)
        slbase = [sl0_0, sl1_0]

        @m.block.gpsimd
        def _(g):
            for e in range(NXC):
                if e >= 2:
                    g.wait_ge(m.s_pe, pe0 + (e - 1) * ntt)
                g.dma_start(out=m.WBUF[:, e % 2], in_=m.wA_h[e]
                            ).then_inc(sl[e % 2], 16)
                m.inc(sl[e % 2], 16)

        @m.block.tensor
        def _(t):
            gi = 0
            for e in range(NXC):
                t.wait_ge(sl[e % 2], slbase[e % 2] + 16 * (e // 2 + 1))
                for ti, (t0, t1) in enumerate(tts):
                    if e == 0:
                        t.wait_ge(m.s_wu, uload - 16 * (ntt - ti - 1))
                    bk = gi % 6
                    if gi >= 6:
                        t.wait_ge(m.s_act, evac0 + (gi - 6 + 1))
                    last = None
                    for k in range(16):
                        last = t.matmul(self.bank(bk, t1 - t0),
                                        m.WBUF[:, e % 2, k],
                                        m.AR_U[:, 0:16 * TW].rearrange(
                                            "p (k t) -> p k t", t=TW)[:, k, t0:t1],
                                        start=(k == 0), stop=(k == 15))
                    last.then_inc(m.s_pe, 1)
                    m.inc(m.s_pe, 1)
                    gi += 1

        @m.block.scalar
        def _(sc):
            gi = 0
            for e in range(NXC):
                for ti, (t0, t1) in enumerate(tts):
                    bk = gi % 6
                    sc.wait_ge(m.s_pe, pe0 + gi + 1)
                    if e < 34:
                        i = sc.copy(m.B1[:, e, t0:t1], self.bank(bk, t1 - t0))
                    else:
                        lo = max(t0, hof)
                        if lo < t1:
                            i = sc.copy(
                                m.dt_park[:, dtc0 + lo - hof:dtc0 + t1 - hof],
                                self.bank(bk, t1 - t0, p=32)[:, lo - t0:t1 - t0])
                        else:
                            i = sc.nop()
                    i.then_inc(m.s_act, 1)
                    m.inc(m.s_act, 1)
                    gi += 1

    def _conv(self, tau):
        m = self
        TW = TAU + HALO if tau == 0 else TAU
        hof = HALO if tau == 0 else 0
        ntt = 3 if tau == 0 else 2
        pre = m.val(m.s_act)
        a1_evac0 = pre - NXC * ntt
        dve0 = m.val(m.s_dve)
        act0 = m.val(m.s_act)
        ldc = m.val(m.s_ldma)

        @m.block.vector
        def _(v):
            v.wait_ge(m.s_ldma, ldc)
            for j in range(34):
                v.wait_ge(m.s_act, a1_evac0 + (j + 1) * ntt)
                acc = m.CONVACC[:, j % 2]
                if j >= 2:
                    v.wait_ge(m.s_act, act0 + (j - 2) + 1)
                last = None
                if tau == 0:
                    for k in range(4):
                        src_ = m.B1[:, j, k:k + TAU]
                        if k == 0:
                            v.tensor_scalar(acc, src_, m.convw[:, j, 0:1],
                                            None, ALU.mult)
                        else:
                            v.drain()
                            last = v.scalar_tensor_tensor(
                                acc, src_, m.convw[:, j, k:k + 1],
                                acc, ALU.mult, ALU.add)
                else:
                    for k in range(4):
                        src_ = m.B1[:, j, k:k + TAU - 3]
                        if k == 0:
                            v.tensor_scalar(acc[:, 3:TAU], src_,
                                            m.convw[:, j, 0:1], None, ALU.mult)
                        else:
                            v.drain()
                            v.scalar_tensor_tensor(
                                acc[:, 3:TAU], src_, m.convw[:, j, k:k + 1],
                                acc[:, 3:TAU], ALU.mult, ALU.add)
                    for t in range(3):
                        first = True
                        for k in range(4):
                            p = t - 3 + k
                            src_ = (m.carry3[:, j, 3 + p:4 + p] if p < 0
                                    else m.B1[:, j, p:p + 1])
                            if first:
                                v.drain()
                                v.tensor_scalar(acc[:, t:t + 1], src_,
                                                m.convw[:, j, k:k + 1], None,
                                                ALU.mult)
                                first = False
                            else:
                                v.drain()
                                last = v.scalar_tensor_tensor(
                                    acc[:, t:t + 1], src_,
                                    m.convw[:, j, k:k + 1],
                                    acc[:, t:t + 1], ALU.mult, ALU.add)
                last.then_inc(m.s_dve, 1)
                m.inc(m.s_dve, 1)
                if tau == 0:
                    v.tensor_copy(m.carry3[:, j, 0:3], m.B1[:, j, TW - 3:TW]
                                  ).then_inc(m.s_dve, 1)
                    m.inc(m.s_dve, 1)

        nd = 2 if tau == 0 else 1

        @m.block.scalar
        def _(sc):
            for j in range(34):
                sc.wait_ge(m.s_dve, dve0 + nd * (j + 1))
                sc.activation(m.B1[:, j, hof:hof + TAU], m.CONVACC[:, j % 2],
                              AF.Silu, bias=m.convb[:, j:j + 1], scale=1.0
                              ).then_inc(m.s_act, 1)
                m.inc(m.s_act, 1)

    def _a2_prep(self, tau):
        m = self
        c0 = tau * TAU
        pre_act = m.val(m.s_act)
        V = m.CONVACC[0:32, 0, 0:1024]
        W = m.CONVACC[0:32, 1, 0:1024]

        @m.block.scalar
        def _(sc):
            sc.wait_ge(m.s_act, pre_act)
            sc.wait_ge(m.s_dve, m.val(m.s_dve))
            sc.drain()
            sc.activation(V, m.dt_park[:, c0:c0 + TAU], AF.Identity,
                          bias=m.dtb[:, 0:1], scale=1.0)
            sc.drain()
            sc.activation(W, V, AF.Abs)
            sc.drain()
            sc.activation(W, W, AF.Exp, scale=-1.0)
            sc.drain()
            sc.activation(W, W, AF.Ln, bias=1.0).then_inc(m.s_act, 1)
            m.inc(m.s_act, 1)
        actv = m.val(m.s_act)

        @m.block.vector
        def _(v):
            v.wait_ge(m.s_act, actv)
            v.tensor_scalar_max(V, V, 0.0)
            v.drain()
            v.tensor_add(m.dt_park[:, c0:c0 + TAU], V, W)
            v.drain()
            v.tensor_scalar_mul(V, m.dt_park[:, c0:c0 + TAU], m.acol[:, 0:1])
            v.drain()
            for kc in range(4):
                i = v.tensor_tensor_scan(
                    m.a_park[:, c0 + kc * CH:c0 + (kc + 1) * CH],
                    m.oscan, V[:, kc * CH:(kc + 1) * CH], 0.0,
                    ALU.mult, ALU.add)
            i.then_inc(m.s_dve, 1)
            m.inc(m.s_dve, 1)

    def pslot16(self, bank, slot, w=128):
        base = (bank % 4) * 1024 + slot * 128
        t = self.PS_A if bank < 4 else self.PS_B
        return t.bitcast(F16)[:, base:base + w]

    def pf32(self, bank, off, w, p=128):
        t = self.PS_A if bank < 4 else self.PS_B
        return t[0:p, (bank % 4) * 512 + off:(bank % 4) * 512 + off + w]

    def _a2_chunk(self, tau, kc):
        m = self
        hof = HALO if tau == 0 else 0
        cc0 = hof + kc * CH
        g0 = tau * TAU + kc * CH
        aw_dve = m.val(m.s_dve)
        pre_act = m.val(m.s_act)

        @m.block.scalar
        def _(sc):
            sc.wait_ge(m.s_dve, aw_dve)
            sc.activation(m.CONVACC[0:32, 1, 0:CH], m.a_park[:, g0:g0 + CH],
                          AF.Exp, bias=m.a_park[:, g0 + CH - 1:g0 + CH],
                          scale=-1.0)
            sc.drain()
            sc.activation(m.er32, m.a_park[:, g0 + CH - 1:g0 + CH], AF.Exp
                          ).then_inc(m.s_act, 1)
            m.inc(m.s_act, 1)
        scd_ready = m.val(m.s_act)

        @m.block.vector
        def _(v):
            v.wait_ge(m.s_act, scd_ready)
            v.tensor_tensor(m.CONVACC[0:32, 1, 0:CH],
                            m.CONVACC[0:32, 1, 0:CH],
                            m.dt_park[:, g0:g0 + CH], ALU.mult).then_inc(m.s_dve, 1)
            m.inc(m.s_dve, 1)
        scdm_ready = m.val(m.s_dve)

        af0 = m.val(m.s_af)
        er0 = m.val(m.s_er)

        @m.block.gpsimd
        def _(g):
            g.wait_ge(m.s_dve, aw_dve)
            g.wait_ge(m.s_ai, m.val(m.s_ai))
            g.dma_start(out=m.aflat_h.rearrange("a (h t) -> (a h) t", h=32),
                        in_=m.a_park[:, g0:g0 + CH]).then_inc(m.s_af, 16)
            m.inc(m.s_af, 16)
            g.wait_ge(m.s_act, scd_ready)
            g.dma_start(out=m.erow_h.rearrange("a h -> h a"), in_=m.er32
                        ).then_inc(m.s_er, 16)
            m.inc(m.s_er, 16)
            g.wait_ge(m.s_er, er0 + 16)
            g.dma_start(out=m.erow[:, 0:32],
                        in_=m.erow_h.partition_broadcast(64)
                        ).then_inc(m.s_er, 16)
            m.inc(m.s_er, 16)
        er_done = m.val(m.s_er)
        af_done = m.val(m.s_af)

        tp0 = m.val(m.s_tp)
        ev0 = m.val(m.s_dve)

        prev_xe = getattr(self, "_a2_prev_xevac", None)
        prev_sse = getattr(self, "_a2_prev_ssevac", None)
        prev_srun = getattr(self, "_a2_prev_srun", None)
        scd_act_mark = m.val(m.s_act)

        @m.block.tensor
        def _(t, prev_xe=prev_xe, prev_sse=prev_sse, prev_srun=prev_srun):
            t.wait_ge(m.s_act, pre_act)
            for grp in range(8):
                bank = 6 + grp % 2
                if grp >= 2:
                    t.wait_ge(m.s_dve, ev0 + (grp - 2) + 1)
                elif prev_xe is not None:
                    t.wait_ge(m.s_dve, prev_xe)
                last = None
                for s8 in range(8):
                    n = grp * 8 + s8
                    jt, ch = n // 32, n % 32
                    pd = self.pslot16(bank, s8)
                    last = t.transpose(
                        pd, m.B1[:, ch, cc0 + jt * 128:cc0 + jt * 128 + 128],
                        m.ident)
                last.then_inc(m.s_tp, 1)
                m.inc(m.s_tp, 1)
            if prev_sse is not None:
                t.wait_ge(m.s_dve, prev_sse)
            if prev_srun is not None:
                t.wait_ge(m.s_dve, prev_srun)
            t.wait_ge(m.s_dve, scdm_ready)
            last = None
            for jt in range(2):
                pd = self.pslot16(5, jt)[:, 0:64]
                last = t.transpose(pd, m.B1[0:64, 32, cc0 + jt * 128:
                                            cc0 + jt * 128 + 128],
                                   m.ident[0:64, 0:64])
            for which in range(3):
                for jt in range(2):
                    srcs = [m.a_park[:, g0 + jt * 128:g0 + jt * 128 + 128],
                            m.dt_park[:, g0 + jt * 128:g0 + jt * 128 + 128],
                            m.CONVACC[0:32, 1, jt * 128:jt * 128 + 128]]
                    pd = self.pf32(5, 192 + (which * 2 + jt) * 32, 32)
                    last = t.transpose(pd, srcs[which],
                                       m.ident32[0:32, 0:32])
            last.then_inc(m.s_tp, 1)
            m.inc(m.s_tp, 1)
            t.matmul(self.pf32(4, 0, 256), m.B1[0:64, 32, cc0:cc0 + 128],
                     m.B1[0:64, 33, cc0:cc0 + 256], start=True, stop=True)
            t.matmul(self.pf32(4, 256, 128),
                     m.B1[0:64, 32, cc0 + 128:cc0 + 256],
                     m.B1[0:64, 33, cc0 + 128:cc0 + 256],
                     start=True, stop=True).then_inc(m.s_tp, 1)
            m.inc(m.s_tp, 1)
        tp_small = tp0 + 9
        tp_sc = tp0 + 10

        @m.block.vector
        def _(v):
            for grp in range(8):
                bank = 6 + grp % 2
                v.wait_ge(m.s_tp, tp0 + grp + 1)
                v.tensor_copy(m.XTM[:, grp * 1024:grp * 1024 + 1024],
                              self.pslot16(bank, 0, 1024)
                              ).then_inc(m.s_dve, 1)
                m.inc(m.s_dve, 1)
        self._a2_prev_xevac = m.val(m.s_dve)

        @m.block.vector
        def _(v):
            v.wait_ge(m.s_tp, tp_small)
            for jt in range(2):
                v.tensor_copy(m.TT32.bitcast(F16)[:, 384 + 64 * jt:
                                                  448 + 64 * jt],
                              self.pslot16(5, jt)[:, 0:64])
            for which in range(3):
                for jt in range(2):
                    k = which * 2 + jt
                    i = v.tensor_copy(m.TT32[:, k * 32:k * 32 + 32],
                                      self.pf32(5, 192 + k * 32, 32))
            i.then_inc(m.s_dve, 1)
            m.inc(m.s_dve, 1)
            v.wait_ge(m.s_tp, tp_sc)
            v.tensor_tensor(m.SSB[:, 0:128], self.pf32(4, 0, 128), m.maskT,
                            ALU.mult)
            v.tensor_copy(m.SSB[:, 128:256], self.pf32(4, 128, 128))
            i = v.tensor_tensor(m.SSB[:, 256:384], self.pf32(4, 256, 128),
                                m.maskT, ALU.mult)
            i.then_inc(m.s_dve, 1)
            m.inc(m.s_dve, 1)
        self._a2_prev_ssevac = m.val(m.s_dve)
        ev_all = m.val(m.s_dve)

        @m.block.vector
        def _(v):
            v.wait_ge(m.s_gp, 1)
            v.tensor_copy(m.prevs, m.s_run).then_inc(m.s_dve, 1)
            m.inc(m.s_dve, 1)
            v.wait_ge(m.s_er, er_done)
            v.drain()
            v.tensor_tensor(
                m.s_run.rearrange("n (h p) -> n h p", h=32),
                m.s_run.rearrange("n (h p) -> n h p", h=32),
                m.erow[:, 0:32].unsqueeze(2).broadcast_to([64, 32, 128]),
                ALU.mult).then_inc(m.s_dve, 1)
            m.inc(m.s_dve, 1)
        prevs_done = m.val(m.s_dve) - 1
        scale_done = m.val(m.s_dve)

        for jt in range(2):
            pe_snap = m.val(m.s_pe)

            @m.block.vector
            def _(v, pe_snap=pe_snap, jt=jt):
                v.wait_ge(m.s_pe, pe_snap)
                v.tensor_tensor(
                    m.SCL[:, 0:4096].rearrange("p (h q) -> p h q", h=32),
                    m.XTM[:, jt * 4096:jt * 4096 + 4096].rearrange(
                        "p (h q) -> p h q", h=32),
                    m.TT32[:, 128 + jt * 32:160 + jt * 32].unsqueeze(2
                        ).broadcast_to([128, 32, 128]),
                    ALU.mult).then_inc(m.s_dve, 1)
                m.inc(m.s_dve, 1)
            xwd_done = m.val(m.s_dve)

            @m.block.tensor
            def _(t, jt=jt, xwd_done=xwd_done):
                t.wait_ge(m.s_dve, xwd_done)
                for p in range(8):
                    if p >= 2:
                        t.wait_ge(m.s_dve, xwd_done + (p - 2) + 1)
                    t.matmul(self.pf32(4 + p % 2, 0, 512, p=64),
                             m.TT32.bitcast(F16)[:, 384 + 64 * jt:448 + 64 * jt],
                             m.SCL[:, 512 * p:512 * p + 512],
                             start=True, stop=True).then_inc(m.s_pe, 1)
                    m.inc(m.s_pe, 1)
            st_pe = m.val(m.s_pe)

            @m.block.vector
            def _(v, jt=jt, st_pe=st_pe):
                for p in range(8):
                    v.wait_ge(m.s_pe, st_pe - 8 + p + 1)
                    if jt == 0 and p < 2:
                        v.wait_ge(m.s_dve, scale_done)
                    if p == 0:
                        v.drain()
                    v.tensor_tensor(m.s_run[:, 512 * p:512 * p + 512],
                                    m.s_run[:, 512 * p:512 * p + 512],
                                    self.pf32(4 + p % 2, 0, 512, p=64),
                                    ALU.add).then_inc(m.s_dve, 1)
                    m.inc(m.s_dve, 1)

        self._a2_prev_srun = m.val(m.s_dve)

        for hg in range(2):
            ai0 = m.val(m.s_ai)

            @m.block.gpsimd
            def _(g, hg=hg, ai0=ai0):
                g.wait_ge(m.s_af, af_done)
                g.wait_ge(m.s_act, m.val(m.s_act))
                g.dma_start(
                    out=m.A_I[:, 0:4096],
                    in_=m.aflat_h[:, hg * 4096:hg * 4096 + 4096]
                    .partition_broadcast(128)).then_inc(m.s_ai, 16)
                m.inc(m.s_ai, 16)
                g.dma_start(
                    out=m.A_I[:, 4096:6144],
                    in_=m.aflat_h.rearrange("a (h t) -> a h t", h=32)
                    [:, hg * 16:hg * 16 + 16, 128:256]
                    .partition_broadcast(128)).then_inc(m.s_ai, 16)
                m.inc(m.s_ai, 16)
            ai_done = m.val(m.s_ai)

            pe_prev = m.val(m.s_pe)

            @m.block.scalar
            def _(sc, hg=hg, ai_done=ai_done, pe_prev=pe_prev):
                sc.wait_ge(m.s_ai, ai_done)
                sc.wait_ge(m.s_pe, pe_prev)
                sc.activation(m.EA16[:, 0:4096], m.A_I[0:64, 0:4096], AF.Exp
                              ).then_inc(m.s_act, 1)
                m.inc(m.s_act, 1)
            ea_done = m.val(m.s_act)

            @m.block.vector
            def _(v, hg=hg, ai_done=ai_done, ea_done=ea_done):
                v.wait_ge(m.s_ai, ai_done)
                v.wait_ge(m.s_act, ea_done)
                v.wait_ge(m.s_dve, ev_all * 0 + ev_all)
                last = None
                for h in range(16):
                    last = v.tensor_scalar(
                        m.A_I[:, h * 256:h * 256 + 256],
                        m.A_I[:, h * 256:h * 256 + 256],
                        m.TT32[:, hg * 16 + h:hg * 16 + h + 1], 0.0,
                        ALU.subtract, ALU.min)
                last.then_inc(m.s_dve, 1)
                m.inc(m.s_dve, 1)
                last = None
                for h in range(16):
                    last = v.tensor_scalar(
                        m.A_I[:, 4096 + h * 128:4096 + h * 128 + 128],
                        m.A_I[:, 4096 + h * 128:4096 + h * 128 + 128],
                        m.TT32[:, 32 + hg * 16 + h:32 + hg * 16 + h + 1], 0.0,
                        ALU.subtract, ALU.min)
                last.then_inc(m.s_dve, 1)
                m.inc(m.s_dve, 1)
            seg_done = m.val(m.s_dve)

            @m.block.scalar
            def _(sc, seg_done=seg_done):
                sc.wait_ge(m.s_dve, seg_done)
                sc.activation(m.W16[:, 0:4096], m.A_I[:, 0:4096], AF.Exp)
                sc.drain()
                sc.activation(m.W16[:, 4096:6144], m.A_I[:, 4096:6144], AF.Exp
                              ).then_inc(m.s_act, 1)
                m.inc(m.s_act, 1)
            wexp_done = m.val(m.s_act)

            @m.block.vector
            def _(v, hg=hg, wexp_done=wexp_done, pe_prev=pe_prev):
                v.wait_ge(m.s_act, wexp_done)
                v.wait_ge(m.s_pe, pe_prev)
                v.tensor_tensor(
                    m.W16[:, 0:4096].rearrange("p (h q) -> p h q", h=16),
                    m.W16[:, 0:4096].rearrange("p (h q) -> p h q", h=16),
                    m.SSB[:, 0:256].unsqueeze(1).broadcast_to([128, 16, 256]),
                    ALU.mult)
                v.drain()
                v.tensor_tensor(
                    m.W16[:, 4096:6144].rearrange("p (h q) -> p h q", h=16),
                    m.W16[:, 4096:6144].rearrange("p (h q) -> p h q", h=16),
                    m.SSB[:, 256:384].unsqueeze(1).broadcast_to([128, 16, 128]),
                    ALU.mult)
                v.drain()
                v.tensor_tensor(
                    m.EA16[:, 0:4096].rearrange("p (h q) -> p h q", h=16),
                    m.EA16[:, 0:4096].rearrange("p (h q) -> p h q", h=16),
                    m.B1[0:64, 33, cc0:cc0 + 256].unsqueeze(1
                        ).broadcast_to([64, 16, 256]),
                    ALU.mult)
                v.drain()
                v.tensor_tensor(
                    m.SCL[:, 0:2048].rearrange("p (h q) -> p h q", h=16),
                    m.XTM[:, 0 * 4096 + hg * 2048:0 * 4096 + hg * 2048 + 2048
                          ].rearrange("p (h q) -> p h q", h=16),
                    m.TT32[:, 64 + hg * 16:64 + hg * 16 + 16].unsqueeze(2
                        ).broadcast_to([128, 16, 128]),
                    ALU.mult)
                v.drain()
                v.tensor_tensor(
                    m.SCL[:, 2048:4096].rearrange("p (h q) -> p h q", h=16),
                    m.XTM[:, 1 * 4096 + hg * 2048:1 * 4096 + hg * 2048 + 2048
                          ].rearrange("p (h q) -> p h q", h=16),
                    m.TT32[:, 96 + hg * 16:96 + hg * 16 + 16].unsqueeze(2
                        ).broadcast_to([128, 16, 128]),
                    ALU.mult).then_inc(m.s_dve, 1)
                m.inc(m.s_dve, 1)
            wready = m.val(m.s_dve)

            pe_y0 = m.val(m.s_pe)

            dve_y0 = m.val(m.s_dve)

            @m.block.tensor
            def _(t, hg=hg, wready=wready, pe_y0=pe_y0, dve_y0=dve_y0):
                t.wait_ge(m.s_dve, wready)
                for l in range(16):
                    h = hg * 16 + l
                    if l >= 4:
                        t.wait_ge(m.s_dve, dve_y0 + (l - 4) + 1)
                    yap = self.pf32(l % 4, 0, 256)
                    t.matmul(yap, m.SCL[:, l * 128:l * 128 + 128],
                             m.W16[:, l * 256:l * 256 + 256],
                             start=True, stop=False)
                    t.matmul(yap[:, 128:256],
                             m.SCL[:, 2048 + l * 128:2048 + l * 128 + 128],
                             m.W16[:, 4096 + l * 128:4096 + l * 128 + 128],
                             start=False, stop=False)
                    t.matmul(yap, m.prevs[:, h * 128:h * 128 + 128],
                             m.EA16[:, l * 256:l * 256 + 256],
                             start=False, stop=True).then_inc(m.s_pe, 1)
                    m.inc(m.s_pe, 1)

            @m.block.vector
            def _(v, hg=hg, pe_y0=pe_y0):
                for l in range(16):
                    h = hg * 16 + l
                    v.wait_ge(m.s_pe, pe_y0 + l + 1)
                    yap = self.pf32(l % 4, 0, 256)
                    v.scalar_tensor_tensor(
                        m.B1[:, h, cc0:cc0 + 256], m.B1[:, h, cc0:cc0 + 256],
                        m.dbc[:, h:h + 1], yap, ALU.mult, ALU.add
                        ).then_inc(m.s_dve, 1)
                    m.inc(m.s_dve, 1)

    def _a2_tau_end(self, tau):
        m = self
        hof = HALO if tau == 0 else 0
        dve_all = m.val(m.s_dve)
        act_all = m.val(m.s_act)

        @m.block.vector
        def _(v):
            v.wait_ge(m.s_dve, dve_all)
            v.tensor_copy(m.c_park[:, tau * TAU:tau * TAU + TAU],
                          m.B1[0:64, 33, hof:hof + TAU]).then_inc(m.s_dve, 1)
            m.inc(m.s_dve, 1)

        @m.block.sync
        def _(s):
            s.wait_ge(m.s_dve, m.val(m.s_dve))
            s.dma_start(out=m.ypark_h[:, :, tau * TAU:tau * TAU + TAU],
                        in_=m.B1[:, 0:32, hof:hof + TAU]).then_inc(m.s_yd, 16)
            m.inc(m.s_yd, 16)

    def _gather(self):
        m = self
        dve_all = m.val(m.s_dve)
        act_all = m.val(m.s_act)

        @m.block.vector
        def _(v):
            v.wait_ge(m.s_dve, dve_all)
            v.wait_ge(m.s_act, act_all)
            v.reduce_sum(m.atot, m.a_park[:, CH - 1:T:CH], mybir.AxisListType.X)
            v.drain()
            v.tensor_tensor_scan(m.acum[:, 1:8], m.oscan[:, 0:7],
                                 m.a_park[:, CH - 1:T - CH:CH], 0.0,
                                 ALU.mult, ALU.add).then_inc(m.s_dve, 1)
            m.inc(m.s_dve, 1)
        red_done = m.val(m.s_dve)

        @m.block.gpsimd
        def _(g):
            g.memset(m.acum[:, 0:1], 0.0)
            g.wait_ge(m.s_dve, red_done)
            g.memset(m.s_in, 0.0).then_inc(m.s_gp, 1)
            m.inc(m.s_gp, 1)

        @m.block.sync
        def _(s):
            s.wait_ge(m.s_dve, red_done)
            s.dma_start(out=m.agin_h[0:64 * DIN].rearrange("(n q) -> n q", n=64),
                        in_=m.s_run).then_inc(m.s_cc, 16)
            m.inc(m.s_cc, 16)
            s.dma_start(out=m.agin_h[64 * DIN:64 * DIN + 32
                                     ].rearrange("(h a) -> h a", a=1),
                        in_=m.atot).then_inc(m.s_cc, 16)
            m.inc(m.s_cc, 16)
        ag_in_done = m.val(m.s_cc)

        @m.block.gpsimd
        def _(g):
            g.wait_ge(m.s_cc, ag_in_done)
            g.collective_compute(
                "AllGather", ALU.bypass,
                replica_groups=[list(range(NCORES))],
                ins=[m.agin_h],
                outs=[m.agout_h],
            ).then_inc(m.s_gp, 1)
            m.inc(m.s_gp, 1)
        cc_done = m.val(m.s_gp)

        @m.block.sync
        def _(s):
            s.wait_ge(m.s_gp, cc_done)
            s.dma_start(out=m.atall,
                        in_=m.agout_h[:, 64 * DIN:64 * DIN + 32]
                        .rearrange("j h -> h j"),
                        allow_slow_non_contiguous=True).then_inc(m.s_cc, 16)
            m.inc(m.s_cc, 16)
        at_done = m.val(m.s_cc)

        for j in range(NCORES):
            er0 = m.val(m.s_er)

            @m.block.scalar
            def _(sc, j=j, at_done=at_done):
                if j == 0:
                    sc.wait_ge(m.s_cc, at_done)
                sc.wait_ge(m.s_er, m.val(m.s_er))
                sc.activation(m.er32, m.atall[:, j:j + 1], AF.Exp,
                              scale=m.gbc[0:32, 8 + j:9 + j]
                              ).then_inc(m.s_act, 1)
                m.inc(m.s_act, 1)
            e_done = m.val(m.s_act)

            @m.block.gpsimd
            def _(g, j=j, e_done=e_done, er0=er0, dve_prev=m.val(m.s_dve)):
                g.wait_ge(m.s_act, e_done)
                g.wait_ge(m.s_dve, dve_prev)
                g.dma_start(out=m.erow_h.rearrange("a h -> h a"), in_=m.er32
                            ).then_inc(m.s_er, 16)
                m.inc(m.s_er, 16)
                g.wait_ge(m.s_er, er0 + 16)
                g.dma_start(out=m.erow[:, 0:32],
                            in_=m.erow_h.partition_broadcast(64)
                            ).then_inc(m.s_er, 16)
                m.inc(m.s_er, 16)
                g.wait_ge(m.s_dve, m.val(m.s_dve))
                g.dma_start(out=m.AB64,
                            in_=m.agout_h[j, 0:64 * DIN]
                            .rearrange("(n q) -> n q", n=64)
                            ).then_inc(m.s_ai, 16)
                m.inc(m.s_ai, 16)
            er_done = m.val(m.s_er)
            sag_done = m.val(m.s_ai)

            @m.block.vector
            def _(v, j=j, er_done=er_done, sag_done=sag_done):
                if j == 0:
                    v.wait_ge(m.s_gp, m.val(m.s_gp))
                v.wait_ge(m.s_er, er_done)
                v.wait_ge(m.s_ai, sag_done)
                v.drain()
                v.tensor_tensor(
                    m.s_in.rearrange("n (h p) -> n h p", h=32),
                    m.s_in.rearrange("n (h p) -> n h p", h=32),
                    m.erow[:, 0:32].unsqueeze(2).broadcast_to([64, 32, 128]),
                    ALU.mult)
                v.drain()
                v.scalar_tensor_tensor(
                    m.s_in, m.AB64, m.gbc[:, j:j + 1], m.s_in,
                    ALU.mult, ALU.add).then_inc(m.s_dve, 1)
                m.inc(m.s_dve, 1)

    def _gather_corrprep(self):
        m = self
        red = m.val(m.s_dve)
        gp0 = m.val(m.s_gp)

        @m.block.sync
        def _(s):
            s.wait_ge(m.s_dve, red)
            s.wait_ge(m.s_gp, gp0)
            s.dma_start(out=m.aflat8_h.rearrange("g (h t) -> h g t", h=32),
                        in_=m.a_park.rearrange("h (g t) -> h g t", g=8)
                        ).then_inc(m.s_yd, 16)
            m.inc(m.s_yd, 16)
        e8_dve = m.val(m.s_dve)

        @m.block.scalar
        def _(sc):
            sc.wait_ge(m.s_dve, e8_dve)
            sc.wait_ge(m.s_gp, gp0)
            sc.drain()
            sc.activation(m.erow8sb, m.acum, AF.Exp).then_inc(m.s_act, 1)
            m.inc(m.s_act, 1)
        e8_done = m.val(m.s_act)
        er0 = m.val(m.s_er)

        @m.block.gpsimd
        def _(g, e8_done=e8_done, er0=er0):
            g.wait_ge(m.s_act, e8_done)
            g.dma_start(out=m.erow8_h.rearrange("g h -> h g"), in_=m.erow8sb,
                        allow_slow_non_contiguous=True).then_inc(m.s_er, 16)
            m.inc(m.s_er, 16)
            g.wait_ge(m.s_er, er0 + 16)
            g.dma_start(out=m.erow8,
                        in_=m.erow8_h.rearrange("g h -> (g h)").unsqueeze(0)
                        .partition_broadcast(64)).then_inc(m.s_er, 16)
            m.inc(m.s_er, 16)
        self.erow8_done = m.val(m.s_er)

    def _a3(self, tau):
        m = self
        pre_dve = m.val(m.s_dve)
        pre_yd = m.val(m.s_yd)

        @m.block.sync
        def _(s):
            s.wait_ge(m.s_dve, pre_dve)
            s.wait_ge(m.s_act, m.val(m.s_act))
            s.wait_ge(m.s_pe, m.val(m.s_pe))
            s.wait_ge(m.s_yd, pre_yd)
            s.dma_start(out=m.B1[:, 0:32, 0:TAU],
                        in_=m.ypark_h[:, :, tau * TAU:tau * TAU + TAU]
                        ).then_inc(m.s_yd, 16)
            m.inc(m.s_yd, 16)
        yl_done = m.val(m.s_yd)

        corr_pe = {}
        ea_act = {}
        sink_dve = {}
        for kc in range(4):
            g = tau * 4 + kc
            pe_prev_kc = m.val(m.s_pe)

            @m.block.vector
            def _(v, kc=kc, g=g, pe_prev_kc=pe_prev_kc):
                if tau == 0 and kc == 0:
                    v.wait_ge(m.s_er, self.erow8_done)
                v.wait_ge(m.s_pe, pe_prev_kc)
                v.drain()
                v.tensor_tensor(
                    m.SINK16.rearrange("n (h p) -> n h p", h=32),
                    m.s_in.rearrange("n (h p) -> n h p", h=32),
                    m.erow8[:, g * 32:g * 32 + 32].unsqueeze(2
                        ).broadcast_to([64, 32, 128]),
                    ALU.mult).then_inc(m.s_dve, 1)
                m.inc(m.s_dve, 1)
            sink_dve[kc] = m.val(m.s_dve)

            for hg in range(2):
                slot = (kc * 2 + hg) % 2
                use = kc * 2 + hg

                @m.block.gpsimd
                def _(g_, g=g, hg=hg, slot=slot, use=use,
                      st_snap=m.val(m.s_st), yd_snap=m.val(m.s_yd)):
                    if tau == 1 and use == 0:
                        g_.wait_ge(m.s_st, st_snap)
                    if use >= 2:
                        g_.wait_ge(m.s_act, ea_act[use - 2])
                    elif tau == 0 and use < 2:
                        g_.wait_ge(m.s_yd, yd_snap)
                    ais = m.s_ai if slot == 0 else m.s_ai2
                    g_.dma_start(
                        out=m.AB64s[slot],
                        in_=m.aflat8_h[g:g + 1, hg * 4096:hg * 4096 + 4096]
                        .partition_broadcast(64)).then_inc(ais, 16)
                    m.inc(ais, 16)
                ai_done = m.val(m.s_ai if slot == 0 else m.s_ai2)
                pe_prev = corr_pe.get(use - 2)

                @m.block.scalar
                def _(sc, slot=slot, ai_done=ai_done, pe_prev=pe_prev):
                    sc.wait_ge(m.s_ai if slot == 0 else m.s_ai2, ai_done)
                    if pe_prev is not None:
                        sc.wait_ge(m.s_pe, pe_prev)
                    sc.drain()
                    sc.activation(m.EA16s[slot], m.AB64s[slot], AF.Exp
                                  ).then_inc(m.s_act, 1)
                    m.inc(m.s_act, 1)
                ea_act[use] = m.val(m.s_act)

                @m.block.vector
                def _(v, g=g, slot=slot, ea_done=m.val(m.s_act)):
                    v.wait_ge(m.s_act, ea_done)
                    v.drain()
                    v.tensor_tensor(
                        m.EA16s[slot].rearrange("n (h p) -> n h p", h=16),
                        m.EA16s[slot].rearrange("n (h p) -> n h p", h=16),
                        m.c_park[:, g * CH:g * CH + CH].unsqueeze(1
                            ).broadcast_to([64, 16, 256]),
                        ALU.mult).then_inc(m.s_dve, 1)
                    m.inc(m.s_dve, 1)
                cp_done = m.val(m.s_dve)
                pe_c0 = m.val(m.s_pe)
                dve_c0 = m.val(m.s_dve)

                @m.block.tensor
                def _(t, hg=hg, slot=slot, kc=kc, cp_done=cp_done,
                      pe_c0=pe_c0, dve_c0=dve_c0):
                    t.wait_ge(m.s_dve, cp_done)
                    t.wait_ge(m.s_dve, sink_dve[kc])
                    t.wait_ge(m.s_yd, yl_done)
                    for l in range(16):
                        h = hg * 16 + l
                        if l >= 8:
                            t.wait_ge(m.s_dve, dve_c0 + (l - 8) + 1)
                        yap = self.pf32(l % 8, 0, 256)
                        t.matmul(yap, m.SINK16[:, h * 128:h * 128 + 128],
                                 m.EA16s[slot][:, l * 256:l * 256 + 256],
                                 start=True, stop=True).then_inc(m.s_pe, 1)
                        m.inc(m.s_pe, 1)
                corr_pe[use] = m.val(m.s_pe)

                @m.block.vector
                def _(v, hg=hg, pe_c0=pe_c0, kc=kc):
                    for l in range(16):
                        h = hg * 16 + l
                        v.wait_ge(m.s_pe, pe_c0 + l + 1)
                        yap = self.pf32(l % 8, 0, 256)
                        cc = kc * CH
                        v.tensor_tensor(m.B1[:, h, cc:cc + 256],
                                        m.B1[:, h, cc:cc + 256], yap,
                                        ALU.add).then_inc(m.s_dve, 1)
                        m.inc(m.s_dve, 1)

        TW = TAU
        u0 = (0 if tau == 0 else TAU + HALO) + (HALO if tau == 0 else 0)
        corr_dve = m.val(m.s_dve)

        @m.block.gpsimd
        def _(g):
            g.wait_ge(m.s_dve, corr_dve)
            g.wait_ge(m.s_act, m.val(m.s_act))
            g.dma_start(
                out=m.AR_U[:, 0:16 * TW].rearrange("p (k t) -> p k t", t=TW),
                in_=m.u_h[:, u0:u0 + TW].rearrange("(k p) t -> p k t", p=128),
            ).then_inc(m.s_wu, 16)
            m.inc(m.s_wu, 16)
        uz_done = m.val(m.s_wu)

        if tau == 0:
            @m.block.gpsimd
            def _(g):
                g.wait_ge(m.s_dve, corr_dve)
                g.dma_start(out=m.ocols, in_=m.ocols_h.rearrange(
                    "p (h j) -> p h j", j=32)).then_inc(m.s_wdma, 16)
                m.inc(m.s_wdma, 16)
        ocols_done = m.val(m.s_wdma)

        sl = [m.s_sl0, m.s_sl1]
        slbase = [m.val(m.s_sl0), m.val(m.s_sl1)]
        zmm_inc = {}
        sig_inc = {}
        sq_inc = {}
        sqm_inc = {}
        yg_inc = {}

        for h in range(32):
            @m.block.gpsimd
            def _(g, h=h):
                if h >= 2:
                    g.wait_ge(m.s_pe, zmm_inc[h - 2])
                g.dma_start(out=m.WBUF[:, h % 2], in_=m.wZ_h[h]
                            ).then_inc(sl[h % 2], 16)
                m.inc(sl[h % 2], 16)

            @m.block.tensor
            def _(t, h=h):
                if h == 0:
                    t.wait_ge(m.s_wu, uz_done)
                t.wait_ge(sl[h % 2], slbase[h % 2] + 16 * (h // 2 + 1))
                if h >= 2:
                    t.wait_ge(m.s_act, sig_inc[h - 2])
                last = None
                for tt in range(2):
                    for k in range(16):
                        last = t.matmul(
                            self.pf32(2 * (h % 2) + tt, 0, 512),
                            m.WBUF[:, h % 2, k],
                            m.AR_U[:, 0:16 * TW].rearrange(
                                "p (k t) -> p k t", t=TW)[:, k,
                                tt * 512:tt * 512 + 512],
                            start=(k == 0), stop=(k == 15))
                last.then_inc(m.s_pe, 1)
                m.inc(m.s_pe, 1)
                zmm_inc[h] = m.val(m.s_pe)
                if h >= 1:
                    if h == 1:
                        t.wait_ge(m.s_wdma, ocols_done)
                    t.wait_ge(m.s_dve, sq_inc[h - 1])
                    last = None
                    for tt in range(2):
                        last = t.matmul(
                            self.pf32(6 + tt, 0, 512, p=32),
                            m.ocols[:, h - 1],
                            m.SQ16[:, ((h - 1) % 2) * 1024 + tt * 512:
                                   ((h - 1) % 2) * 1024 + tt * 512 + 512],
                            start=(h - 1 == 0), stop=(h - 1 == 31))
                    last.then_inc(m.s_pe, 1)
                    m.inc(m.s_pe, 1)
                    sqm_inc[h - 1] = m.val(m.s_pe)

            zap = (self.PS_A[:, 0:1024] if h % 2 == 0
                   else self.PS_A[:, 1024:2048])
            dve_snap = m.val(m.s_dve)

            @m.block.scalar
            def _(sc, h=h, zap=zap, dve_snap=dve_snap):
                sc.wait_ge(m.s_pe, zmm_inc[h])
                if h >= 2:
                    sc.wait_ge(m.s_dve, yg_inc[h - 2])
                sc.drain()
                sc.activation(m.CONVACC[:, h % 2], zap, AF.Sigmoid
                              ).then_inc(m.s_act, 1)
                m.inc(m.s_act, 1)
            sig_inc[h] = m.val(m.s_act)

            @m.block.vector
            def _(v, h=h):
                v.wait_ge(m.s_act, sig_inc[h])
                v.drain()
                v.tensor_tensor(m.B1[:, h, 0:TAU], m.B1[:, h, 0:TAU],
                                m.CONVACC[:, h % 2], ALU.mult).then_inc(m.s_dve, 1)
                m.inc(m.s_dve, 1)
            yg_inc[h] = m.val(m.s_dve)

            @m.block.vector
            def _(v, h=h):
                if h >= 2:
                    v.wait_ge(m.s_pe, sqm_inc[h - 2])
                v.drain()
                v.tensor_tensor(m.SQ16[:, (h % 2) * 1024:(h % 2) * 1024 + 1024],
                                m.B1[:, h, 0:TAU], m.B1[:, h, 0:TAU],
                                ALU.mult).then_inc(m.s_dve, 1)
                m.inc(m.s_dve, 1)
            sq_inc[h] = m.val(m.s_dve)

        @m.block.tensor
        def _(t):
            t.wait_ge(m.s_dve, sq_inc[31])
            last = None
            for tt in range(2):
                last = t.matmul(
                    self.pf32(6 + tt, 0, 512, p=32),
                    m.ocols[:, 31],
                    m.SQ16[:, (31 % 2) * 1024 + tt * 512:
                           (31 % 2) * 1024 + tt * 512 + 512],
                    start=False, stop=True)
            last.then_inc(m.s_pe, 1)
            m.inc(m.s_pe, 1)
            sqm_inc[31] = m.val(m.s_pe)

        @m.block.vector
        def _(v, yg31=m.val(m.s_dve)):
            v.wait_ge(m.s_pe, sqm_inc[31])
            v.wait_ge(m.s_dve, yg31)
            v.drain()
            v.tensor_scalar(m.RST32[:, 0:512], self.pf32(6, 0, 512, p=32),
                            1.0 / 128, EPS_RMS, ALU.mult, ALU.add)
            v.drain()
            v.tensor_scalar(m.RST32[:, 512:1024], self.pf32(7, 0, 512, p=32),
                            1.0 / 128, EPS_RMS, ALU.mult, ALU.add).then_inc(m.s_dve, 1)
            m.inc(m.s_dve, 1)
        ms_done = m.val(m.s_dve)

        @m.block.scalar
        def _(sc, ms_done=ms_done):
            sc.wait_ge(m.s_dve, ms_done)
            sc.drain()
            sc.activation(m.RST32, m.RST32, AF.Sqrt).then_inc(m.s_act, 1)
            m.inc(m.s_act, 1)
        ars_done = m.val(m.s_act)

        @m.block.vector
        def _(v, ars_done=ars_done):
            v.wait_ge(m.s_act, ars_done)
            v.drain()
            v.reciprocal(m.RST32, m.RST32).then_inc(m.s_dve, 1)
            m.inc(m.s_dve, 1)
        rec_done = m.val(m.s_dve)

        @m.block.sync
        def _(s, rec_done=rec_done):
            s.wait_ge(m.s_dve, rec_done)
            s.dma_start(out=m.rstdh_h, in_=m.RST32).then_inc(m.s_yd, 16)
            m.inc(m.s_yd, 16)
        rsth_done = m.val(m.s_yd)

        yn_inc = {}
        ers = [m.s_er, m.s_er2]
        for h in range(32):
            @m.block.gpsimd
            def _(g, h=h):
                if h == 0:
                    g.wait_ge(m.s_yd, rsth_done)
                if h >= 2:
                    g.wait_ge(m.s_dve, yn_inc[h - 2])
                g.dma_start(out=m.MRB2[:, h % 2],
                            in_=m.rstdh_h[h:h + 1, :].partition_broadcast(128)
                            ).then_inc(ers[h % 2], 16)
                m.inc(ers[h % 2], 16)
            bc_done = m.val(ers[h % 2])

            @m.block.vector
            def _(v, h=h, bc_done=bc_done):
                v.wait_ge(ers[h % 2], bc_done)
                v.drain()
                v.scalar_tensor_tensor(
                    m.B1[:, h, 0:TAU], m.B1[:, h, 0:TAU], m.nw[:, h:h + 1],
                    m.MRB2[:, h % 2], ALU.mult, ALU.mult
                    ).then_inc(m.s_dve, 1)
                m.inc(m.s_dve, 1)
            yn_inc[h] = m.val(m.s_dve)

    def _a4(self, tau):
        m = self
        pre_dve = m.val(m.s_dve)
        sl = [m.s_sl0, m.s_sl1]
        slbase = [m.val(m.s_sl0), m.val(m.s_sl1)]
        pe0 = m.val(m.s_pe)
        dve0 = m.val(m.s_dve)
        u0 = (0 if tau == 0 else TAU + HALO) + (HALO if tau == 0 else 0)

        @m.block.gpsimd
        def _(g):
            g.wait_ge(m.s_dve, pre_dve)
            for e in range(16):
                if e >= 2:
                    g.wait_ge(m.s_pe, pe0 + 2 * (e - 1))
                g.dma_start(out=m.WO_BUF[:, e % 2], in_=m.wO_h[e]
                            ).then_inc(sl[e % 2], 16)
                m.inc(sl[e % 2], 16)

        for e in range(16):
            for tt in range(2):
                gidx = e * 2 + tt

                @m.block.sync
                def _(s, e=e, tt=tt, gidx=gidx):
                    if gidx >= 2:
                        s.wait_ge(m.s_dve, dve0 + (gidx - 2) + 1)
                    else:
                        s.wait_ge(m.s_dve, pre_dve)
                    us = m.s_ur if gidx % 2 == 0 else m.s_ur1
                    s.dma_start(
                        out=m.URES[:, gidx % 2],
                        in_=m.u_h[e * 128:e * 128 + 128,
                                  u0 + tt * 512:u0 + tt * 512 + 512]
                        ).then_inc(us, 16)
                    m.inc(us, 16)
                ures_done = m.val(m.s_ur if gidx % 2 == 0 else m.s_ur1)

                @m.block.tensor
                def _(t, e=e, tt=tt, gidx=gidx, ures_done=ures_done):
                    t.wait_ge(sl[e % 2], slbase[e % 2] + 16 * (e // 2 + 1))
                    if gidx >= 4:
                        t.wait_ge(m.s_dve, dve0 + (gidx - 4) + 1)
                    last = None
                    for k in range(32):
                        last = t.matmul(self.pf32(gidx % 4, 0, 512),
                                        m.WO_BUF[:, e % 2, k],
                                        m.B1[:, k, tt * 512:tt * 512 + 512],
                                        start=(k == 0), stop=(k == 31))
                    last.then_inc(m.s_pe, 1)
                    m.inc(m.s_pe, 1)
                mm_done = m.val(m.s_pe)

                @m.block.vector
                def _(v, e=e, tt=tt, gidx=gidx, mm_done=mm_done,
                      ures_done=ures_done):
                    v.wait_ge(m.s_pe, mm_done)
                    v.wait_ge(m.s_ur if gidx % 2 == 0 else m.s_ur1, ures_done)
                    v.drain()
                    v.tensor_tensor(
                        m.X1PB[:, e, tt * 512:tt * 512 + 512],
                        self.pf32(gidx % 4, 0, 512),
                        m.URES[:, gidx % 2], ALU.add).then_inc(m.s_dve, 1)
                    m.inc(m.s_dve, 1)

        evac_all = m.val(m.s_dve)

        sq0 = m.val(m.s_dve)
        pe_before = m.val(m.s_pe)

        @m.block.vector
        def _(v, evac_all=evac_all, pe_before=pe_before):
            v.wait_ge(m.s_dve, evac_all)
            v.wait_ge(m.s_pe, pe_before)
            n = 0
            for tt in range(2):
                for e in range(16):
                    if n > 0:
                        v.wait_ge(m.s_pe, pe_before + n)
                    v.drain()
                    v.tensor_tensor(m.SQF2[:, n % 2],
                                    m.X1PB[:, e, tt * 512:tt * 512 + 512],
                                    m.X1PB[:, e, tt * 512:tt * 512 + 512],
                                    ALU.mult).then_inc(m.s_dve, 1)
                    m.inc(m.s_dve, 1)
                    n += 1

        pe_ln0 = m.val(m.s_pe)

        @m.block.tensor
        def _(t, evac_all=evac_all, sq0=sq0):
            t.wait_ge(m.s_dve, evac_all)
            n = 0
            for tt in range(2):
                for e in range(16):
                    t.matmul(self.pf32(4 + tt, 0, 512)[0:1, :], m.o128x1,
                             m.X1PB[:, e, tt * 512:tt * 512 + 512],
                             start=(e == 0), stop=(e == 15))
                    t.wait_ge(m.s_dve, sq0 + n + 1)
                    t.matmul(self.pf32(6 + tt, 0, 512)[0:1, :], m.o128x1,
                             m.SQF2[:, n % 2],
                             start=(e == 0), stop=(e == 15)
                             ).then_inc(m.s_pe, 1)
                    m.inc(m.s_pe, 1)
                    n += 1

        ln_pe = m.val(m.s_pe)
        for tt in range(2):
            @m.block.vector
            def _(v, tt=tt):
                v.wait_ge(m.s_pe, ln_pe)
                v.wait_ge(m.s_er, m.val(m.s_er))
                v.drain()
                v.tensor_scalar_mul(m.MROW[:, 0:512],
                                    self.pf32(4 + tt, 0, 512, p=1), 1.0 / D)
                v.drain()
                v.tensor_scalar_mul(m.RROW[:, 0:512],
                                    self.pf32(6 + tt, 0, 512, p=1), 1.0 / D)
                v.drain()
                v.tensor_tensor(m.SGM[0:1, 0:512], m.MROW[:, 0:512],
                                m.MROW[:, 0:512], ALU.mult)
                v.drain()
                v.tensor_tensor(m.RROW[:, 0:512], m.RROW[:, 0:512],
                                m.SGM[0:1, 0:512], ALU.subtract)
                v.drain()
                v.tensor_scalar_add(m.RROW[:, 0:512], m.RROW[:, 0:512],
                                    EPS_LN).then_inc(m.s_dve, 1)
                m.inc(m.s_dve, 1)
            var_done = m.val(m.s_dve)

            @m.block.scalar
            def _(sc, var_done=var_done):
                sc.wait_ge(m.s_dve, var_done)
                sc.drain()
                sc.activation(m.RROW[:, 0:512], m.RROW[:, 0:512], AF.Sqrt).then_inc(m.s_act, 1)
                m.inc(m.s_act, 1)
            st_done = m.val(m.s_act)

            @m.block.vector
            def _(v, st_done=st_done):
                v.wait_ge(m.s_act, st_done)
                v.drain()
                v.reciprocal(m.RROW[:, 0:512], m.RROW[:, 0:512]).then_inc(m.s_dve, 1)
                m.inc(m.s_dve, 1)
            rr_done = m.val(m.s_dve)
            er0 = m.val(m.s_er)

            @m.block.gpsimd
            def _(g, rr_done=rr_done, er0=er0):
                g.wait_ge(m.s_dve, rr_done)
                g.dma_start(out=m.rst_h[:, 0:512], in_=m.RROW[:, 0:512]
                            ).then_inc(m.s_er, 16)
                m.inc(m.s_er, 16)
                g.dma_start(out=m.rst_h[:, 512:1024], in_=m.MROW[:, 0:512]
                            ).then_inc(m.s_er, 16)
                m.inc(m.s_er, 16)
                g.wait_ge(m.s_er, er0 + 32)
                g.dma_start(out=m.MRB[:, 0], in_=m.rst_h[:, 0:512]
                            .partition_broadcast(128)).then_inc(m.s_er, 16)
                m.inc(m.s_er, 16)
                g.dma_start(out=m.MRB[:, 1], in_=m.rst_h[:, 512:1024]
                            .partition_broadcast(128)).then_inc(m.s_er, 16)
                m.inc(m.s_er, 16)
            mrb_done = m.val(m.s_er)

            @m.block.vector
            def _(v, tt=tt, mrb_done=mrb_done):
                v.wait_ge(m.s_er, mrb_done)
                for e in range(16):
                    v.drain()
                    v.tensor_tensor(m.X1PB[:, e, tt * 512:tt * 512 + 512],
                                    m.X1PB[:, e, tt * 512:tt * 512 + 512],
                                    m.MRB[:, 1], ALU.subtract)
                    v.drain()
                    i = v.tensor_tensor(
                        m.X1PB[:, e, tt * 512:tt * 512 + 512],
                        m.X1PB[:, e, tt * 512:tt * 512 + 512],
                        m.MRB[:, 0], ALU.mult)
                i.then_inc(m.s_dve, 1)
                m.inc(m.s_dve, 1)
            norm_done = m.val(m.s_dve)

        fin = m.val(m.s_dve)

        @m.block.sync
        def _(s, tau=tau, fin=fin):
            s.wait_ge(m.s_dve, fin)
            s.dma_start(out=m.x1park_h[:, :, tau * TAU:tau * TAU + TAU],
                        in_=m.X1PB).then_inc(m.s_st, 16)
            m.inc(m.s_st, 16)

    def _f1(self):
        m = self
        X1 = self.sb_at("X1F", [128, 16, T], F16, self.sballoc["B1"][0])
        m.X1V = X1
        WB4 = self.sb_at("WB4", [128, 4, 16, 128], F16,
                         self.sballoc["XTM"][0])
        GB = self.sb_at("GB2", [128, 2, 2048], F16, self.sballoc["CONVACC"][0])
        UB = self.sb_at("UB2", [128, 2, 2048], F16, self.sballoc["SCL"][0])
        HB8 = self.sb_at("HB8", [128, 8, 2048], F16, self.sballoc["dt_park"][0])
        AB4 = self.sb_at("AB4", [128, 4, 2048], F16, self.sballoc["W16"][0])
        F1S = self.sb_at("F1Sv", [128, 4, 2048], F16, self.sballoc["AR_U"][0])
        pre_pe = m.val(m.s_pe)
        pre_dve = m.val(m.s_dve)
        pre_act = m.val(m.s_act)
        pre_st = m.val(m.s_st)

        @m.block.sync
        def _(s):
            s.wait_ge(m.s_pe, pre_pe)
            s.wait_ge(m.s_dve, pre_dve)
            s.wait_ge(m.s_act, pre_act)
            s.wait_ge(m.s_st, pre_st)
            s.dma_start(out=X1, in_=m.x1park_h).then_inc(m.s_ur, 16)
            m.inc(m.s_ur, 16)
        x1_done = m.val(m.s_ur)

        sl = [m.s_sl0, m.s_sl1]
        silu_inc = {}
        ucp_inc = {}
        hb_inc = {}
        hb_inc_d = {}
        sp_inc = {}
        st_inc = {}

        def sparsify(g):
            s0 = (4 * g) % 8

            @m.block.vector
            def _(v, g=g, s0=s0):
                A0, A1_, A2_, A3_ = (AB4[:, i] for i in range(4))
                S1, T1, S2, T2 = (F1S[:, i] for i in range(4))
                for i in range(4):
                    v.drain()
                    v.tensor_scalar(AB4[:, i], HB8[:, s0 + i], -1.0, None,
                                    ALU.mult)
                    v.drain()
                    v.tensor_tensor(AB4[:, i], AB4[:, i], HB8[:, s0 + i],
                                    ALU.max)
                v.drain()
                v.tensor_tensor(S1, A0, A1_, ALU.max)
                v.tensor_tensor(T1, A0, A1_, ALU.min)
                v.tensor_tensor(S2, A2_, A3_, ALU.max)
                v.tensor_tensor(T2, A2_, A3_, ALU.min)
                v.drain()
                v.tensor_tensor(S1, S1, S2, ALU.min)
                v.tensor_tensor(T1, T1, T2, ALU.max)
                v.drain()
                v.tensor_tensor(S1, S1, T1, ALU.max)
                v.drain()
                for i in range(4):
                    v.tensor_tensor(AB4[:, i], AB4[:, i], S1, ALU.is_ge)
                v.drain()
                for i in range(4):
                    ii = v.tensor_tensor(HB8[:, s0 + i], HB8[:, s0 + i],
                                         AB4[:, i], ALU.mult)
                ii.then_inc(m.s_dve, 1)
                m.inc(m.s_dve, 1)
            sp_inc[g] = m.val(m.s_dve)

            @m.block.sync
            def _(s, g=g, s0=s0):
                s.wait_ge(m.s_dve, sp_inc[g])
                for i in range(4):
                    s.dma_start(out=m.hscr_h[4 * g + i], in_=HB8[:, s0 + i]
                                ).then_inc(m.s_st, 16)
                    m.inc(m.s_st, 16)
            st_inc[g] = m.val(m.s_st)

        mmv = {}
        sl4 = [m.s_sl0, m.s_sl1, m.s_sl2, m.s_sl3]
        for f in range(64):
            wld = {}
            for wi, wh in [(0, m.wG_h), (1, m.wU_h)]:
                u = 2 * f + wi
                @m.block.gpsimd
                def _(g, f=f, wi=wi, wh=wh, u=u):
                    if u >= 4:
                        g.wait_ge(m.s_pe, mmv[u - 4])
                    g.dma_start(out=WB4[:, u % 4], in_=wh[f]
                                ).then_inc(sl4[u % 4], 16)
                    m.inc(sl4[u % 4], 16)
                wld[wi] = (sl4[u % 4], m.val(sl4[u % 4]))
            g_done = wld[0]
            u_done = wld[1]

            @m.block.tensor
            def _(t, f=f, g_done=g_done, u_done=u_done):
                if f == 0:
                    t.wait_ge(m.s_ur, x1_done)
                t.wait_ge(g_done[0], g_done[1])
                if f >= 1:
                    t.wait_ge(m.s_act, silu_inc[f - 1])
                last = None
                for k in range(16):
                    for tt in range(4):
                        last = t.matmul(self.pf32(tt, 0, 512),
                                        WB4[:, (2 * f) % 4, k],
                                        X1[:, k, tt * 512:tt * 512 + 512],
                                        start=(k == 0), stop=(k == 15))
                last.then_inc(m.s_pe, 1)
                m.inc(m.s_pe, 1)
                mmv[2 * f] = m.val(m.s_pe)
                t.wait_ge(u_done[0], u_done[1])
                if f >= 1:
                    t.wait_ge(m.s_act, ucp_inc[f - 1])
                last = None
                for k in range(16):
                    for tt in range(4):
                        last = t.matmul(self.pf32(4 + tt, 0, 512),
                                        WB4[:, (2 * f + 1) % 4, k],
                                        X1[:, k, tt * 512:tt * 512 + 512],
                                        start=(k == 0), stop=(k == 15))
                last.then_inc(m.s_pe, 1)
                m.inc(m.s_pe, 1)
                mmv[2 * f + 1] = m.val(m.s_pe)
            gmm = m.val(m.s_pe) - 1
            umm = m.val(m.s_pe)
            dve_snap = m.val(m.s_dve)

            @m.block.scalar
            def _(sc, f=f, gmm=gmm, umm=umm, dve_snap=dve_snap):
                sc.wait_ge(m.s_pe, gmm)
                if f >= 2:
                    sc.wait_ge(m.s_dve, hb_inc_d[f - 2])
                sc.drain()
                sc.activation(GB[:, f % 2], self.PS_A[:, 0:2048], AF.Silu
                              ).then_inc(m.s_act, 1)
                m.inc(m.s_act, 1)
                silu_inc[f] = m.val(m.s_act)
                sc.wait_ge(m.s_pe, umm)
                sc.drain()
                sc.copy(UB[:, f % 2], self.PS_B[:, 0:2048]
                        ).then_inc(m.s_act, 1)
                m.inc(m.s_act, 1)
                ucp_inc[f] = m.val(m.s_act)

            @m.block.vector
            def _(v, f=f):
                v.wait_ge(m.s_act, ucp_inc[f])
                if f >= 8:
                    v.wait_ge(m.s_st, st_inc[(f - 8) // 4])
                v.drain()
                v.tensor_tensor(HB8[:, f % 8], GB[:, f % 2], UB[:, f % 2],
                                ALU.mult).then_inc(m.s_dve, 1)
                m.inc(m.s_dve, 1)
            hb_inc_d[f] = m.val(m.s_dve)
            if f >= 6 and (f - 6) % 4 == 0:
                sparsify((f - 6) // 4)
        sparsify(15)

    def _f2(self):
        m = self
        X1 = m.X1V
        WDB = self.sb_at("WDBv", [128, 4, 2, 512], F16, self.sballoc["WBUF"][0])
        HB2 = self.sb_at("HB2v", [128, 4, 2, 1024], F16,
                         self.sballoc["s_run"][0])
        sl = [m.s_sl0, m.s_sl1, m.s_sl2, m.s_sl3]
        hs = [m.s_ur, m.s_ur1, m.s_ur2, m.s_ur3]
        pre_st = m.val(m.s_st)

        for eq in range(4):
            for ttp in range(2):
                ev0 = m.val(m.s_dve)
                pe00 = m.val(m.s_pe)
                for fp in range(32):
                    f = 2 * fp
                    snap_pe = m.val(m.s_pe)

                    @m.block.gpsimd
                    def _(g, eq=eq, fp=fp, f=f, snap_pe=snap_pe, pe00=pe00):
                        if fp >= 4:
                            g.wait_ge(m.s_pe, snap_pe - 3)
                        elif pe00 > 0:
                            g.wait_ge(m.s_pe, pe00)
                        g.dma_start(
                            out=WDB[:, fp % 4].rearrange(
                                "p f j -> p (f j)").rearrange(
                                "p (f e j) -> p f e j", f=2, e=4),
                            in_=m.wD_h[eq, f:f + 2].rearrange(
                                "f e p j -> p f e j")
                            ).then_inc(sl[fp % 4], 16)
                        m.inc(sl[fp % 4], 16)
                    wd_done = m.val(sl[fp % 4])

                    @m.block.sync
                    def _(s, eq=eq, ttp=ttp, fp=fp, f=f, snap_pe=snap_pe,
                          pe00=pe00, pre_st=pre_st):
                        if fp >= 4:
                            s.wait_ge(m.s_pe, snap_pe - 3)
                        else:
                            s.wait_ge(m.s_st, pre_st)
                            s.wait_ge(m.s_pe, pe00)
                        s.dma_start(
                            out=HB2[:, fp % 4],
                            in_=m.hscr_h[f:f + 2, :,
                                         ttp * 1024:ttp * 1024 + 1024]
                            .rearrange("f p t -> p f t")
                            ).then_inc(hs[fp % 4], 16)
                        m.inc(hs[fp % 4], 16)
                    h_done = m.val(hs[fp % 4])

                    @m.block.tensor
                    def _(t, eq=eq, ttp=ttp, fp=fp, wd_done=wd_done,
                          h_done=h_done, ev0=ev0):
                        t.wait_ge(sl[fp % 4], wd_done)
                        t.wait_ge(hs[fp % 4], h_done)
                        if fp == 0:
                            t.wait_ge(m.s_dve, ev0)
                        last = None
                        for fi in range(2):
                            for e2 in range(4):
                                for tt in range(2):
                                    last = t.matmul(
                                        self.pf32(e2 * 2 + tt, 0, 512),
                                        WDB[:, fp % 4, fi,
                                            e2 * 128:e2 * 128 + 128],
                                        HB2[:, fp % 4, fi,
                                            tt * 512:tt * 512 + 512],
                                        start=(fp == 0 and fi == 0),
                                        stop=(fp == 31 and fi == 1))
                        last.then_inc(m.s_pe, 1)
                        m.inc(m.s_pe, 1)
                dn_done = m.val(m.s_pe)

                @m.block.vector
                def _(v, eq=eq, ttp=ttp, dn_done=dn_done):
                    v.wait_ge(m.s_pe, dn_done)
                    for e2 in range(4):
                        for tt in range(2):
                            c = eq * 4 + e2
                            col = ttp * 1024 + tt * 512
                            i = v.tensor_tensor(
                                X1[:, c, col:col + 512],
                                X1[:, c, col:col + 512],
                                self.pf32(e2 * 2 + tt, 0, 512), ALU.add)
                    i.then_inc(m.s_dve, 1)
                    m.inc(m.s_dve, 1)

    def _f3(self):
        m = self
        X1 = m.X1V
        MRB4 = self.sb_at("MRB4", [128, 4, 512], F32, self.sballoc["prevs"][0])
        f2_done = m.val(m.s_dve)
        stats_read = {}
        norm_done = {}
        for tq in range(4):
            par = tq % 2
            bA, bB = 4 + 2 * par, 5 + 2 * par
            sq0 = m.val(m.s_dve)
            pe_before = m.val(m.s_pe)

            @m.block.vector
            def _(v, tq=tq, pe_before=pe_before, f2_done=f2_done):
                if tq == 0:
                    v.wait_ge(m.s_dve, f2_done)
                n = 0
                for e in range(16):
                    if n > 0:
                        v.wait_ge(m.s_pe, pe_before + n)
                    v.drain()
                    v.tensor_tensor(m.SQF2[:, tq % 2],
                                    X1[:, e, tq * 512:tq * 512 + 512],
                                    X1[:, e, tq * 512:tq * 512 + 512],
                                    ALU.mult).then_inc(m.s_dve, 1)
                    m.inc(m.s_dve, 1)
                    n += 1

            @m.block.tensor
            def _(t, tq=tq, sq0=sq0, f2_done=f2_done, bA=bA, bB=bB):
                if tq == 0:
                    t.wait_ge(m.s_dve, f2_done)
                elif tq >= 2:
                    t.wait_ge(m.s_dve, stats_read[tq - 2])
                for e in range(16):
                    t.matmul(self.pf32(bA, 0, 512)[0:1, :], m.o128x1,
                             X1[:, e, tq * 512:tq * 512 + 512],
                             start=(e == 0), stop=(e == 15))
                    t.wait_ge(m.s_dve, sq0 + e + 1)
                    t.matmul(self.pf32(bB, 0, 512)[0:1, :], m.o128x1,
                             m.SQF2[:, tq % 2],
                             start=(e == 0), stop=(e == 15)
                             ).then_inc(m.s_pe, 1)
                    m.inc(m.s_pe, 1)
            ln_pe = m.val(m.s_pe)

            @m.block.vector
            def _(v, tq=tq, ln_pe=ln_pe, bA=bA, bB=bB,
                  er_snap=m.val(m.s_er)):
                v.wait_ge(m.s_pe, ln_pe)
                v.wait_ge(m.s_er, er_snap)
                v.drain()
                v.tensor_scalar_mul(m.MROW[:, 0:512],
                                    self.pf32(bA, 0, 512, p=1), 1.0 / D)
                v.drain()
                v.tensor_scalar_mul(m.RROW[:, 0:512],
                                    self.pf32(bB, 0, 512, p=1), 1.0 / D)
                v.drain()
                v.tensor_tensor(m.SGM[0:1, 0:512], m.MROW[:, 0:512],
                                m.MROW[:, 0:512], ALU.mult)
                v.drain()
                v.tensor_tensor(m.RROW[:, 0:512], m.RROW[:, 0:512],
                                m.SGM[0:1, 0:512], ALU.subtract)
                v.drain()
                v.tensor_scalar_add(m.RROW[:, 0:512], m.RROW[:, 0:512],
                                    EPS_LN).then_inc(m.s_dve, 1)
                m.inc(m.s_dve, 1)
            stats_read[tq] = m.val(m.s_dve)
            var_done = m.val(m.s_dve)

            @m.block.scalar
            def _(sc, var_done=var_done):
                sc.wait_ge(m.s_dve, var_done)
                sc.drain()
                sc.activation(m.RROW[:, 0:512], m.RROW[:, 0:512], AF.Sqrt).then_inc(m.s_act, 1)
                m.inc(m.s_act, 1)
            st_done = m.val(m.s_act)

            @m.block.vector
            def _(v, st_done=st_done):
                v.wait_ge(m.s_act, st_done)
                v.drain()
                v.reciprocal(m.RROW[:, 0:512], m.RROW[:, 0:512]).then_inc(m.s_dve, 1)
                m.inc(m.s_dve, 1)
            rr_done = m.val(m.s_dve)
            er0 = m.val(m.s_er)

            @m.block.gpsimd
            def _(g, tq=tq, rr_done=rr_done, er0=er0, par=par):
                g.wait_ge(m.s_dve, rr_done)
                g.dma_start(out=m.rst_h[:, 0:512], in_=m.RROW[:, 0:512]
                            ).then_inc(m.s_er, 16)
                m.inc(m.s_er, 16)
                g.dma_start(out=m.rst_h[:, 512:1024], in_=m.MROW[:, 0:512]
                            ).then_inc(m.s_er, 16)
                m.inc(m.s_er, 16)
                g.wait_ge(m.s_er, er0 + 32)
                g.dma_start(out=MRB4[:, 2 * par], in_=m.rst_h[:, 0:512]
                            .partition_broadcast(128)).then_inc(m.s_er, 16)
                m.inc(m.s_er, 16)
                g.dma_start(out=MRB4[:, 2 * par + 1], in_=m.rst_h[:, 512:1024]
                            .partition_broadcast(128)).then_inc(m.s_er, 16)
                m.inc(m.s_er, 16)
            mrb_done = m.val(m.s_er)

            @m.block.vector
            def _(v, tq=tq, mrb_done=mrb_done, par=par,
                  st_snap=m.val(m.s_wdma)):
                v.wait_ge(m.s_er, mrb_done)
                v.wait_ge(m.s_wdma, st_snap)
                for e in range(16):
                    v.drain()
                    v.tensor_tensor(X1[:, e, tq * 512:tq * 512 + 512],
                                    X1[:, e, tq * 512:tq * 512 + 512],
                                    MRB4[:, 2 * par + 1], ALU.subtract)
                    v.drain()
                    i = v.tensor_tensor(X1[:, e, tq * 512:tq * 512 + 512],
                                        X1[:, e, tq * 512:tq * 512 + 512],
                                        MRB4[:, 2 * par], ALU.mult)
                i.then_inc(m.s_dve, 1)
                m.inc(m.s_dve, 1)
            norm_done[tq] = m.val(m.s_dve)

            @m.block.gpsimd
            def _(g, tq=tq, nd=norm_done[tq]):
                g.wait_ge(m.s_dve, nd)
                g.dma_start(
                    out=m.out_h.rearrange("(c p) t -> p c t", p=128)
                    [:, :, tq * 512:tq * 512 + 512],
                    in_=X1[:, :, tq * 512:tq * 512 + 512]
                    ).then_inc(m.s_wdma, 16)
                m.inc(m.s_wdma, 16)

    def _dump_tau(self, tau):
        m = self
        d = m.dbg_out(f"dbg_b1t{tau}", [128, 34, TAU + HALO], F16)
        if d is not None:
            hof = HALO if tau == 0 else 0
            @m.block.sync
            def _(s):
                s.wait_ge(m.s_act, m.val(m.s_act))
                s.wait_ge(m.s_dve, m.val(m.s_dve))
                s.dma_start(out=d[:, :, hof:hof + TAU],
                            in_=m.B1[:, 0:34, hof:hof + TAU]).then_inc(m.s_st, 16)
                m.inc(m.s_st, 16)
                s.wait_ge(m.s_st, m.val(m.s_st))

    def _finish_debug(self):
        m = self

        d3 = m.dbg_out("dbg_srun", [64, DIN], F32)
        if d3 is not None:
            @m.block.sync
            def _(s):
                s.wait_ge(m.s_dve, m.val(m.s_dve))
                s.dma_start(out=d3, in_=m.s_run).then_inc(m.s_st, 16)
                m.inc(m.s_st, 16)
        d4 = m.dbg_out("dbg_a", [32, T], F32)
        if d4 is not None:
            @m.block.sync
            def _(s):
                s.wait_ge(m.s_dve, m.val(m.s_dve))
                s.dma_start(out=d4, in_=m.a_park).then_inc(m.s_st, 16)
                m.inc(m.s_st, 16)
        d5 = m.dbg_out("dbg_y", [128, 32, T], F16)
        if d5 is not None:
            @m.block.sync
            def _(s):
                s.wait_ge(m.s_yd, m.val(m.s_yd))
                s.dma_start(out=d5, in_=m.ypark_h).then_inc(m.s_st, 16)
                m.inc(m.s_st, 16)
        d6 = m.dbg_out("dbg_x1", [128, 16, T], F16)
        if d6 is not None:
            @m.block.sync
            def _(s):
                s.wait_ge(m.s_st, m.val(m.s_st))
                s.dma_start(out=d6, in_=m.x1park_h).then_inc(m.s_st, 16)
                m.inc(m.s_st, 16)
        d2 = m.dbg_out("dbg_dt", [32, T], F32)
        if d2 is not None:
            @m.block.sync
            def _(s):
                s.wait_ge(m.s_act, m.val(m.s_act))
                s.dma_start(out=d2, in_=m.dt_park).then_inc(m.s_st, 16)
                m.inc(m.s_st, 16)

    def _final_wait(self):
        m = self

        @m.block.sync
        def _(s):
            for name in ["s_wdma", "s_ldma", "s_pe", "s_dve", "s_act", "s_gp",
                         "s_st", "s_cc", "s_wu", "s_sl0", "s_sl1", "s_yd",
                         "s_ai", "s_ai2", "s_af", "s_er", "s_er2", "s_tp",
                      "s_ur", "s_ur1", "s_sl2", "s_sl3", "s_ur2", "s_ur3"]:
                sem = getattr(m, name)
                s.wait_ge(sem, m.val(sem))



def prep_inputs(inputs):
    W = {}
    ip = np.ascontiguousarray(inputs["L0_attn_in_proj_w"])
    rows = np.zeros((NXC * 128, D), np.float32)
    rows[:4096] = ip[4096:8192]
    rows[4096:4160] = ip[8192:8256]
    rows[4224:4288] = ip[8256:8320]
    rows[4352:4384] = ip[8320:8352]
    W["wA"] = np.ascontiguousarray(
        rows.reshape(NXC, 128, 16, 128).transpose(0, 3, 2, 1))
    W["wZ"] = np.ascontiguousarray(
        ip[0:4096].reshape(32, 128, 16, 128).transpose(0, 3, 2, 1))
    op = np.ascontiguousarray(inputs["L0_attn_o_proj_w"])
    W["wO"] = np.ascontiguousarray(
        op.reshape(16, 128, 32, 128).transpose(0, 3, 2, 1))

    f_idx = np.arange(64)
    p_idx = np.arange(128)
    iperm = (4 * (128 * (f_idx[:, None] // 4) + p_idx[None, :])
             + (f_idx[:, None] % 4))
    wg = np.ascontiguousarray(inputs["L0_ffn_w_gate"])
    wu = np.ascontiguousarray(inputs["L0_ffn_w_up"])
    wd = np.ascontiguousarray(inputs["L0_ffn_w_down"])
    wgp = wg[:, iperm.reshape(-1)].reshape(D, 64, 128)
    W["wG"] = np.ascontiguousarray(
        wgp.reshape(16, 128, 64, 128).transpose(2, 1, 0, 3))
    wup = wu[:, iperm.reshape(-1)].reshape(D, 64, 128)
    W["wU"] = np.ascontiguousarray(
        wup.reshape(16, 128, 64, 128).transpose(2, 1, 0, 3))
    wdp = wd[iperm.reshape(-1), :].reshape(64, 128, 2048)
    W["wD"] = np.ascontiguousarray(
        wdp.reshape(64, 128, 4, 4, 128).transpose(2, 0, 3, 1, 4))

    cw = np.asarray(inputs["L0_attn_conv_w"], np.float32)
    cb = np.asarray(inputs["L0_attn_conv_b"], np.float32)
    cwp = np.zeros((34 * 128, 4), np.float32)
    cbp = np.zeros((34 * 128,), np.float32)
    cwp[:4096] = cw[:4096]
    cbp[:4096] = cb[:4096]
    cwp[4096:4160] = cw[4096:4160]
    cbp[4096:4160] = cb[4096:4160]
    cwp[4224:4288] = cw[4160:4224]
    cbp[4224:4288] = cb[4160:4224]
    W["convw"] = np.ascontiguousarray(
        cwp.reshape(34, 128, 4).transpose(1, 0, 2).reshape(128, 34 * 4))
    W["convb"] = np.ascontiguousarray(cbp.reshape(34, 128).T)

    W["alog"] = np.asarray(inputs["L0_attn_A_log"], np.float32).reshape(32, 1)
    W["dtb"] = np.asarray(inputs["L0_attn_dt_bias"], np.float32).reshape(32, 1)
    W["dskip"] = np.asarray(inputs["L0_attn_D_skip"], np.float32).reshape(1, 32)
    W["nw"] = np.ascontiguousarray(
        np.asarray(inputs["L0_attn_norm_w"], np.float32).reshape(32, 128).T)
    W["identc"] = np.eye(128, dtype=np.float32)
    W["ident32c"] = np.eye(128, dtype=np.float32)
    i_, j_ = np.mgrid[0:128, 0:128]
    W["maskTc"] = (j_ >= i_).astype(np.float32)
    W["o1x128"] = np.ones((1, 128), np.float32)
    W["o128x1"] = np.ones((128, 1), np.float32)
    oc = np.zeros((128, 32, 32), np.float32)
    for h in range(32):
        oc[:, h, h] = 1.0
    W["ocols"] = oc.reshape(128, 32 * 32)
    W["oscan"] = np.ones((32, CH), np.float32)
    return W


def prep_core_inputs(inputs, W, core):
    uT = inputs["_uT"]
    d = dict(W)
    d["u"] = np.ascontiguousarray(uT[:, core * T:core * T + T + HALO])
    g = (np.arange(8) < core).astype(np.float32)
    dd = (np.arange(8) <= core - 1).astype(np.float32)
    d["gates"] = np.concatenate([g, dd]).reshape(1, 16).astype(np.float32)
    return d


_BUILD_CACHE = {}


def kernel(**inputs):
    u = np.asarray(inputs["L0_attn_u"], np.float32)
    uT = np.zeros((D, HALO + 16384), np.float32)
    uT[:, HALO:] = u[0].T
    inputs = dict(inputs)
    inputs["_uT"] = uT

    W = prep_inputs(inputs)

    mk = MK()
    nc = mk.build()
    in_maps = [prep_core_inputs(inputs, W, c) for c in range(NCORES)]
    res = run_bass_kernel_spmd(nc, in_maps, core_ids=list(range(NCORES)))
    outs = [res.results[c]["out"] for c in range(NCORES)]
    full = np.concatenate([o.T for o in outs], axis=0)
    return full.reshape(1, 16384, D).astype(np.float32)


# version 14
# speedup vs baseline: 1.0109x; 1.0064x over previous
import contextlib
import numpy as np
import concourse.bass as bass
import concourse.mybir as mybir
from concourse.bass_utils import run_bass_kernel_spmd

F32 = mybir.dt.float32
F16 = mybir.dt.float16
AF = mybir.ActivationFunctionType
ALU = mybir.AluOpType

NCORES = 8
T = 2048
TAU = 1024
HALO = 3
CH = 256
D = 2048
DIN = 4096
NH = 32
FFI = 8192
NXC = 35
EPS_RMS = 1e-6
EPS_LN = 1e-5
AG_N = 64 * DIN + 32
SBUF_BASE = 16512
SBUF_MAX = 212736


class MK:
    def __init__(self, debug=()):
        self.debug = set(debug)
        nc = bass.Bass("TRN2", target_bir_lowering=False)
        self.nc = nc
        self.semvals = {}
        self.dbg = {}
        self.sballoc = {}

        dt_ = nc.dram_tensor
        self.ident32_h = dt_("ident32c", [128, 128], F32, kind="ExternalInput").ap()
        self.u_h = dt_("u", [D, T + HALO], F32, kind="ExternalInput").ap()
        self.wA_h = dt_("wA", [NXC, 128, 16, 128], F32, kind="ExternalInput").ap()
        self.wZ_h = dt_("wZ", [32, 128, 16, 128], F32, kind="ExternalInput").ap()
        self.wO_h = dt_("wO", [16, 128, 32, 128], F32, kind="ExternalInput").ap()
        self.wG_h = dt_("wG", [64, 128, 16, 128], F32, kind="ExternalInput").ap()
        self.wU_h = dt_("wU", [64, 128, 16, 128], F32, kind="ExternalInput").ap()
        self.wD_h = dt_("wD", [4, 64, 4, 128, 128], F32, kind="ExternalInput").ap()
        self.convw_h = dt_("convw", [128, 34 * 4], F32, kind="ExternalInput").ap()
        self.convb_h = dt_("convb", [128, 34], F32, kind="ExternalInput").ap()
        self.alog_h = dt_("alog", [32, 1], F32, kind="ExternalInput").ap()
        self.dtb_h = dt_("dtb", [32, 1], F32, kind="ExternalInput").ap()
        self.dskip_h = dt_("dskip", [1, 32], F32, kind="ExternalInput").ap()
        self.nw_h = dt_("nw", [128, 32], F32, kind="ExternalInput").ap()
        self.ident_h = dt_("identc", [128, 128], F32, kind="ExternalInput").ap()
        self.maskT_h = dt_("maskTc", [128, 128], F32, kind="ExternalInput").ap()
        self.o1x128_h = dt_("o1x128", [1, 128], F32, kind="ExternalInput").ap()
        self.o128x1_h = dt_("o128x1", [128, 1], F32, kind="ExternalInput").ap()
        self.oscan_h = dt_("oscan", [32, CH], F32, kind="ExternalInput").ap()
        self.gates_h = dt_("gates", [1, 16], F32, kind="ExternalInput").ap()
        self.ocols_h = dt_("ocols", [128, 32 * 32], F32, kind="ExternalInput").ap()

        self.out_h = dt_("out", [D, T], F32, kind="ExternalOutput").ap()

        self.ypark_h = dt_("ypark", [128, 32, T], F16, kind="Internal").ap()
        self.hscr_h = dt_("hscr", [64, 128, T], F16, kind="Internal").ap()
        self.agin_h = dt_("agin", [AG_N], F32, kind="Internal").ap()
        self.agout_h = dt_("agout", [NCORES, AG_N], F32, kind="Internal").ap()
        self.aflat_h = dt_("aflat", [1, 8192], F32, kind="Internal").ap()
        self.erow_h = dt_("erowh", [1, 32], F32, kind="Internal").ap()
        self.rst_h = dt_("rsth", [1, 1024], F32, kind="Internal").ap()
        self.rstdh_h = dt_("rstdh", [32, 1024], F32, kind="Internal").ap()
        self.aflat8_h = dt_("aflat8", [8, 8192], F32, kind="Internal").ap()
        self.erow8_h = dt_("erow8", [8, 32], F32, kind="Internal").ap()
        self.x1park_h = dt_("x1park", [128, 16, T], F16, kind="Internal").ap()

    def dbg_out(self, name, shape, dtype):
        if name in self.debug and name not in self.dbg:
            self.dbg[name] = self.nc.dram_tensor(
                name, shape, dtype, kind="ExternalOutput").ap()
        return self.dbg.get(name)

    def sem(self, name):
        s = self.es.enter_context(self.nc.semaphore(name))
        self.semvals[s.name] = 0
        setattr(self, name, s)
        return s

    def inc(self, s, n=1):
        self.semvals[s.name] += n
        return self.semvals[s.name]

    def val(self, s):
        return self.semvals[s.name]

    def sb_at(self, name, shape, dtype, off):
        nbytes = int(np.prod(shape[1:])) * mybir.dt.size(dtype)
        assert off + nbytes <= SBUF_MAX, (name, off, nbytes)
        self.sballoc[name] = (off, nbytes)
        return self.nc.alloc_sbuf_tensor_at(
            name, list(shape), dtype, offset=SBUF_BASE + off).ap()

    def build(self):
        nc = self.nc
        with contextlib.ExitStack() as es:
            self.es = es
            self._alloc_sbuf()
            self._alloc_psum()
            for s in ["s_wdma", "s_ldma", "s_pe", "s_dve", "s_act", "s_gp",
                      "s_st", "s_cc", "s_wu", "s_sl0", "s_sl1", "s_yd",
                      "s_ai", "s_ai2", "s_af", "s_er", "s_er2", "s_tp",
                      "s_ur", "s_ur1", "s_sl2", "s_sl3", "s_ur2", "s_ur3"]:
                self.sem(s)
            self.block = es.enter_context(nc.Block())
            self._emit_all()
        return nc

    def _alloc_sbuf(self):
        off = 0

        def A(name, shape, dtype):
            nonlocal off
            t = self.sb_at(name, shape, dtype, off)
            off += (self.sballoc[name][1] + 31) // 32 * 32
            return t

        self.ident = A("identS", [128, 128], F16)
        self.ident32 = A("ident32S", [128, 128], F32)
        self.maskT = A("maskTS", [128, 128], F16)
        self.o1x128 = A("o1x128S", [1, 128], F16)
        self.o128x1 = A("o128x1S", [128, 1], F16)
        self.oscan = A("oscanS", [32, CH], F32)
        self.convw = A("convwS", [128, 34, 4], F32)
        self.convb = A("convbS", [128, 34], F32)
        self.alog = A("alogS", [32, 1], F32)
        self.dtb = A("dtbS", [32, 1], F32)
        self.acol = A("acolS", [32, 1], F32)
        self.dbc = A("dbcS", [128, 32], F32)
        self.nw = A("nwS", [128, 32], F32)
        self.gbc = A("gbcS", [64, 16], F32)
        self.atot = A("atotS", [32, 1], F32)
        self.acum = A("acumS", [32, 8], F32)
        self.ag_atot = A("agatotS", [64, 8], F32)
        self.erow = A("erowS", [64, 32], F32)
        self.er32 = A("er32S", [32, 1], F32)

        self.B1 = A("B1", [128, NXC, TAU + HALO], F16)
        self.dt_park = A("dt_park", [32, T], F32)
        self.a_park = A("a_park", [32, T], F32)
        self.c_park = A("c_park", [64, T], F16)
        self.s_run = A("s_run", [64, DIN], F32)
        self.prevs = A("prevs", [64, DIN], F16)
        self.carry3 = A("carry3", [128, 34, 4], F16)

        self.AR_U = A("AR_U", [128, 16 * (TAU + HALO)], F16)
        self.A_I = self.sb_at("A_I", [128, 8192], F32,
                              self.sballoc["AR_U"][0])
        self.EA16 = self.sb_at("EA16", [64, 4096], F16,
                               self.sballoc["AR_U"][0] + 24576)
        self.XTM = A("XTM", [128, 2 * DIN], F16)
        self.s_in = self.sb_at("s_in", [64, DIN], F32,
                               self.sballoc["XTM"][0])
        self.SCL = A("SCL", [128, DIN], F16)
        self.W16 = A("W16", [128, 8192], F16)
        self.SSB = A("SSB", [128, 384], F32)
        self.WBUF = A("WBUF", [128, 2, 16, 128], F16)
        self.CONVACC = A("CONVACC", [128, 2, 1024], F32)
        self.TT32 = self.sb_at("TT32", [128, 256], F32,
                               self.sballoc["CONVACC"][0])
        self.atall = A("atallS", [32, 8], F32)
        self.X1F = None
        self.SINKG = self.sb_at("SINKG", [64, 4096], F32,
                                self.sballoc["W16"][0])
        self.SINK16 = self.sb_at("SINK16", [64, 4096], F16,
                                 self.sballoc["SCL"][0])
        self.AB64 = self.sb_at("AB64", [64, 4096], F32,
                               self.sballoc["AR_U"][0])
        self.WO_BUF = self.sb_at("WO_BUF", [128, 2, 32, 128], F16,
                                 self.sballoc["s_run"][0])
        self.X1PB = self.sb_at("X1PB", [128, 16, TAU], F16,
                               self.sballoc["AR_U"][0])
        self.X1 = self.sb_at("X1", [128, 16, T], F16,
                             self.sballoc["B1"][0])
        self.RROW = self.sb_at("RROW", [1, 512], F32,
                               self.sballoc["CONVACC"][0] + 4096)
        self.MROW = self.sb_at("MROW", [1, 512], F32,
                               self.sballoc["CONVACC"][0] + 6144)
        self.MRB = self.sb_at("MRB", [128, 2, 512], F32,
                              self.sballoc["prevs"][0])
        self.MRB2 = self.sb_at("MRB2", [128, 2, 1024], F32,
                               self.sballoc["prevs"][0])
        self.ocols = self.sb_at("ocolsS", [128, 32, 32], F16,
                                self.sballoc["dt_park"][0])
        self.erow8sb = self.sb_at("erow8sb", [32, 8], F32,
                                  self.sballoc["agatotS"][0])
        self.erow8 = self.sb_at("erow8", [64, 256], F32,
                                self.sballoc["SSB"][0])
        self.AB64s = [self.sb_at("AB64a", [64, 4096], F32,
                                 self.sballoc["AR_U"][0]),
                      self.sb_at("AB64b", [64, 4096], F32,
                                 self.sballoc["AR_U"][0] + 16384)]
        self.EA16s = [self.sb_at("EA16a", [64, 4096], F16,
                                 self.sballoc["W16"][0]),
                      self.sb_at("EA16b", [64, 4096], F16,
                                 self.sballoc["W16"][0] + 8192)]
        self.RST32 = self.sb_at("RST32", [32, 1024], F32,
                                self.sballoc["CONVACC"][0])
        self.SGM = self.sb_at("SGM", [128, 1024], F32,
                              self.sballoc["CONVACC"][0] + 0)
        self.URES = self.sb_at("URES", [128, 2, 512], F32,
                               self.sballoc["CONVACC"][0] + 0)
        self.SQ16 = self.sb_at("SQ16", [128, 2048], F16,
                               self.sballoc["SCL"][0])
        self.SQF2 = self.sb_at("SQF2", [128, 2, 512], F16,
                               self.sballoc["SCL"][0])
        self.GBUF = self.sb_at("GBUF", [128, 2048], F16,
                               self.sballoc["CONVACC"][0])
        self.HBUF = self.sb_at("HBUF", [128, 4, 2048], F16,
                               self.sballoc["dt_park"][0])
        self.ABS4 = self.sb_at("ABS4", [128, 4, 2048], F16,
                               self.sballoc["s_run"][0])
        self.F1S = self.sb_at("F1S", [128, 4, 2048], F16,
                              self.sballoc["AR_U"][0])
        self.WDB = self.sb_at("WDB", [128, 2, 512], F16,
                              self.sballoc["WBUF"][0])
        self.HB2 = self.sb_at("HB2", [128, 2, 1024], F16,
                              self.sballoc["WBUF"][0] + 2048)
        self.OUT32 = self.sb_at("OUT32", [128, 16, 512], F32,
                                self.sballoc["AR_U"][0])
        self.SQF = self.sb_at("SQF", [128, 512], F16,
                              self.sballoc["SCL"][0])
        self.end_fixed = off
        assert off <= SBUF_MAX, off

    def _alloc_psum(self):
        self.PS_A = self.nc.alloc_psum_tensor("PS_A", [128, 2048], F32).ap()
        self.PS_B = self.nc.alloc_psum_tensor("PS_B", [128, 2048], F32).ap()

    def bank(self, i, w=512, p=128):
        base = self.PS_A if i < 4 else self.PS_B
        j = i % 4
        return base[0:p, 512 * j:512 * j + w]

    def bank16(self, i, w=128, p=128):
        base = self.PS_A if i < 4 else self.PS_B
        j = i % 4
        return base.bitcast(F16)[0:p, 1024 * j:1024 * j + w]

    def _emit_all(self):
        self._consts()
        for tau in range(2):
            self._a1(tau)
            self._conv(tau)
            self._a2_prep(tau)
            for kc in range(4):
                self._a2_chunk(tau, kc)
            self._a2_tau_end(tau)
            self._dump_tau(tau)
        self._gather()
        self._gather_corrprep()
        for tau in range(2):
            self._a3(tau)
            self._a4(tau)
        self._f1()
        self._f2()
        self._f3()
        self._finish_debug()
        self._final_wait()

    def _consts(self):
        m = self

        @m.block.gpsimd
        def _(g):
            g.memset(m.s_run, 0.0).then_inc(m.s_gp, 1)
            m.inc(m.s_gp, 1)
            for dst, src in [(m.ident, m.ident_h), (m.o1x128, m.o1x128_h),
                             (m.o128x1, m.o128x1_h), (m.maskT, m.maskT_h)]:
                g.dma_start(out=dst, in_=src).then_inc(m.s_wdma, 16)
                m.inc(m.s_wdma, 16)
            for dst, src, p in [(m.dbc, m.dskip_h, 128), (m.gbc, m.gates_h, 64)]:
                g.dma_start(out=dst, in_=src.partition_broadcast(p)
                            ).then_inc(m.s_wdma, 16)
                m.inc(m.s_wdma, 16)

        @m.block.sync
        def _(s):
            for dst, src in [(m.ident32, m.ident32_h),
                             (m.oscan, m.oscan_h),
                             (m.convb, m.convb_h), (m.alog, m.alog_h),
                             (m.dtb, m.dtb_h), (m.nw, m.nw_h)]:
                s.dma_start(out=dst, in_=src).then_inc(m.s_ldma, 16)
                m.inc(m.s_ldma, 16)
            s.dma_start(out=m.convw,
                        in_=m.convw_h.rearrange("p (j k) -> p j k", k=4)
                        ).then_inc(m.s_ldma, 16)
            m.inc(m.s_ldma, 16)

        @m.block.scalar
        def _(sc):
            sc.wait_ge(m.s_ldma, m.val(m.s_ldma))
            sc.wait_ge(m.s_wdma, m.val(m.s_wdma))
            sc.activation(m.acol, m.alog, AF.Exp, scale=1.0)
            sc.drain()
            sc.mul(m.acol, m.acol, -1.0)
            sc.nop().then_inc(m.s_act, 1)
            m.inc(m.s_act, 1)

    def _a1(self, tau):
        m = self
        TW = TAU + HALO if tau == 0 else TAU
        u0 = 0 if tau == 0 else TAU + HALO
        tts = [(i, min(i + 512, TW)) for i in range(0, TW, 512)]
        ntt = len(tts)
        dtc0 = tau * TAU
        hof = HALO if tau == 0 else 0

        pre_act = m.val(m.s_act)
        pre_dve = m.val(m.s_dve)
        pre_pe = m.val(m.s_pe)
        pre_st = m.val(m.s_st)
        pre_yd = m.val(m.s_yd)

        @m.block.gpsimd
        def _(g):
            g.wait_ge(m.s_act, pre_act)
            g.wait_ge(m.s_dve, pre_dve)
            g.wait_ge(m.s_pe, pre_pe)
            g.wait_ge(m.s_st, pre_st)
            g.wait_ge(m.s_yd, pre_yd)
            for si, (t0, t1) in enumerate(tts):
                if si > 0:
                    g.wait_ge(m.s_wu, m.val(m.s_wu))
                g.dma_start(
                    out=m.AR_U[:, 0:16 * TW].rearrange(
                        "p (k t) -> p k t", t=TW)[:, :, t0:t1],
                    in_=m.u_h[:, u0:u0 + TW].rearrange(
                        "(k p) t -> p k t", p=128)[:, :, t0:t1],
                ).then_inc(m.s_wu, 16)
                m.inc(m.s_wu, 16)
        uload = m.val(m.s_wu)

        evac0 = m.val(m.s_act)
        pe0 = m.val(m.s_pe)
        sl = [m.s_sl0, m.s_sl1]
        sl0_0, sl1_0 = m.val(m.s_sl0), m.val(m.s_sl1)
        slbase = [sl0_0, sl1_0]

        @m.block.gpsimd
        def _(g):
            for e in range(NXC):
                if e >= 2:
                    g.wait_ge(m.s_pe, pe0 + (e - 1) * ntt)
                g.dma_start(out=m.WBUF[:, e % 2], in_=m.wA_h[e]
                            ).then_inc(sl[e % 2], 16)
                m.inc(sl[e % 2], 16)

        @m.block.tensor
        def _(t):
            gi = 0
            for e in range(NXC):
                t.wait_ge(sl[e % 2], slbase[e % 2] + 16 * (e // 2 + 1))
                for ti, (t0, t1) in enumerate(tts):
                    if e == 0:
                        t.wait_ge(m.s_wu, uload - 16 * (ntt - ti - 1))
                    bk = gi % 6
                    if gi >= 6:
                        t.wait_ge(m.s_act, evac0 + (gi - 6 + 1))
                    last = None
                    for k in range(16):
                        last = t.matmul(self.bank(bk, t1 - t0),
                                        m.WBUF[:, e % 2, k],
                                        m.AR_U[:, 0:16 * TW].rearrange(
                                            "p (k t) -> p k t", t=TW)[:, k, t0:t1],
                                        start=(k == 0), stop=(k == 15))
                    last.then_inc(m.s_pe, 1)
                    m.inc(m.s_pe, 1)
                    gi += 1

        @m.block.scalar
        def _(sc):
            gi = 0
            for e in range(NXC):
                for ti, (t0, t1) in enumerate(tts):
                    bk = gi % 6
                    sc.wait_ge(m.s_pe, pe0 + gi + 1)
                    if e < 34:
                        i = sc.copy(m.B1[:, e, t0:t1], self.bank(bk, t1 - t0))
                    else:
                        lo = max(t0, hof)
                        if lo < t1:
                            i = sc.copy(
                                m.dt_park[:, dtc0 + lo - hof:dtc0 + t1 - hof],
                                self.bank(bk, t1 - t0, p=32)[:, lo - t0:t1 - t0])
                        else:
                            i = sc.nop()
                    i.then_inc(m.s_act, 1)
                    m.inc(m.s_act, 1)
                    gi += 1

    def _conv(self, tau):
        m = self
        TW = TAU + HALO if tau == 0 else TAU
        hof = HALO if tau == 0 else 0
        ntt = 3 if tau == 0 else 2
        pre = m.val(m.s_act)
        a1_evac0 = pre - NXC * ntt
        dve0 = m.val(m.s_dve)
        act0 = m.val(m.s_act)
        ldc = m.val(m.s_ldma)

        @m.block.vector
        def _(v):
            v.wait_ge(m.s_ldma, ldc)
            for j in range(34):
                v.wait_ge(m.s_act, a1_evac0 + (j + 1) * ntt)
                acc = m.CONVACC[:, j % 2]
                if j >= 2:
                    v.wait_ge(m.s_act, act0 + (j - 2) + 1)
                last = None
                if tau == 0:
                    for k in range(4):
                        src_ = m.B1[:, j, k:k + TAU]
                        if k == 0:
                            v.tensor_scalar(acc, src_, m.convw[:, j, 0:1],
                                            None, ALU.mult)
                        else:
                            v.drain()
                            last = v.scalar_tensor_tensor(
                                acc, src_, m.convw[:, j, k:k + 1],
                                acc, ALU.mult, ALU.add)
                else:
                    for k in range(4):
                        src_ = m.B1[:, j, k:k + TAU - 3]
                        if k == 0:
                            v.tensor_scalar(acc[:, 3:TAU], src_,
                                            m.convw[:, j, 0:1], None, ALU.mult)
                        else:
                            v.drain()
                            v.scalar_tensor_tensor(
                                acc[:, 3:TAU], src_, m.convw[:, j, k:k + 1],
                                acc[:, 3:TAU], ALU.mult, ALU.add)
                    for t in range(3):
                        first = True
                        for k in range(4):
                            p = t - 3 + k
                            src_ = (m.carry3[:, j, 3 + p:4 + p] if p < 0
                                    else m.B1[:, j, p:p + 1])
                            if first:
                                v.drain()
                                v.tensor_scalar(acc[:, t:t + 1], src_,
                                                m.convw[:, j, k:k + 1], None,
                                                ALU.mult)
                                first = False
                            else:
                                v.drain()
                                last = v.scalar_tensor_tensor(
                                    acc[:, t:t + 1], src_,
                                    m.convw[:, j, k:k + 1],
                                    acc[:, t:t + 1], ALU.mult, ALU.add)
                last.then_inc(m.s_dve, 1)
                m.inc(m.s_dve, 1)
                if tau == 0:
                    v.tensor_copy(m.carry3[:, j, 0:3], m.B1[:, j, TW - 3:TW]
                                  ).then_inc(m.s_dve, 1)
                    m.inc(m.s_dve, 1)

        nd = 2 if tau == 0 else 1

        @m.block.scalar
        def _(sc):
            for j in range(34):
                sc.wait_ge(m.s_dve, dve0 + nd * (j + 1))
                sc.activation(m.B1[:, j, hof:hof + TAU], m.CONVACC[:, j % 2],
                              AF.Silu, bias=m.convb[:, j:j + 1], scale=1.0
                              ).then_inc(m.s_act, 1)
                m.inc(m.s_act, 1)

    def _a2_prep(self, tau):
        m = self
        c0 = tau * TAU
        pre_act = m.val(m.s_act)
        V = m.CONVACC[0:32, 0, 0:1024]
        W = m.CONVACC[0:32, 1, 0:1024]

        @m.block.scalar
        def _(sc):
            sc.wait_ge(m.s_act, pre_act)
            sc.wait_ge(m.s_dve, m.val(m.s_dve))
            sc.drain()
            sc.activation(V, m.dt_park[:, c0:c0 + TAU], AF.Identity,
                          bias=m.dtb[:, 0:1], scale=1.0)
            sc.drain()
            sc.activation(W, V, AF.Abs)
            sc.drain()
            sc.activation(W, W, AF.Exp, scale=-1.0)
            sc.drain()
            sc.activation(W, W, AF.Ln, bias=1.0).then_inc(m.s_act, 1)
            m.inc(m.s_act, 1)
        actv = m.val(m.s_act)

        @m.block.vector
        def _(v):
            v.wait_ge(m.s_act, actv)
            v.tensor_scalar_max(V, V, 0.0)
            v.drain()
            v.tensor_add(m.dt_park[:, c0:c0 + TAU], V, W)
            v.drain()
            v.tensor_scalar_mul(V, m.dt_park[:, c0:c0 + TAU], m.acol[:, 0:1])
            v.drain()
            for kc in range(4):
                i = v.tensor_tensor_scan(
                    m.a_park[:, c0 + kc * CH:c0 + (kc + 1) * CH],
                    m.oscan, V[:, kc * CH:(kc + 1) * CH], 0.0,
                    ALU.mult, ALU.add)
            i.then_inc(m.s_dve, 1)
            m.inc(m.s_dve, 1)

    def pslot16(self, bank, slot, w=128):
        base = (bank % 4) * 1024 + slot * 128
        t = self.PS_A if bank < 4 else self.PS_B
        return t.bitcast(F16)[:, base:base + w]

    def pf32(self, bank, off, w, p=128):
        t = self.PS_A if bank < 4 else self.PS_B
        return t[0:p, (bank % 4) * 512 + off:(bank % 4) * 512 + off + w]

    def _a2_chunk(self, tau, kc):
        m = self
        hof = HALO if tau == 0 else 0
        cc0 = hof + kc * CH
        g0 = tau * TAU + kc * CH
        aw_dve = m.val(m.s_dve)
        pre_act = m.val(m.s_act)

        @m.block.scalar
        def _(sc):
            sc.wait_ge(m.s_dve, aw_dve)
            sc.activation(m.CONVACC[0:32, 1, 0:CH], m.a_park[:, g0:g0 + CH],
                          AF.Exp, bias=m.a_park[:, g0 + CH - 1:g0 + CH],
                          scale=-1.0)
            sc.activation(m.er32, m.a_park[:, g0 + CH - 1:g0 + CH], AF.Exp
                          ).then_inc(m.s_act, 1)
            m.inc(m.s_act, 1)
        scd_ready = m.val(m.s_act)

        @m.block.vector
        def _(v):
            v.wait_ge(m.s_act, scd_ready)
            v.tensor_tensor(m.CONVACC[0:32, 1, 0:CH],
                            m.CONVACC[0:32, 1, 0:CH],
                            m.dt_park[:, g0:g0 + CH], ALU.mult).then_inc(m.s_dve, 1)
            m.inc(m.s_dve, 1)
        scdm_ready = m.val(m.s_dve)

        af0 = m.val(m.s_af)
        er0 = m.val(m.s_er)

        @m.block.gpsimd
        def _(g):
            g.wait_ge(m.s_dve, aw_dve)
            g.wait_ge(m.s_ai, m.val(m.s_ai))
            g.dma_start(out=m.aflat_h.rearrange("a (h t) -> (a h) t", h=32),
                        in_=m.a_park[:, g0:g0 + CH]).then_inc(m.s_af, 16)
            m.inc(m.s_af, 16)
            g.wait_ge(m.s_act, scd_ready)
            g.dma_start(out=m.erow_h.rearrange("a h -> h a"), in_=m.er32
                        ).then_inc(m.s_er, 16)
            m.inc(m.s_er, 16)
            g.wait_ge(m.s_er, er0 + 16)
            g.dma_start(out=m.erow[:, 0:32],
                        in_=m.erow_h.partition_broadcast(64)
                        ).then_inc(m.s_er, 16)
            m.inc(m.s_er, 16)
        er_done = m.val(m.s_er)
        af_done = m.val(m.s_af)

        tp0 = m.val(m.s_tp)
        ev0 = m.val(m.s_dve)

        prev_xe = getattr(self, "_a2_prev_xevac", None)
        prev_sse = getattr(self, "_a2_prev_ssevac", None)
        prev_srun = getattr(self, "_a2_prev_srun", None)
        scd_act_mark = m.val(m.s_act)

        @m.block.tensor
        def _(t, prev_xe=prev_xe, prev_sse=prev_sse, prev_srun=prev_srun):
            t.wait_ge(m.s_act, pre_act)
            for grp in range(8):
                bank = 6 + grp % 2
                if grp >= 2:
                    t.wait_ge(m.s_dve, ev0 + (grp - 2) + 1)
                elif prev_xe is not None:
                    t.wait_ge(m.s_dve, prev_xe)
                last = None
                for s8 in range(8):
                    n = grp * 8 + s8
                    jt, ch = n // 32, n % 32
                    pd = self.pslot16(bank, s8)
                    last = t.transpose(
                        pd, m.B1[:, ch, cc0 + jt * 128:cc0 + jt * 128 + 128],
                        m.ident)
                last.then_inc(m.s_tp, 1)
                m.inc(m.s_tp, 1)
            if prev_sse is not None:
                t.wait_ge(m.s_dve, prev_sse)
            if prev_srun is not None:
                t.wait_ge(m.s_dve, prev_srun)
            t.wait_ge(m.s_dve, scdm_ready)
            last = None
            for jt in range(2):
                pd = self.pslot16(5, jt)[:, 0:64]
                last = t.transpose(pd, m.B1[0:64, 32, cc0 + jt * 128:
                                            cc0 + jt * 128 + 128],
                                   m.ident[0:64, 0:64])
            for which in range(3):
                for jt in range(2):
                    srcs = [m.a_park[:, g0 + jt * 128:g0 + jt * 128 + 128],
                            m.dt_park[:, g0 + jt * 128:g0 + jt * 128 + 128],
                            m.CONVACC[0:32, 1, jt * 128:jt * 128 + 128]]
                    pd = self.pf32(5, 192 + (which * 2 + jt) * 32, 32)
                    last = t.transpose(pd, srcs[which],
                                       m.ident32[0:32, 0:32])
            last.then_inc(m.s_tp, 1)
            m.inc(m.s_tp, 1)
            t.matmul(self.pf32(4, 0, 256), m.B1[0:64, 32, cc0:cc0 + 128],
                     m.B1[0:64, 33, cc0:cc0 + 256], start=True, stop=True)
            t.matmul(self.pf32(4, 256, 128),
                     m.B1[0:64, 32, cc0 + 128:cc0 + 256],
                     m.B1[0:64, 33, cc0 + 128:cc0 + 256],
                     start=True, stop=True).then_inc(m.s_tp, 1)
            m.inc(m.s_tp, 1)
        tp_small = tp0 + 9
        tp_sc = tp0 + 10

        @m.block.vector
        def _(v):
            for grp in range(8):
                bank = 6 + grp % 2
                v.wait_ge(m.s_tp, tp0 + grp + 1)
                v.tensor_copy(m.XTM[:, grp * 1024:grp * 1024 + 1024],
                              self.pslot16(bank, 0, 1024)
                              ).then_inc(m.s_dve, 1)
                m.inc(m.s_dve, 1)
        self._a2_prev_xevac = m.val(m.s_dve)

        @m.block.vector
        def _(v):
            v.wait_ge(m.s_tp, tp_small)
            for jt in range(2):
                v.tensor_copy(m.TT32.bitcast(F16)[:, 384 + 64 * jt:
                                                  448 + 64 * jt],
                              self.pslot16(5, jt)[:, 0:64])
            for which in range(3):
                for jt in range(2):
                    k = which * 2 + jt
                    i = v.tensor_copy(m.TT32[:, k * 32:k * 32 + 32],
                                      self.pf32(5, 192 + k * 32, 32))
            i.then_inc(m.s_dve, 1)
            m.inc(m.s_dve, 1)
            v.wait_ge(m.s_tp, tp_sc)
            v.tensor_tensor(m.SSB[:, 0:128], self.pf32(4, 0, 128), m.maskT,
                            ALU.mult)
            v.tensor_copy(m.SSB[:, 128:256], self.pf32(4, 128, 128))
            i = v.tensor_tensor(m.SSB[:, 256:384], self.pf32(4, 256, 128),
                                m.maskT, ALU.mult)
            i.then_inc(m.s_dve, 1)
            m.inc(m.s_dve, 1)
        self._a2_prev_ssevac = m.val(m.s_dve)
        ev_all = m.val(m.s_dve)

        @m.block.vector
        def _(v):
            v.wait_ge(m.s_gp, 1)
            v.tensor_copy(m.prevs, m.s_run).then_inc(m.s_dve, 1)
            m.inc(m.s_dve, 1)
            v.wait_ge(m.s_er, er_done)
            v.drain()
            v.tensor_tensor(
                m.s_run.rearrange("n (h p) -> n h p", h=32),
                m.s_run.rearrange("n (h p) -> n h p", h=32),
                m.erow[:, 0:32].unsqueeze(2).broadcast_to([64, 32, 128]),
                ALU.mult).then_inc(m.s_dve, 1)
            m.inc(m.s_dve, 1)
        prevs_done = m.val(m.s_dve) - 1
        scale_done = m.val(m.s_dve)

        for jt in range(2):
            pe_snap = m.val(m.s_pe)

            @m.block.vector
            def _(v, pe_snap=pe_snap, jt=jt):
                v.wait_ge(m.s_pe, pe_snap)
                v.tensor_tensor(
                    m.SCL[:, 0:4096].rearrange("p (h q) -> p h q", h=32),
                    m.XTM[:, jt * 4096:jt * 4096 + 4096].rearrange(
                        "p (h q) -> p h q", h=32),
                    m.TT32[:, 128 + jt * 32:160 + jt * 32].unsqueeze(2
                        ).broadcast_to([128, 32, 128]),
                    ALU.mult).then_inc(m.s_dve, 1)
                m.inc(m.s_dve, 1)
            xwd_done = m.val(m.s_dve)

            @m.block.tensor
            def _(t, jt=jt, xwd_done=xwd_done):
                t.wait_ge(m.s_dve, xwd_done)
                for p in range(8):
                    if p >= 2:
                        t.wait_ge(m.s_dve, xwd_done + (p - 2) + 1)
                    t.matmul(self.pf32(4 + p % 2, 0, 512, p=64),
                             m.TT32.bitcast(F16)[:, 384 + 64 * jt:448 + 64 * jt],
                             m.SCL[:, 512 * p:512 * p + 512],
                             start=True, stop=True).then_inc(m.s_pe, 1)
                    m.inc(m.s_pe, 1)
            st_pe = m.val(m.s_pe)

            @m.block.vector
            def _(v, jt=jt, st_pe=st_pe):
                for p in range(8):
                    v.wait_ge(m.s_pe, st_pe - 8 + p + 1)
                    if jt == 0 and p < 2:
                        v.wait_ge(m.s_dve, scale_done)
                    if p == 0:
                        v.drain()
                    v.tensor_tensor(m.s_run[:, 512 * p:512 * p + 512],
                                    m.s_run[:, 512 * p:512 * p + 512],
                                    self.pf32(4 + p % 2, 0, 512, p=64),
                                    ALU.add).then_inc(m.s_dve, 1)
                    m.inc(m.s_dve, 1)

        self._a2_prev_srun = m.val(m.s_dve)

        for hg in range(2):
            ai0 = m.val(m.s_ai)

            @m.block.gpsimd
            def _(g, hg=hg, ai0=ai0):
                g.wait_ge(m.s_af, af_done)
                g.wait_ge(m.s_act, m.val(m.s_act))
                g.dma_start(
                    out=m.A_I[:, 0:4096],
                    in_=m.aflat_h[:, hg * 4096:hg * 4096 + 4096]
                    .partition_broadcast(128)).then_inc(m.s_ai, 16)
                m.inc(m.s_ai, 16)
                g.dma_start(
                    out=m.A_I[:, 4096:6144],
                    in_=m.aflat_h.rearrange("a (h t) -> a h t", h=32)
                    [:, hg * 16:hg * 16 + 16, 128:256]
                    .partition_broadcast(128)).then_inc(m.s_ai, 16)
                m.inc(m.s_ai, 16)
            ai_done = m.val(m.s_ai)

            pe_prev = m.val(m.s_pe)

            @m.block.scalar
            def _(sc, hg=hg, ai_done=ai_done, pe_prev=pe_prev):
                sc.wait_ge(m.s_ai, ai_done)
                sc.wait_ge(m.s_pe, pe_prev)
                sc.activation(m.EA16[:, 0:4096], m.A_I[0:64, 0:4096], AF.Exp
                              ).then_inc(m.s_act, 1)
                m.inc(m.s_act, 1)
            ea_done = m.val(m.s_act)

            @m.block.vector
            def _(v, hg=hg, ai_done=ai_done, ea_done=ea_done):
                v.wait_ge(m.s_ai, ai_done)
                v.wait_ge(m.s_act, ea_done)
                v.wait_ge(m.s_dve, ev_all * 0 + ev_all)
                last = None
                for h in range(16):
                    last = v.tensor_scalar(
                        m.A_I[:, h * 256:h * 256 + 256],
                        m.A_I[:, h * 256:h * 256 + 256],
                        m.TT32[:, hg * 16 + h:hg * 16 + h + 1], 0.0,
                        ALU.subtract, ALU.min)
                last.then_inc(m.s_dve, 1)
                m.inc(m.s_dve, 1)
                last = None
                for h in range(16):
                    last = v.tensor_scalar(
                        m.A_I[:, 4096 + h * 128:4096 + h * 128 + 128],
                        m.A_I[:, 4096 + h * 128:4096 + h * 128 + 128],
                        m.TT32[:, 32 + hg * 16 + h:32 + hg * 16 + h + 1], 0.0,
                        ALU.subtract, ALU.min)
                last.then_inc(m.s_dve, 1)
                m.inc(m.s_dve, 1)
            seg_done = m.val(m.s_dve)

            @m.block.scalar
            def _(sc, seg_done=seg_done):
                sc.wait_ge(m.s_dve, seg_done)
                sc.activation(m.W16[:, 0:4096], m.A_I[:, 0:4096], AF.Exp)
                sc.activation(m.W16[:, 4096:6144], m.A_I[:, 4096:6144], AF.Exp
                              ).then_inc(m.s_act, 1)
                m.inc(m.s_act, 1)
            wexp_done = m.val(m.s_act)

            @m.block.vector
            def _(v, hg=hg, wexp_done=wexp_done, pe_prev=pe_prev):
                v.wait_ge(m.s_act, wexp_done)
                v.wait_ge(m.s_pe, pe_prev)
                v.tensor_tensor(
                    m.W16[:, 0:4096].rearrange("p (h q) -> p h q", h=16),
                    m.W16[:, 0:4096].rearrange("p (h q) -> p h q", h=16),
                    m.SSB[:, 0:256].unsqueeze(1).broadcast_to([128, 16, 256]),
                    ALU.mult)
                v.tensor_tensor(
                    m.W16[:, 4096:6144].rearrange("p (h q) -> p h q", h=16),
                    m.W16[:, 4096:6144].rearrange("p (h q) -> p h q", h=16),
                    m.SSB[:, 256:384].unsqueeze(1).broadcast_to([128, 16, 128]),
                    ALU.mult)
                v.tensor_tensor(
                    m.EA16[:, 0:4096].rearrange("p (h q) -> p h q", h=16),
                    m.EA16[:, 0:4096].rearrange("p (h q) -> p h q", h=16),
                    m.B1[0:64, 33, cc0:cc0 + 256].unsqueeze(1
                        ).broadcast_to([64, 16, 256]),
                    ALU.mult)
                v.tensor_tensor(
                    m.SCL[:, 0:2048].rearrange("p (h q) -> p h q", h=16),
                    m.XTM[:, 0 * 4096 + hg * 2048:0 * 4096 + hg * 2048 + 2048
                          ].rearrange("p (h q) -> p h q", h=16),
                    m.TT32[:, 64 + hg * 16:64 + hg * 16 + 16].unsqueeze(2
                        ).broadcast_to([128, 16, 128]),
                    ALU.mult)
                v.tensor_tensor(
                    m.SCL[:, 2048:4096].rearrange("p (h q) -> p h q", h=16),
                    m.XTM[:, 1 * 4096 + hg * 2048:1 * 4096 + hg * 2048 + 2048
                          ].rearrange("p (h q) -> p h q", h=16),
                    m.TT32[:, 96 + hg * 16:96 + hg * 16 + 16].unsqueeze(2
                        ).broadcast_to([128, 16, 128]),
                    ALU.mult).then_inc(m.s_dve, 1)
                m.inc(m.s_dve, 1)
            wready = m.val(m.s_dve)

            pe_y0 = m.val(m.s_pe)

            dve_y0 = m.val(m.s_dve)

            @m.block.tensor
            def _(t, hg=hg, wready=wready, pe_y0=pe_y0, dve_y0=dve_y0):
                t.wait_ge(m.s_dve, wready)
                for l in range(16):
                    h = hg * 16 + l
                    if l >= 4:
                        t.wait_ge(m.s_dve, dve_y0 + (l - 4) + 1)
                    yap = self.pf32(l % 4, 0, 256)
                    t.matmul(yap, m.SCL[:, l * 128:l * 128 + 128],
                             m.W16[:, l * 256:l * 256 + 256],
                             start=True, stop=False)
                    t.matmul(yap[:, 128:256],
                             m.SCL[:, 2048 + l * 128:2048 + l * 128 + 128],
                             m.W16[:, 4096 + l * 128:4096 + l * 128 + 128],
                             start=False, stop=False)
                    t.matmul(yap, m.prevs[:, h * 128:h * 128 + 128],
                             m.EA16[:, l * 256:l * 256 + 256],
                             start=False, stop=True).then_inc(m.s_pe, 1)
                    m.inc(m.s_pe, 1)

            @m.block.vector
            def _(v, hg=hg, pe_y0=pe_y0):
                for l in range(16):
                    h = hg * 16 + l
                    v.wait_ge(m.s_pe, pe_y0 + l + 1)
                    yap = self.pf32(l % 4, 0, 256)
                    v.scalar_tensor_tensor(
                        m.B1[:, h, cc0:cc0 + 256], m.B1[:, h, cc0:cc0 + 256],
                        m.dbc[:, h:h + 1], yap, ALU.mult, ALU.add
                        ).then_inc(m.s_dve, 1)
                    m.inc(m.s_dve, 1)

    def _a2_tau_end(self, tau):
        m = self
        hof = HALO if tau == 0 else 0
        dve_all = m.val(m.s_dve)
        act_all = m.val(m.s_act)

        @m.block.vector
        def _(v):
            v.wait_ge(m.s_dve, dve_all)
            v.tensor_copy(m.c_park[:, tau * TAU:tau * TAU + TAU],
                          m.B1[0:64, 33, hof:hof + TAU]).then_inc(m.s_dve, 1)
            m.inc(m.s_dve, 1)

        @m.block.sync
        def _(s):
            s.wait_ge(m.s_dve, m.val(m.s_dve))
            s.dma_start(out=m.ypark_h[:, :, tau * TAU:tau * TAU + TAU],
                        in_=m.B1[:, 0:32, hof:hof + TAU]).then_inc(m.s_yd, 16)
            m.inc(m.s_yd, 16)

    def _gather(self):
        m = self
        dve_all = m.val(m.s_dve)
        act_all = m.val(m.s_act)

        @m.block.vector
        def _(v):
            v.wait_ge(m.s_dve, dve_all)
            v.wait_ge(m.s_act, act_all)
            v.reduce_sum(m.atot, m.a_park[:, CH - 1:T:CH], mybir.AxisListType.X)
            v.drain()
            v.tensor_tensor_scan(m.acum[:, 1:8], m.oscan[:, 0:7],
                                 m.a_park[:, CH - 1:T - CH:CH], 0.0,
                                 ALU.mult, ALU.add).then_inc(m.s_dve, 1)
            m.inc(m.s_dve, 1)
        red_done = m.val(m.s_dve)

        @m.block.gpsimd
        def _(g):
            g.memset(m.acum[:, 0:1], 0.0)
            g.wait_ge(m.s_dve, red_done)
            g.memset(m.s_in, 0.0).then_inc(m.s_gp, 1)
            m.inc(m.s_gp, 1)

        @m.block.sync
        def _(s):
            s.wait_ge(m.s_dve, red_done)
            s.dma_start(out=m.agin_h[0:64 * DIN].rearrange("(n q) -> n q", n=64),
                        in_=m.s_run).then_inc(m.s_cc, 16)
            m.inc(m.s_cc, 16)
            s.dma_start(out=m.agin_h[64 * DIN:64 * DIN + 32
                                     ].rearrange("(h a) -> h a", a=1),
                        in_=m.atot).then_inc(m.s_cc, 16)
            m.inc(m.s_cc, 16)
        ag_in_done = m.val(m.s_cc)

        @m.block.gpsimd
        def _(g):
            g.wait_ge(m.s_cc, ag_in_done)
            g.collective_compute(
                "AllGather", ALU.bypass,
                replica_groups=[list(range(NCORES))],
                ins=[m.agin_h],
                outs=[m.agout_h],
            ).then_inc(m.s_gp, 1)
            m.inc(m.s_gp, 1)
        cc_done = m.val(m.s_gp)

        @m.block.sync
        def _(s):
            s.wait_ge(m.s_gp, cc_done)
            s.dma_start(out=m.atall,
                        in_=m.agout_h[:, 64 * DIN:64 * DIN + 32]
                        .rearrange("j h -> h j"),
                        allow_slow_non_contiguous=True).then_inc(m.s_cc, 16)
            m.inc(m.s_cc, 16)
        at_done = m.val(m.s_cc)

        for j in range(NCORES):
            er0 = m.val(m.s_er)

            @m.block.scalar
            def _(sc, j=j, at_done=at_done):
                if j == 0:
                    sc.wait_ge(m.s_cc, at_done)
                sc.wait_ge(m.s_er, m.val(m.s_er))
                sc.activation(m.er32, m.atall[:, j:j + 1], AF.Exp,
                              scale=m.gbc[0:32, 8 + j:9 + j]
                              ).then_inc(m.s_act, 1)
                m.inc(m.s_act, 1)
            e_done = m.val(m.s_act)

            @m.block.gpsimd
            def _(g, j=j, e_done=e_done, er0=er0, dve_prev=m.val(m.s_dve)):
                g.wait_ge(m.s_act, e_done)
                g.wait_ge(m.s_dve, dve_prev)
                g.dma_start(out=m.erow_h.rearrange("a h -> h a"), in_=m.er32
                            ).then_inc(m.s_er, 16)
                m.inc(m.s_er, 16)
                g.wait_ge(m.s_er, er0 + 16)
                g.dma_start(out=m.erow[:, 0:32],
                            in_=m.erow_h.partition_broadcast(64)
                            ).then_inc(m.s_er, 16)
                m.inc(m.s_er, 16)
                g.wait_ge(m.s_dve, m.val(m.s_dve))
                g.dma_start(out=m.AB64,
                            in_=m.agout_h[j, 0:64 * DIN]
                            .rearrange("(n q) -> n q", n=64)
                            ).then_inc(m.s_ai, 16)
                m.inc(m.s_ai, 16)
            er_done = m.val(m.s_er)
            sag_done = m.val(m.s_ai)

            @m.block.vector
            def _(v, j=j, er_done=er_done, sag_done=sag_done):
                if j == 0:
                    v.wait_ge(m.s_gp, m.val(m.s_gp))
                v.wait_ge(m.s_er, er_done)
                v.wait_ge(m.s_ai, sag_done)
                v.drain()
                v.tensor_tensor(
                    m.s_in.rearrange("n (h p) -> n h p", h=32),
                    m.s_in.rearrange("n (h p) -> n h p", h=32),
                    m.erow[:, 0:32].unsqueeze(2).broadcast_to([64, 32, 128]),
                    ALU.mult)
                v.drain()
                v.scalar_tensor_tensor(
                    m.s_in, m.AB64, m.gbc[:, j:j + 1], m.s_in,
                    ALU.mult, ALU.add).then_inc(m.s_dve, 1)
                m.inc(m.s_dve, 1)

    def _gather_corrprep(self):
        m = self
        red = m.val(m.s_dve)
        gp0 = m.val(m.s_gp)

        @m.block.sync
        def _(s):
            s.wait_ge(m.s_dve, red)
            s.wait_ge(m.s_gp, gp0)
            s.dma_start(out=m.aflat8_h.rearrange("g (h t) -> h g t", h=32),
                        in_=m.a_park.rearrange("h (g t) -> h g t", g=8)
                        ).then_inc(m.s_yd, 16)
            m.inc(m.s_yd, 16)
        e8_dve = m.val(m.s_dve)

        @m.block.scalar
        def _(sc):
            sc.wait_ge(m.s_dve, e8_dve)
            sc.wait_ge(m.s_gp, gp0)
            sc.drain()
            sc.activation(m.erow8sb, m.acum, AF.Exp).then_inc(m.s_act, 1)
            m.inc(m.s_act, 1)
        e8_done = m.val(m.s_act)
        er0 = m.val(m.s_er)

        @m.block.gpsimd
        def _(g, e8_done=e8_done, er0=er0):
            g.wait_ge(m.s_act, e8_done)
            g.dma_start(out=m.erow8_h.rearrange("g h -> h g"), in_=m.erow8sb,
                        allow_slow_non_contiguous=True).then_inc(m.s_er, 16)
            m.inc(m.s_er, 16)
            g.wait_ge(m.s_er, er0 + 16)
            g.dma_start(out=m.erow8,
                        in_=m.erow8_h.rearrange("g h -> (g h)").unsqueeze(0)
                        .partition_broadcast(64)).then_inc(m.s_er, 16)
            m.inc(m.s_er, 16)
        self.erow8_done = m.val(m.s_er)

    def _a3(self, tau):
        m = self
        pre_dve = m.val(m.s_dve)
        pre_yd = m.val(m.s_yd)

        @m.block.sync
        def _(s):
            s.wait_ge(m.s_dve, pre_dve)
            s.wait_ge(m.s_act, m.val(m.s_act))
            s.wait_ge(m.s_pe, m.val(m.s_pe))
            s.wait_ge(m.s_yd, pre_yd)
            s.dma_start(out=m.B1[:, 0:32, 0:TAU],
                        in_=m.ypark_h[:, :, tau * TAU:tau * TAU + TAU]
                        ).then_inc(m.s_yd, 16)
            m.inc(m.s_yd, 16)
        yl_done = m.val(m.s_yd)

        corr_pe = {}
        ea_act = {}
        sink_dve = {}
        for kc in range(4):
            g = tau * 4 + kc
            pe_prev_kc = m.val(m.s_pe)

            @m.block.vector
            def _(v, kc=kc, g=g, pe_prev_kc=pe_prev_kc):
                if tau == 0 and kc == 0:
                    v.wait_ge(m.s_er, self.erow8_done)
                v.wait_ge(m.s_pe, pe_prev_kc)
                v.drain()
                v.tensor_tensor(
                    m.SINK16.rearrange("n (h p) -> n h p", h=32),
                    m.s_in.rearrange("n (h p) -> n h p", h=32),
                    m.erow8[:, g * 32:g * 32 + 32].unsqueeze(2
                        ).broadcast_to([64, 32, 128]),
                    ALU.mult).then_inc(m.s_dve, 1)
                m.inc(m.s_dve, 1)
            sink_dve[kc] = m.val(m.s_dve)

            for hg in range(2):
                slot = (kc * 2 + hg) % 2
                use = kc * 2 + hg

                @m.block.gpsimd
                def _(g_, g=g, hg=hg, slot=slot, use=use,
                      st_snap=m.val(m.s_st), yd_snap=m.val(m.s_yd)):
                    if tau == 1 and use == 0:
                        g_.wait_ge(m.s_st, st_snap)
                    if use >= 2:
                        g_.wait_ge(m.s_act, ea_act[use - 2])
                    elif tau == 0 and use < 2:
                        g_.wait_ge(m.s_yd, yd_snap)
                    ais = m.s_ai if slot == 0 else m.s_ai2
                    g_.dma_start(
                        out=m.AB64s[slot],
                        in_=m.aflat8_h[g:g + 1, hg * 4096:hg * 4096 + 4096]
                        .partition_broadcast(64)).then_inc(ais, 16)
                    m.inc(ais, 16)
                ai_done = m.val(m.s_ai if slot == 0 else m.s_ai2)
                pe_prev = corr_pe.get(use - 2)

                @m.block.scalar
                def _(sc, slot=slot, ai_done=ai_done, pe_prev=pe_prev):
                    sc.wait_ge(m.s_ai if slot == 0 else m.s_ai2, ai_done)
                    if pe_prev is not None:
                        sc.wait_ge(m.s_pe, pe_prev)
                    sc.drain()
                    sc.activation(m.EA16s[slot], m.AB64s[slot], AF.Exp
                                  ).then_inc(m.s_act, 1)
                    m.inc(m.s_act, 1)
                ea_act[use] = m.val(m.s_act)

                @m.block.vector
                def _(v, g=g, slot=slot, ea_done=m.val(m.s_act)):
                    v.wait_ge(m.s_act, ea_done)
                    v.drain()
                    v.tensor_tensor(
                        m.EA16s[slot].rearrange("n (h p) -> n h p", h=16),
                        m.EA16s[slot].rearrange("n (h p) -> n h p", h=16),
                        m.c_park[:, g * CH:g * CH + CH].unsqueeze(1
                            ).broadcast_to([64, 16, 256]),
                        ALU.mult).then_inc(m.s_dve, 1)
                    m.inc(m.s_dve, 1)
                cp_done = m.val(m.s_dve)
                pe_c0 = m.val(m.s_pe)
                dve_c0 = m.val(m.s_dve)

                @m.block.tensor
                def _(t, hg=hg, slot=slot, kc=kc, cp_done=cp_done,
                      pe_c0=pe_c0, dve_c0=dve_c0):
                    t.wait_ge(m.s_dve, cp_done)
                    t.wait_ge(m.s_dve, sink_dve[kc])
                    t.wait_ge(m.s_yd, yl_done)
                    for l in range(16):
                        h = hg * 16 + l
                        if l >= 8:
                            t.wait_ge(m.s_dve, dve_c0 + (l - 8) + 1)
                        yap = self.pf32(l % 8, 0, 256)
                        t.matmul(yap, m.SINK16[:, h * 128:h * 128 + 128],
                                 m.EA16s[slot][:, l * 256:l * 256 + 256],
                                 start=True, stop=True).then_inc(m.s_pe, 1)
                        m.inc(m.s_pe, 1)
                corr_pe[use] = m.val(m.s_pe)

                @m.block.vector
                def _(v, hg=hg, pe_c0=pe_c0, kc=kc):
                    for l in range(16):
                        h = hg * 16 + l
                        v.wait_ge(m.s_pe, pe_c0 + l + 1)
                        yap = self.pf32(l % 8, 0, 256)
                        cc = kc * CH
                        v.tensor_tensor(m.B1[:, h, cc:cc + 256],
                                        m.B1[:, h, cc:cc + 256], yap,
                                        ALU.add).then_inc(m.s_dve, 1)
                        m.inc(m.s_dve, 1)

        TW = TAU
        u0 = (0 if tau == 0 else TAU + HALO) + (HALO if tau == 0 else 0)
        corr_dve = m.val(m.s_dve)

        @m.block.gpsimd
        def _(g):
            g.wait_ge(m.s_dve, corr_dve)
            g.wait_ge(m.s_act, m.val(m.s_act))
            g.dma_start(
                out=m.AR_U[:, 0:16 * TW].rearrange("p (k t) -> p k t", t=TW),
                in_=m.u_h[:, u0:u0 + TW].rearrange("(k p) t -> p k t", p=128),
            ).then_inc(m.s_wu, 16)
            m.inc(m.s_wu, 16)
        uz_done = m.val(m.s_wu)

        if tau == 0:
            @m.block.gpsimd
            def _(g):
                g.wait_ge(m.s_dve, corr_dve)
                g.dma_start(out=m.ocols, in_=m.ocols_h.rearrange(
                    "p (h j) -> p h j", j=32)).then_inc(m.s_wdma, 16)
                m.inc(m.s_wdma, 16)
        ocols_done = m.val(m.s_wdma)

        sl = [m.s_sl0, m.s_sl1]
        slbase = [m.val(m.s_sl0), m.val(m.s_sl1)]
        zmm_inc = {}
        sig_inc = {}
        sq_inc = {}
        sqm_inc = {}
        yg_inc = {}

        for h in range(32):
            @m.block.gpsimd
            def _(g, h=h):
                if h >= 2:
                    g.wait_ge(m.s_pe, zmm_inc[h - 2])
                g.dma_start(out=m.WBUF[:, h % 2], in_=m.wZ_h[h]
                            ).then_inc(sl[h % 2], 16)
                m.inc(sl[h % 2], 16)

            @m.block.tensor
            def _(t, h=h):
                if h == 0:
                    t.wait_ge(m.s_wu, uz_done)
                t.wait_ge(sl[h % 2], slbase[h % 2] + 16 * (h // 2 + 1))
                if h >= 2:
                    t.wait_ge(m.s_act, sig_inc[h - 2])
                last = None
                for tt in range(2):
                    for k in range(16):
                        last = t.matmul(
                            self.pf32(2 * (h % 2) + tt, 0, 512),
                            m.WBUF[:, h % 2, k],
                            m.AR_U[:, 0:16 * TW].rearrange(
                                "p (k t) -> p k t", t=TW)[:, k,
                                tt * 512:tt * 512 + 512],
                            start=(k == 0), stop=(k == 15))
                last.then_inc(m.s_pe, 1)
                m.inc(m.s_pe, 1)
                zmm_inc[h] = m.val(m.s_pe)
                if h >= 1:
                    if h == 1:
                        t.wait_ge(m.s_wdma, ocols_done)
                    t.wait_ge(m.s_dve, sq_inc[h - 1])
                    last = None
                    for tt in range(2):
                        last = t.matmul(
                            self.pf32(6 + tt, 0, 512, p=32),
                            m.ocols[:, h - 1],
                            m.SQ16[:, ((h - 1) % 2) * 1024 + tt * 512:
                                   ((h - 1) % 2) * 1024 + tt * 512 + 512],
                            start=(h - 1 == 0), stop=(h - 1 == 31))
                    last.then_inc(m.s_pe, 1)
                    m.inc(m.s_pe, 1)
                    sqm_inc[h - 1] = m.val(m.s_pe)

            zap = (self.PS_A[:, 0:1024] if h % 2 == 0
                   else self.PS_A[:, 1024:2048])
            dve_snap = m.val(m.s_dve)

            @m.block.scalar
            def _(sc, h=h, zap=zap, dve_snap=dve_snap):
                sc.wait_ge(m.s_pe, zmm_inc[h])
                if h >= 2:
                    sc.wait_ge(m.s_dve, yg_inc[h - 2])
                sc.drain()
                sc.activation(m.CONVACC[:, h % 2], zap, AF.Sigmoid
                              ).then_inc(m.s_act, 1)
                m.inc(m.s_act, 1)
            sig_inc[h] = m.val(m.s_act)

            @m.block.vector
            def _(v, h=h):
                v.wait_ge(m.s_act, sig_inc[h])
                v.drain()
                v.tensor_tensor(m.B1[:, h, 0:TAU], m.B1[:, h, 0:TAU],
                                m.CONVACC[:, h % 2], ALU.mult).then_inc(m.s_dve, 1)
                m.inc(m.s_dve, 1)
            yg_inc[h] = m.val(m.s_dve)

            @m.block.vector
            def _(v, h=h):
                if h >= 2:
                    v.wait_ge(m.s_pe, sqm_inc[h - 2])
                v.drain()
                v.tensor_tensor(m.SQ16[:, (h % 2) * 1024:(h % 2) * 1024 + 1024],
                                m.B1[:, h, 0:TAU], m.B1[:, h, 0:TAU],
                                ALU.mult).then_inc(m.s_dve, 1)
                m.inc(m.s_dve, 1)
            sq_inc[h] = m.val(m.s_dve)

        @m.block.tensor
        def _(t):
            t.wait_ge(m.s_dve, sq_inc[31])
            last = None
            for tt in range(2):
                last = t.matmul(
                    self.pf32(6 + tt, 0, 512, p=32),
                    m.ocols[:, 31],
                    m.SQ16[:, (31 % 2) * 1024 + tt * 512:
                           (31 % 2) * 1024 + tt * 512 + 512],
                    start=False, stop=True)
            last.then_inc(m.s_pe, 1)
            m.inc(m.s_pe, 1)
            sqm_inc[31] = m.val(m.s_pe)

        @m.block.vector
        def _(v, yg31=m.val(m.s_dve)):
            v.wait_ge(m.s_pe, sqm_inc[31])
            v.wait_ge(m.s_dve, yg31)
            v.drain()
            v.tensor_scalar(m.RST32[:, 0:512], self.pf32(6, 0, 512, p=32),
                            1.0 / 128, EPS_RMS, ALU.mult, ALU.add)
            v.drain()
            v.tensor_scalar(m.RST32[:, 512:1024], self.pf32(7, 0, 512, p=32),
                            1.0 / 128, EPS_RMS, ALU.mult, ALU.add).then_inc(m.s_dve, 1)
            m.inc(m.s_dve, 1)
        ms_done = m.val(m.s_dve)

        @m.block.scalar
        def _(sc, ms_done=ms_done):
            sc.wait_ge(m.s_dve, ms_done)
            sc.drain()
            sc.activation(m.RST32, m.RST32, AF.Sqrt).then_inc(m.s_act, 1)
            m.inc(m.s_act, 1)
        ars_done = m.val(m.s_act)

        @m.block.vector
        def _(v, ars_done=ars_done):
            v.wait_ge(m.s_act, ars_done)
            v.drain()
            v.reciprocal(m.RST32, m.RST32).then_inc(m.s_dve, 1)
            m.inc(m.s_dve, 1)
        rec_done = m.val(m.s_dve)

        @m.block.sync
        def _(s, rec_done=rec_done):
            s.wait_ge(m.s_dve, rec_done)
            s.dma_start(out=m.rstdh_h, in_=m.RST32).then_inc(m.s_yd, 16)
            m.inc(m.s_yd, 16)
        rsth_done = m.val(m.s_yd)

        yn_inc = {}
        ers = [m.s_er, m.s_er2]
        for h in range(32):
            @m.block.gpsimd
            def _(g, h=h):
                if h == 0:
                    g.wait_ge(m.s_yd, rsth_done)
                if h >= 2:
                    g.wait_ge(m.s_dve, yn_inc[h - 2])
                g.dma_start(out=m.MRB2[:, h % 2],
                            in_=m.rstdh_h[h:h + 1, :].partition_broadcast(128)
                            ).then_inc(ers[h % 2], 16)
                m.inc(ers[h % 2], 16)
            bc_done = m.val(ers[h % 2])

            @m.block.vector
            def _(v, h=h, bc_done=bc_done):
                v.wait_ge(ers[h % 2], bc_done)
                v.drain()
                v.scalar_tensor_tensor(
                    m.B1[:, h, 0:TAU], m.B1[:, h, 0:TAU], m.nw[:, h:h + 1],
                    m.MRB2[:, h % 2], ALU.mult, ALU.mult
                    ).then_inc(m.s_dve, 1)
                m.inc(m.s_dve, 1)
            yn_inc[h] = m.val(m.s_dve)

    def _a4(self, tau):
        m = self
        pre_dve = m.val(m.s_dve)
        sl = [m.s_sl0, m.s_sl1]
        slbase = [m.val(m.s_sl0), m.val(m.s_sl1)]
        pe0 = m.val(m.s_pe)
        dve0 = m.val(m.s_dve)
        u0 = (0 if tau == 0 else TAU + HALO) + (HALO if tau == 0 else 0)

        @m.block.gpsimd
        def _(g):
            g.wait_ge(m.s_dve, pre_dve)
            for e in range(16):
                if e >= 2:
                    g.wait_ge(m.s_pe, pe0 + 2 * (e - 1))
                g.dma_start(out=m.WO_BUF[:, e % 2], in_=m.wO_h[e]
                            ).then_inc(sl[e % 2], 16)
                m.inc(sl[e % 2], 16)

        for e in range(16):
            for tt in range(2):
                gidx = e * 2 + tt

                @m.block.sync
                def _(s, e=e, tt=tt, gidx=gidx):
                    if gidx >= 2:
                        s.wait_ge(m.s_dve, dve0 + (gidx - 2) + 1)
                    else:
                        s.wait_ge(m.s_dve, pre_dve)
                    us = m.s_ur if gidx % 2 == 0 else m.s_ur1
                    s.dma_start(
                        out=m.URES[:, gidx % 2],
                        in_=m.u_h[e * 128:e * 128 + 128,
                                  u0 + tt * 512:u0 + tt * 512 + 512]
                        ).then_inc(us, 16)
                    m.inc(us, 16)
                ures_done = m.val(m.s_ur if gidx % 2 == 0 else m.s_ur1)

                @m.block.tensor
                def _(t, e=e, tt=tt, gidx=gidx, ures_done=ures_done):
                    t.wait_ge(sl[e % 2], slbase[e % 2] + 16 * (e // 2 + 1))
                    if gidx >= 4:
                        t.wait_ge(m.s_dve, dve0 + (gidx - 4) + 1)
                    last = None
                    for k in range(32):
                        last = t.matmul(self.pf32(gidx % 4, 0, 512),
                                        m.WO_BUF[:, e % 2, k],
                                        m.B1[:, k, tt * 512:tt * 512 + 512],
                                        start=(k == 0), stop=(k == 31))
                    last.then_inc(m.s_pe, 1)
                    m.inc(m.s_pe, 1)
                mm_done = m.val(m.s_pe)

                @m.block.vector
                def _(v, e=e, tt=tt, gidx=gidx, mm_done=mm_done,
                      ures_done=ures_done):
                    v.wait_ge(m.s_pe, mm_done)
                    v.wait_ge(m.s_ur if gidx % 2 == 0 else m.s_ur1, ures_done)
                    v.drain()
                    v.tensor_tensor(
                        m.X1PB[:, e, tt * 512:tt * 512 + 512],
                        self.pf32(gidx % 4, 0, 512),
                        m.URES[:, gidx % 2], ALU.add).then_inc(m.s_dve, 1)
                    m.inc(m.s_dve, 1)

        evac_all = m.val(m.s_dve)

        sq0 = m.val(m.s_dve)
        pe_before = m.val(m.s_pe)

        @m.block.vector
        def _(v, evac_all=evac_all, pe_before=pe_before):
            v.wait_ge(m.s_dve, evac_all)
            v.wait_ge(m.s_pe, pe_before)
            n = 0
            for tt in range(2):
                for e in range(16):
                    if n > 0:
                        v.wait_ge(m.s_pe, pe_before + n)
                    v.drain()
                    v.tensor_tensor(m.SQF2[:, n % 2],
                                    m.X1PB[:, e, tt * 512:tt * 512 + 512],
                                    m.X1PB[:, e, tt * 512:tt * 512 + 512],
                                    ALU.mult).then_inc(m.s_dve, 1)
                    m.inc(m.s_dve, 1)
                    n += 1

        pe_ln0 = m.val(m.s_pe)

        @m.block.tensor
        def _(t, evac_all=evac_all, sq0=sq0):
            t.wait_ge(m.s_dve, evac_all)
            n = 0
            for tt in range(2):
                for e in range(16):
                    t.matmul(self.pf32(4 + tt, 0, 512)[0:1, :], m.o128x1,
                             m.X1PB[:, e, tt * 512:tt * 512 + 512],
                             start=(e == 0), stop=(e == 15))
                    t.wait_ge(m.s_dve, sq0 + n + 1)
                    t.matmul(self.pf32(6 + tt, 0, 512)[0:1, :], m.o128x1,
                             m.SQF2[:, n % 2],
                             start=(e == 0), stop=(e == 15)
                             ).then_inc(m.s_pe, 1)
                    m.inc(m.s_pe, 1)
                    n += 1

        ln_pe = m.val(m.s_pe)
        for tt in range(2):
            @m.block.vector
            def _(v, tt=tt):
                v.wait_ge(m.s_pe, ln_pe)
                v.wait_ge(m.s_er, m.val(m.s_er))
                v.drain()
                v.tensor_scalar_mul(m.MROW[:, 0:512],
                                    self.pf32(4 + tt, 0, 512, p=1), 1.0 / D)
                v.drain()
                v.tensor_scalar_mul(m.RROW[:, 0:512],
                                    self.pf32(6 + tt, 0, 512, p=1), 1.0 / D)
                v.drain()
                v.tensor_tensor(m.SGM[0:1, 0:512], m.MROW[:, 0:512],
                                m.MROW[:, 0:512], ALU.mult)
                v.drain()
                v.tensor_tensor(m.RROW[:, 0:512], m.RROW[:, 0:512],
                                m.SGM[0:1, 0:512], ALU.subtract)
                v.drain()
                v.tensor_scalar_add(m.RROW[:, 0:512], m.RROW[:, 0:512],
                                    EPS_LN).then_inc(m.s_dve, 1)
                m.inc(m.s_dve, 1)
            var_done = m.val(m.s_dve)

            @m.block.scalar
            def _(sc, var_done=var_done):
                sc.wait_ge(m.s_dve, var_done)
                sc.drain()
                sc.activation(m.RROW[:, 0:512], m.RROW[:, 0:512], AF.Sqrt).then_inc(m.s_act, 1)
                m.inc(m.s_act, 1)
            st_done = m.val(m.s_act)

            @m.block.vector
            def _(v, st_done=st_done):
                v.wait_ge(m.s_act, st_done)
                v.drain()
                v.reciprocal(m.RROW[:, 0:512], m.RROW[:, 0:512]).then_inc(m.s_dve, 1)
                m.inc(m.s_dve, 1)
            rr_done = m.val(m.s_dve)
            er0 = m.val(m.s_er)

            @m.block.gpsimd
            def _(g, rr_done=rr_done, er0=er0):
                g.wait_ge(m.s_dve, rr_done)
                g.dma_start(out=m.rst_h[:, 0:512], in_=m.RROW[:, 0:512]
                            ).then_inc(m.s_er, 16)
                m.inc(m.s_er, 16)
                g.dma_start(out=m.rst_h[:, 512:1024], in_=m.MROW[:, 0:512]
                            ).then_inc(m.s_er, 16)
                m.inc(m.s_er, 16)
                g.wait_ge(m.s_er, er0 + 32)
                g.dma_start(out=m.MRB[:, 0], in_=m.rst_h[:, 0:512]
                            .partition_broadcast(128)).then_inc(m.s_er, 16)
                m.inc(m.s_er, 16)
                g.dma_start(out=m.MRB[:, 1], in_=m.rst_h[:, 512:1024]
                            .partition_broadcast(128)).then_inc(m.s_er, 16)
                m.inc(m.s_er, 16)
            mrb_done = m.val(m.s_er)

            @m.block.vector
            def _(v, tt=tt, mrb_done=mrb_done):
                v.wait_ge(m.s_er, mrb_done)
                for e in range(16):
                    v.drain()
                    v.tensor_tensor(m.X1PB[:, e, tt * 512:tt * 512 + 512],
                                    m.X1PB[:, e, tt * 512:tt * 512 + 512],
                                    m.MRB[:, 1], ALU.subtract)
                    v.drain()
                    i = v.tensor_tensor(
                        m.X1PB[:, e, tt * 512:tt * 512 + 512],
                        m.X1PB[:, e, tt * 512:tt * 512 + 512],
                        m.MRB[:, 0], ALU.mult)
                i.then_inc(m.s_dve, 1)
                m.inc(m.s_dve, 1)
            norm_done = m.val(m.s_dve)

        fin = m.val(m.s_dve)

        @m.block.sync
        def _(s, tau=tau, fin=fin):
            s.wait_ge(m.s_dve, fin)
            s.dma_start(out=m.x1park_h[:, :, tau * TAU:tau * TAU + TAU],
                        in_=m.X1PB).then_inc(m.s_st, 16)
            m.inc(m.s_st, 16)

    def _f1(self):
        m = self
        X1 = self.sb_at("X1F", [128, 16, T], F16, self.sballoc["B1"][0])
        m.X1V = X1
        WB4 = self.sb_at("WB4", [128, 4, 16, 128], F16,
                         self.sballoc["XTM"][0])
        GB = self.sb_at("GB2", [128, 2, 2048], F16, self.sballoc["CONVACC"][0])
        UB = self.sb_at("UB2", [128, 2, 2048], F16, self.sballoc["SCL"][0])
        HB8 = self.sb_at("HB8", [128, 8, 2048], F16, self.sballoc["dt_park"][0])
        AB4 = self.sb_at("AB4", [128, 4, 2048], F16, self.sballoc["W16"][0])
        F1S = self.sb_at("F1Sv", [128, 4, 2048], F16, self.sballoc["AR_U"][0])
        pre_pe = m.val(m.s_pe)
        pre_dve = m.val(m.s_dve)
        pre_act = m.val(m.s_act)
        pre_st = m.val(m.s_st)

        @m.block.sync
        def _(s):
            s.wait_ge(m.s_pe, pre_pe)
            s.wait_ge(m.s_dve, pre_dve)
            s.wait_ge(m.s_act, pre_act)
            s.wait_ge(m.s_st, pre_st)
            s.dma_start(out=X1, in_=m.x1park_h).then_inc(m.s_ur, 16)
            m.inc(m.s_ur, 16)
        x1_done = m.val(m.s_ur)

        sl = [m.s_sl0, m.s_sl1]
        silu_inc = {}
        ucp_inc = {}
        hb_inc = {}
        hb_inc_d = {}
        sp_inc = {}
        st_inc = {}

        def sparsify(g):
            s0 = (4 * g) % 8

            @m.block.vector
            def _(v, g=g, s0=s0):
                A0, A1_, A2_, A3_ = (AB4[:, i] for i in range(4))
                S1, T1, S2, T2 = (F1S[:, i] for i in range(4))
                for i in range(4):
                    v.drain()
                    v.tensor_scalar(AB4[:, i], HB8[:, s0 + i], -1.0, None,
                                    ALU.mult)
                    v.drain()
                    v.tensor_tensor(AB4[:, i], AB4[:, i], HB8[:, s0 + i],
                                    ALU.max)
                v.drain()
                v.tensor_tensor(S1, A0, A1_, ALU.max)
                v.tensor_tensor(T1, A0, A1_, ALU.min)
                v.tensor_tensor(S2, A2_, A3_, ALU.max)
                v.tensor_tensor(T2, A2_, A3_, ALU.min)
                v.drain()
                v.tensor_tensor(S1, S1, S2, ALU.min)
                v.tensor_tensor(T1, T1, T2, ALU.max)
                v.drain()
                v.tensor_tensor(S1, S1, T1, ALU.max)
                v.drain()
                for i in range(4):
                    v.tensor_tensor(AB4[:, i], AB4[:, i], S1, ALU.is_ge)
                v.drain()
                for i in range(4):
                    ii = v.tensor_tensor(HB8[:, s0 + i], HB8[:, s0 + i],
                                         AB4[:, i], ALU.mult)
                ii.then_inc(m.s_dve, 1)
                m.inc(m.s_dve, 1)
            sp_inc[g] = m.val(m.s_dve)

            @m.block.sync
            def _(s, g=g, s0=s0):
                s.wait_ge(m.s_dve, sp_inc[g])
                for i in range(4):
                    s.dma_start(out=m.hscr_h[4 * g + i], in_=HB8[:, s0 + i]
                                ).then_inc(m.s_st, 16)
                    m.inc(m.s_st, 16)
            st_inc[g] = m.val(m.s_st)

        mmv = {}
        sl4 = [m.s_sl0, m.s_sl1, m.s_sl2, m.s_sl3]
        for f in range(64):
            wld = {}
            for wi, wh in [(0, m.wG_h), (1, m.wU_h)]:
                u = 2 * f + wi
                @m.block.gpsimd
                def _(g, f=f, wi=wi, wh=wh, u=u):
                    if u >= 4:
                        g.wait_ge(m.s_pe, mmv[u - 4])
                    g.dma_start(out=WB4[:, u % 4], in_=wh[f]
                                ).then_inc(sl4[u % 4], 16)
                    m.inc(sl4[u % 4], 16)
                wld[wi] = (sl4[u % 4], m.val(sl4[u % 4]))
            g_done = wld[0]
            u_done = wld[1]

            @m.block.tensor
            def _(t, f=f, g_done=g_done, u_done=u_done):
                if f == 0:
                    t.wait_ge(m.s_ur, x1_done)
                t.wait_ge(g_done[0], g_done[1])
                if f >= 1:
                    t.wait_ge(m.s_act, silu_inc[f - 1])
                last = None
                for k in range(16):
                    for tt in range(4):
                        last = t.matmul(self.pf32(tt, 0, 512),
                                        WB4[:, (2 * f) % 4, k],
                                        X1[:, k, tt * 512:tt * 512 + 512],
                                        start=(k == 0), stop=(k == 15))
                last.then_inc(m.s_pe, 1)
                m.inc(m.s_pe, 1)
                mmv[2 * f] = m.val(m.s_pe)
                t.wait_ge(u_done[0], u_done[1])
                if f >= 1:
                    t.wait_ge(m.s_act, ucp_inc[f - 1])
                last = None
                for k in range(16):
                    for tt in range(4):
                        last = t.matmul(self.pf32(4 + tt, 0, 512),
                                        WB4[:, (2 * f + 1) % 4, k],
                                        X1[:, k, tt * 512:tt * 512 + 512],
                                        start=(k == 0), stop=(k == 15))
                last.then_inc(m.s_pe, 1)
                m.inc(m.s_pe, 1)
                mmv[2 * f + 1] = m.val(m.s_pe)
            gmm = m.val(m.s_pe) - 1
            umm = m.val(m.s_pe)
            dve_snap = m.val(m.s_dve)

            @m.block.scalar
            def _(sc, f=f, gmm=gmm, umm=umm, dve_snap=dve_snap):
                sc.wait_ge(m.s_pe, gmm)
                if f >= 2:
                    sc.wait_ge(m.s_dve, hb_inc_d[f - 2])
                sc.drain()
                sc.activation(GB[:, f % 2], self.PS_A[:, 0:2048], AF.Silu
                              ).then_inc(m.s_act, 1)
                m.inc(m.s_act, 1)
                silu_inc[f] = m.val(m.s_act)
                sc.wait_ge(m.s_pe, umm)
                sc.drain()
                sc.copy(UB[:, f % 2], self.PS_B[:, 0:2048]
                        ).then_inc(m.s_act, 1)
                m.inc(m.s_act, 1)
                ucp_inc[f] = m.val(m.s_act)

            @m.block.vector
            def _(v, f=f):
                v.wait_ge(m.s_act, ucp_inc[f])
                if f >= 8:
                    v.wait_ge(m.s_st, st_inc[(f - 8) // 4])
                v.drain()
                v.tensor_tensor(HB8[:, f % 8], GB[:, f % 2], UB[:, f % 2],
                                ALU.mult).then_inc(m.s_dve, 1)
                m.inc(m.s_dve, 1)
            hb_inc_d[f] = m.val(m.s_dve)
            if f >= 6 and (f - 6) % 4 == 0:
                sparsify((f - 6) // 4)
        sparsify(15)

    def _f2(self):
        m = self
        X1 = m.X1V
        WDB = self.sb_at("WDBv", [128, 4, 2, 512], F16, self.sballoc["WBUF"][0])
        HB2 = self.sb_at("HB2v", [128, 4, 2, 1024], F16,
                         self.sballoc["s_run"][0])
        sl = [m.s_sl0, m.s_sl1, m.s_sl2, m.s_sl3]
        hs = [m.s_ur, m.s_ur1, m.s_ur2, m.s_ur3]
        pre_st = m.val(m.s_st)

        for eq in range(4):
            for ttp in range(2):
                ev0 = m.val(m.s_dve)
                pe00 = m.val(m.s_pe)
                for fp in range(32):
                    f = 2 * fp
                    snap_pe = m.val(m.s_pe)

                    @m.block.gpsimd
                    def _(g, eq=eq, fp=fp, f=f, snap_pe=snap_pe, pe00=pe00):
                        if fp >= 4:
                            g.wait_ge(m.s_pe, snap_pe - 3)
                        elif pe00 > 0:
                            g.wait_ge(m.s_pe, pe00)
                        g.dma_start(
                            out=WDB[:, fp % 4].rearrange(
                                "p f j -> p (f j)").rearrange(
                                "p (f e j) -> p f e j", f=2, e=4),
                            in_=m.wD_h[eq, f:f + 2].rearrange(
                                "f e p j -> p f e j")
                            ).then_inc(sl[fp % 4], 16)
                        m.inc(sl[fp % 4], 16)
                    wd_done = m.val(sl[fp % 4])

                    @m.block.sync
                    def _(s, eq=eq, ttp=ttp, fp=fp, f=f, snap_pe=snap_pe,
                          pe00=pe00, pre_st=pre_st):
                        if fp >= 4:
                            s.wait_ge(m.s_pe, snap_pe - 3)
                        else:
                            s.wait_ge(m.s_st, pre_st)
                            s.wait_ge(m.s_pe, pe00)
                        s.dma_start(
                            out=HB2[:, fp % 4],
                            in_=m.hscr_h[f:f + 2, :,
                                         ttp * 1024:ttp * 1024 + 1024]
                            .rearrange("f p t -> p f t")
                            ).then_inc(hs[fp % 4], 16)
                        m.inc(hs[fp % 4], 16)
                    h_done = m.val(hs[fp % 4])

                    @m.block.tensor
                    def _(t, eq=eq, ttp=ttp, fp=fp, wd_done=wd_done,
                          h_done=h_done, ev0=ev0):
                        t.wait_ge(sl[fp % 4], wd_done)
                        t.wait_ge(hs[fp % 4], h_done)
                        if fp == 0:
                            t.wait_ge(m.s_dve, ev0)
                        last = None
                        for fi in range(2):
                            for e2 in range(4):
                                for tt in range(2):
                                    last = t.matmul(
                                        self.pf32(e2 * 2 + tt, 0, 512),
                                        WDB[:, fp % 4, fi,
                                            e2 * 128:e2 * 128 + 128],
                                        HB2[:, fp % 4, fi,
                                            tt * 512:tt * 512 + 512],
                                        start=(fp == 0 and fi == 0),
                                        stop=(fp == 31 and fi == 1))
                        last.then_inc(m.s_pe, 1)
                        m.inc(m.s_pe, 1)
                dn_done = m.val(m.s_pe)

                @m.block.vector
                def _(v, eq=eq, ttp=ttp, dn_done=dn_done):
                    v.wait_ge(m.s_pe, dn_done)
                    for e2 in range(4):
                        for tt in range(2):
                            c = eq * 4 + e2
                            col = ttp * 1024 + tt * 512
                            i = v.tensor_tensor(
                                X1[:, c, col:col + 512],
                                X1[:, c, col:col + 512],
                                self.pf32(e2 * 2 + tt, 0, 512), ALU.add)
                    i.then_inc(m.s_dve, 1)
                    m.inc(m.s_dve, 1)

    def _f3(self):
        m = self
        X1 = m.X1V
        MRB4 = self.sb_at("MRB4", [128, 4, 512], F32, self.sballoc["prevs"][0])
        f2_done = m.val(m.s_dve)
        stats_read = {}
        norm_done = {}
        for tq in range(4):
            par = tq % 2
            bA, bB = 4 + 2 * par, 5 + 2 * par
            sq0 = m.val(m.s_dve)
            pe_before = m.val(m.s_pe)

            @m.block.vector
            def _(v, tq=tq, pe_before=pe_before, f2_done=f2_done):
                if tq == 0:
                    v.wait_ge(m.s_dve, f2_done)
                n = 0
                for e in range(16):
                    if n > 0:
                        v.wait_ge(m.s_pe, pe_before + n)
                    v.drain()
                    v.tensor_tensor(m.SQF2[:, tq % 2],
                                    X1[:, e, tq * 512:tq * 512 + 512],
                                    X1[:, e, tq * 512:tq * 512 + 512],
                                    ALU.mult).then_inc(m.s_dve, 1)
                    m.inc(m.s_dve, 1)
                    n += 1

            @m.block.tensor
            def _(t, tq=tq, sq0=sq0, f2_done=f2_done, bA=bA, bB=bB):
                if tq == 0:
                    t.wait_ge(m.s_dve, f2_done)
                elif tq >= 2:
                    t.wait_ge(m.s_dve, stats_read[tq - 2])
                for e in range(16):
                    t.matmul(self.pf32(bA, 0, 512)[0:1, :], m.o128x1,
                             X1[:, e, tq * 512:tq * 512 + 512],
                             start=(e == 0), stop=(e == 15))
                    t.wait_ge(m.s_dve, sq0 + e + 1)
                    t.matmul(self.pf32(bB, 0, 512)[0:1, :], m.o128x1,
                             m.SQF2[:, tq % 2],
                             start=(e == 0), stop=(e == 15)
                             ).then_inc(m.s_pe, 1)
                    m.inc(m.s_pe, 1)
            ln_pe = m.val(m.s_pe)

            @m.block.vector
            def _(v, tq=tq, ln_pe=ln_pe, bA=bA, bB=bB,
                  er_snap=m.val(m.s_er)):
                v.wait_ge(m.s_pe, ln_pe)
                v.wait_ge(m.s_er, er_snap)
                v.drain()
                v.tensor_scalar_mul(m.MROW[:, 0:512],
                                    self.pf32(bA, 0, 512, p=1), 1.0 / D)
                v.drain()
                v.tensor_scalar_mul(m.RROW[:, 0:512],
                                    self.pf32(bB, 0, 512, p=1), 1.0 / D)
                v.drain()
                v.tensor_tensor(m.SGM[0:1, 0:512], m.MROW[:, 0:512],
                                m.MROW[:, 0:512], ALU.mult)
                v.drain()
                v.tensor_tensor(m.RROW[:, 0:512], m.RROW[:, 0:512],
                                m.SGM[0:1, 0:512], ALU.subtract)
                v.drain()
                v.tensor_scalar_add(m.RROW[:, 0:512], m.RROW[:, 0:512],
                                    EPS_LN).then_inc(m.s_dve, 1)
                m.inc(m.s_dve, 1)
            stats_read[tq] = m.val(m.s_dve)
            var_done = m.val(m.s_dve)

            @m.block.scalar
            def _(sc, var_done=var_done):
                sc.wait_ge(m.s_dve, var_done)
                sc.drain()
                sc.activation(m.RROW[:, 0:512], m.RROW[:, 0:512], AF.Sqrt).then_inc(m.s_act, 1)
                m.inc(m.s_act, 1)
            st_done = m.val(m.s_act)

            @m.block.vector
            def _(v, st_done=st_done):
                v.wait_ge(m.s_act, st_done)
                v.drain()
                v.reciprocal(m.RROW[:, 0:512], m.RROW[:, 0:512]).then_inc(m.s_dve, 1)
                m.inc(m.s_dve, 1)
            rr_done = m.val(m.s_dve)
            er0 = m.val(m.s_er)

            @m.block.gpsimd
            def _(g, tq=tq, rr_done=rr_done, er0=er0, par=par):
                g.wait_ge(m.s_dve, rr_done)
                g.dma_start(out=m.rst_h[:, 0:512], in_=m.RROW[:, 0:512]
                            ).then_inc(m.s_er, 16)
                m.inc(m.s_er, 16)
                g.dma_start(out=m.rst_h[:, 512:1024], in_=m.MROW[:, 0:512]
                            ).then_inc(m.s_er, 16)
                m.inc(m.s_er, 16)
                g.wait_ge(m.s_er, er0 + 32)
                g.dma_start(out=MRB4[:, 2 * par], in_=m.rst_h[:, 0:512]
                            .partition_broadcast(128)).then_inc(m.s_er, 16)
                m.inc(m.s_er, 16)
                g.dma_start(out=MRB4[:, 2 * par + 1], in_=m.rst_h[:, 512:1024]
                            .partition_broadcast(128)).then_inc(m.s_er, 16)
                m.inc(m.s_er, 16)
            mrb_done = m.val(m.s_er)

            @m.block.vector
            def _(v, tq=tq, mrb_done=mrb_done, par=par,
                  st_snap=m.val(m.s_wdma)):
                v.wait_ge(m.s_er, mrb_done)
                v.wait_ge(m.s_wdma, st_snap)
                for e in range(16):
                    v.drain()
                    v.tensor_tensor(X1[:, e, tq * 512:tq * 512 + 512],
                                    X1[:, e, tq * 512:tq * 512 + 512],
                                    MRB4[:, 2 * par + 1], ALU.subtract)
                    v.drain()
                    i = v.tensor_tensor(X1[:, e, tq * 512:tq * 512 + 512],
                                        X1[:, e, tq * 512:tq * 512 + 512],
                                        MRB4[:, 2 * par], ALU.mult)
                i.then_inc(m.s_dve, 1)
                m.inc(m.s_dve, 1)
            norm_done[tq] = m.val(m.s_dve)

            @m.block.gpsimd
            def _(g, tq=tq, nd=norm_done[tq]):
                g.wait_ge(m.s_dve, nd)
                g.dma_start(
                    out=m.out_h.rearrange("(c p) t -> p c t", p=128)
                    [:, :, tq * 512:tq * 512 + 512],
                    in_=X1[:, :, tq * 512:tq * 512 + 512]
                    ).then_inc(m.s_wdma, 16)
                m.inc(m.s_wdma, 16)

    def _dump_tau(self, tau):
        m = self
        d = m.dbg_out(f"dbg_b1t{tau}", [128, 34, TAU + HALO], F16)
        if d is not None:
            hof = HALO if tau == 0 else 0
            @m.block.sync
            def _(s):
                s.wait_ge(m.s_act, m.val(m.s_act))
                s.wait_ge(m.s_dve, m.val(m.s_dve))
                s.dma_start(out=d[:, :, hof:hof + TAU],
                            in_=m.B1[:, 0:34, hof:hof + TAU]).then_inc(m.s_st, 16)
                m.inc(m.s_st, 16)
                s.wait_ge(m.s_st, m.val(m.s_st))

    def _finish_debug(self):
        m = self

        d3 = m.dbg_out("dbg_srun", [64, DIN], F32)
        if d3 is not None:
            @m.block.sync
            def _(s):
                s.wait_ge(m.s_dve, m.val(m.s_dve))
                s.dma_start(out=d3, in_=m.s_run).then_inc(m.s_st, 16)
                m.inc(m.s_st, 16)
        d4 = m.dbg_out("dbg_a", [32, T], F32)
        if d4 is not None:
            @m.block.sync
            def _(s):
                s.wait_ge(m.s_dve, m.val(m.s_dve))
                s.dma_start(out=d4, in_=m.a_park).then_inc(m.s_st, 16)
                m.inc(m.s_st, 16)
        d5 = m.dbg_out("dbg_y", [128, 32, T], F16)
        if d5 is not None:
            @m.block.sync
            def _(s):
                s.wait_ge(m.s_yd, m.val(m.s_yd))
                s.dma_start(out=d5, in_=m.ypark_h).then_inc(m.s_st, 16)
                m.inc(m.s_st, 16)
        d6 = m.dbg_out("dbg_x1", [128, 16, T], F16)
        if d6 is not None:
            @m.block.sync
            def _(s):
                s.wait_ge(m.s_st, m.val(m.s_st))
                s.dma_start(out=d6, in_=m.x1park_h).then_inc(m.s_st, 16)
                m.inc(m.s_st, 16)
        d2 = m.dbg_out("dbg_dt", [32, T], F32)
        if d2 is not None:
            @m.block.sync
            def _(s):
                s.wait_ge(m.s_act, m.val(m.s_act))
                s.dma_start(out=d2, in_=m.dt_park).then_inc(m.s_st, 16)
                m.inc(m.s_st, 16)

    def _final_wait(self):
        m = self

        @m.block.sync
        def _(s):
            for name in ["s_wdma", "s_ldma", "s_pe", "s_dve", "s_act", "s_gp",
                         "s_st", "s_cc", "s_wu", "s_sl0", "s_sl1", "s_yd",
                         "s_ai", "s_ai2", "s_af", "s_er", "s_er2", "s_tp",
                      "s_ur", "s_ur1", "s_sl2", "s_sl3", "s_ur2", "s_ur3"]:
                sem = getattr(m, name)
                s.wait_ge(sem, m.val(sem))



def prep_inputs(inputs):
    W = {}
    ip = np.ascontiguousarray(inputs["L0_attn_in_proj_w"])
    rows = np.zeros((NXC * 128, D), np.float32)
    rows[:4096] = ip[4096:8192]
    rows[4096:4160] = ip[8192:8256]
    rows[4224:4288] = ip[8256:8320]
    rows[4352:4384] = ip[8320:8352]
    W["wA"] = np.ascontiguousarray(
        rows.reshape(NXC, 128, 16, 128).transpose(0, 3, 2, 1))
    W["wZ"] = np.ascontiguousarray(
        ip[0:4096].reshape(32, 128, 16, 128).transpose(0, 3, 2, 1))
    op = np.ascontiguousarray(inputs["L0_attn_o_proj_w"])
    W["wO"] = np.ascontiguousarray(
        op.reshape(16, 128, 32, 128).transpose(0, 3, 2, 1))

    f_idx = np.arange(64)
    p_idx = np.arange(128)
    iperm = (4 * (128 * (f_idx[:, None] // 4) + p_idx[None, :])
             + (f_idx[:, None] % 4))
    wg = np.ascontiguousarray(inputs["L0_ffn_w_gate"])
    wu = np.ascontiguousarray(inputs["L0_ffn_w_up"])
    wd = np.ascontiguousarray(inputs["L0_ffn_w_down"])
    wgp = wg[:, iperm.reshape(-1)].reshape(D, 64, 128)
    W["wG"] = np.ascontiguousarray(
        wgp.reshape(16, 128, 64, 128).transpose(2, 1, 0, 3))
    wup = wu[:, iperm.reshape(-1)].reshape(D, 64, 128)
    W["wU"] = np.ascontiguousarray(
        wup.reshape(16, 128, 64, 128).transpose(2, 1, 0, 3))
    wdp = wd[iperm.reshape(-1), :].reshape(64, 128, 2048)
    W["wD"] = np.ascontiguousarray(
        wdp.reshape(64, 128, 4, 4, 128).transpose(2, 0, 3, 1, 4))

    cw = np.asarray(inputs["L0_attn_conv_w"], np.float32)
    cb = np.asarray(inputs["L0_attn_conv_b"], np.float32)
    cwp = np.zeros((34 * 128, 4), np.float32)
    cbp = np.zeros((34 * 128,), np.float32)
    cwp[:4096] = cw[:4096]
    cbp[:4096] = cb[:4096]
    cwp[4096:4160] = cw[4096:4160]
    cbp[4096:4160] = cb[4096:4160]
    cwp[4224:4288] = cw[4160:4224]
    cbp[4224:4288] = cb[4160:4224]
    W["convw"] = np.ascontiguousarray(
        cwp.reshape(34, 128, 4).transpose(1, 0, 2).reshape(128, 34 * 4))
    W["convb"] = np.ascontiguousarray(cbp.reshape(34, 128).T)

    W["alog"] = np.asarray(inputs["L0_attn_A_log"], np.float32).reshape(32, 1)
    W["dtb"] = np.asarray(inputs["L0_attn_dt_bias"], np.float32).reshape(32, 1)
    W["dskip"] = np.asarray(inputs["L0_attn_D_skip"], np.float32).reshape(1, 32)
    W["nw"] = np.ascontiguousarray(
        np.asarray(inputs["L0_attn_norm_w"], np.float32).reshape(32, 128).T)
    W["identc"] = np.eye(128, dtype=np.float32)
    W["ident32c"] = np.eye(128, dtype=np.float32)
    i_, j_ = np.mgrid[0:128, 0:128]
    W["maskTc"] = (j_ >= i_).astype(np.float32)
    W["o1x128"] = np.ones((1, 128), np.float32)
    W["o128x1"] = np.ones((128, 1), np.float32)
    oc = np.zeros((128, 32, 32), np.float32)
    for h in range(32):
        oc[:, h, h] = 1.0
    W["ocols"] = oc.reshape(128, 32 * 32)
    W["oscan"] = np.ones((32, CH), np.float32)
    return W


def prep_core_inputs(inputs, W, core):
    uT = inputs["_uT"]
    d = dict(W)
    d["u"] = np.ascontiguousarray(uT[:, core * T:core * T + T + HALO])
    g = (np.arange(8) < core).astype(np.float32)
    dd = (np.arange(8) <= core - 1).astype(np.float32)
    d["gates"] = np.concatenate([g, dd]).reshape(1, 16).astype(np.float32)
    return d


_BUILD_CACHE = {}


def kernel(**inputs):
    u = np.asarray(inputs["L0_attn_u"], np.float32)
    uT = np.zeros((D, HALO + 16384), np.float32)
    uT[:, HALO:] = u[0].T
    inputs = dict(inputs)
    inputs["_uT"] = uT

    W = prep_inputs(inputs)

    mk = MK()
    nc = mk.build()
    in_maps = [prep_core_inputs(inputs, W, c) for c in range(NCORES)]
    res = run_bass_kernel_spmd(nc, in_maps, core_ids=list(range(NCORES)))
    outs = [res.results[c]["out"] for c in range(NCORES)]
    full = np.concatenate([o.T for o in outs], axis=0)
    return full.reshape(1, 16384, D).astype(np.float32)
